# version 03
# baseline (speedup 1.0000x reference)
import math
import numpy as np
from contextlib import ExitStack

import concourse.bass as bass
import concourse.mybir as mybir
from concourse.alu_op_type import AluOpType
from concourse.bass_utils import run_bass_kernel_spmd

F32 = mybir.dt.float32
BF16 = mybir.dt.bfloat16
AF = mybir.ActivationFunctionType
AX = mybir.AxisListType

NCORES = 8
D = 1024
T = 4096
SEQ = 2048
NB = 2
KC = D // 128
LN_EPS = 1e-5



class Sem:
    def __init__(self, handle):
        self.h = handle
        self.n = 0

    def inc(self, k=1):
        self.n += k
        return self.n


class Prog:

    ENGS = ("tensor", "vector", "scalar", "gpsimd", "sync")

    def __init__(self):
        self.nc = bass.Bass("TRN2", target_bir_lowering=False, debug=False)
        self.es = ExitStack()
        self.stacks = [self.es]
        self.steps = []
        self._dram_cache = {}
        self._dram_n = 0

    def dram(self, name, shape, dtype=F32, kind=None):
        if kind is None:
            self._dram_n += 1
            return self.nc.dram_tensor(f"{name}__{self._dram_n}", shape, dtype)
        if name in self._dram_cache:
            return self._dram_cache[name]
        t = self.nc.dram_tensor(name, shape, dtype, kind=kind)
        self._dram_cache[name] = t
        return t

    def scope_enter(self):
        self.stacks.append(ExitStack())

    def scope_exit(self):
        self.stacks.pop().close()

    def sbuf(self, name, shape, dtype):
        return self.stacks[-1].enter_context(
            self.nc.sbuf_tensor(name, shape, dtype, allow_name_mangling=True))

    def psum(self, name, shape, dtype=F32):
        return self.stacks[-1].enter_context(
            self.nc.psum_tensor(name, shape, dtype, allow_name_mangling=True))

    def sem(self, name):
        return Sem(self.es.enter_context(self.nc.semaphore(name)))

    def _step(self, eng, fn, w, i):
        iv = None
        if i is not None:
            s, k = i
            s.inc(k)
            iv = (s.h, k)
        wv = [(s.h, v) for (s, v) in w if v > 0]

        def run(E, fn=fn, wv=wv, iv=iv):
            for h, v in wv:
                E.wait_ge(h, v)
            inst = fn(E)
            if iv is not None:
                if inst is None:
                    inst = E.nop()
                inst.then_inc(iv[0], iv[1])
        self.steps.append((eng, run))

    def mm(self, out, lhsT, rhs, start=True, stop=True, w=(), i=None):
        self._step("tensor",
                   lambda E: E.matmul(out, lhsT, rhs, start=start, stop=stop), w, i)

    def transpose(self, out, in_, ident, w=(), i=None):
        self._step("tensor", lambda E: E.transpose(out, in_, ident), w, i)

    def tt(self, out, in0, in1, op, w=(), i=None):
        self._step("vector", lambda E: E.tensor_tensor(out=out, in0=in0, in1=in1, op=op), w, i)

    def ts(self, out, in0, s1, s2=None, op0=AluOpType.mult, op1=None, accum=None, w=(), i=None):
        kw = {}
        if op1 is not None:
            kw["op1"] = op1
        if accum is not None:
            kw["accum_out"] = accum
        self._step("vector",
                   lambda E: E.tensor_scalar(out=out, in0=in0, scalar1=s1, scalar2=s2,
                                             op0=op0, **kw), w, i)

    def vcopy(self, out, in_, w=(), i=None):
        self._step("vector", lambda E: E.tensor_copy(out=out, in_=in_), w, i)

    def stt(self, out, in0, scalar, in1, op0, op1, w=(), i=None):
        self._step("vector",
                   lambda E: E.scalar_tensor_tensor(out=out, in0=in0, scalar=scalar,
                                                    in1=in1, op0=op0, op1=op1), w, i)

    def vmemset(self, ap, val, w=(), i=None):
        self._step("vector", lambda E: E.memset(ap, val), w, i)

    def vrecip(self, out, in_, w=(), i=None):
        self._step("vector", lambda E: E.reciprocal(out=out, in_=in_), w, i)

    def vredmax(self, out, in_, w=(), i=None):
        self._step("vector", lambda E: E.reduce_max(out=out, in_=in_, axis=AX.X), w, i)

    def vselect(self, out, mask, on_true, on_false, w=(), i=None):
        self._step("vector",
                   lambda E: E.select(out=out, mask=mask, on_true=on_true, on_false=on_false),
                   w, i)

    def act(self, out, in_, func, bias=None, scale=None, accum=None, w=(), i=None):
        kw = {}
        if bias is not None:
            kw["bias"] = bias
        if scale is not None:
            kw["scale"] = scale
        if accum is not None:
            kw["accum_out"] = accum
        self._step("scalar", lambda E: E.activation(out=out, in_=in_, func=func, **kw), w, i)

    def scopy(self, out, in_, w=(), i=None):
        self._step("scalar", lambda E: E.copy(out=out, in_=in_), w, i)

    def gdma(self, out, in_, accum_op=None, w=(), i=None):
        if accum_op is None:
            self._step("gpsimd", lambda E: E.dma_start(out=out, in_=in_), w, i)
        else:
            self._step("gpsimd",
                       lambda E: E.dma_start(out=out, in_=in_, accum_op=accum_op), w, i)

    def sydma(self, out, in_, w=(), i=None):
        self._step("sync", lambda E: E.dma_start(out=out, in_=in_), w, i)

    def collective(self, kind, op, groups, ins, outs, w=(), i=None):
        self._step("gpsimd",
                   lambda E: E.collective_compute(kind, op, replica_groups=groups,
                                                  ins=ins, outs=outs), w, i)

    def wait(self, eng, sem, val):
        self._step(eng, lambda E: None, [(sem, val)], None)

    def build(self):
        with self.nc.Block() as block:
            for eng in self.ENGS:
                deco = getattr(block, eng)

                def body(E, eng=eng):
                    for e2, fn in self.steps:
                        if e2 == eng:
                            fn(E)
                deco(body)
        self.es.close()
        return self.nc



def make_env(p):
    env = {}
    env["xbuf"] = p.sbuf("xbuf", [128, KC * 512], F32)
    env["xbf"] = p.sbuf("xbf", [128, KC * 512], BF16)
    env["sqb"] = p.sbuf("sqb", [128, KC * 512], BF16)
    env["rows"] = p.sbuf("rows", [1, 1024], F32)
    env["small"] = p.sbuf("small", [1, 32], F32)
    env["eps1"] = p.sbuf("eps1", [1, 1], F32)
    env["eps6"] = p.sbuf("eps6", [1, 1], F32)
    env["ones128"] = p.sbuf("ones128", [128, 1], BF16)
    env["ones1f"] = p.sbuf("ones1f", [1, 128], F32)
    env["ones1b"] = p.sbuf("ones1b", [1, 128], BF16)
    env["lnscr"] = p.sbuf("lnscr", [128, 512], F32)
    env["ps_a"] = p.psum("ps_a", [128, 1024])
    env["ps_o"] = p.psum("ps_o", [128, 1024])
    env["ps_c"] = p.psum("ps_c", [128, 512])
    env["ps_bc"] = p.psum("ps_bc", [128, 1024])
    for s in ("s_x", "s_v", "s_s", "s_t", "s_st", "s_w", "s_g", "s_ac", "s_wr",
              "s_cc"):
        env[s] = p.sem(s)
    return env


def emit_barrier(p, env):
    sems = [env[k] for k in ("s_x", "s_v", "s_s", "s_t", "s_st", "s_w", "s_ac",
                             "s_wr", "s_cc")]
    for eng in Prog.ENGS:
        for s in sems:
            if s.n:
                p.wait(eng, s, s.n)


def emit_env_init(p, env):
    s_v = env["s_v"]
    p.vmemset(env["ones128"][:], 1.0, i=(s_v, 1))
    p.vmemset(env["ones1f"][:], 1.0, i=(s_v, 1))
    p.vmemset(env["ones1b"][:], 1.0, i=(s_v, 1))
    p.vmemset(env["eps1"][:], LN_EPS, i=(s_v, 1))
    p.vmemset(env["eps6"][:], 1e-6, i=(s_v, 1))


def emit_ln_block(p, env, x_ap, xln_slice_fn, b, blk):
    xbuf, xbf, sqb = env["xbuf"], env["xbf"], env["sqb"]
    rows = env["rows"]
    ps_a, ps_bc = env["ps_a"], env["ps_bc"]
    s_x, s_v, s_s, s_t = env["s_x"], env["s_v"], env["s_s"], env["s_t"]

    tof = b * SEQ + blk * 512
    p.gdma(out=xbuf[:].rearrange("p (c t) -> p c t", c=KC),
           in_=x_ap[:, tof:tof + 512].rearrange("(c p) t -> p c t", p=128),
           w=[(s_v, s_v.n), (s_t, s_t.n)], i=(s_x, 16))
    p.vcopy(out=xbf[:], in_=xbuf[:], w=[(s_x, s_x.n), (s_s, s_s.n)], i=(s_v, 1))
    p.act(out=sqb[:], in_=xbf[:], func=AF.Square, w=[(s_v, s_v.n)], i=(s_s, 1))
    for c in range(KC):
        p.mm(ps_a[0:1, 0:512], env["ones128"][:], xbf[:, c * 512:(c + 1) * 512],
             start=(c == 0), stop=(c == KC - 1),
             w=([(s_v, s_v.n)] if c == 0 else ()),
             i=((s_t, 1) if c == KC - 1 else None))
    for c in range(KC):
        p.mm(ps_a[0:1, 512:1024], env["ones128"][:], sqb[:, c * 512:(c + 1) * 512],
             start=(c == 0), stop=(c == KC - 1),
             w=([(s_s, s_s.n)] if c == 0 else ()),
             i=((s_t, 1) if c == KC - 1 else None))
    mu = rows[0:1, 0:512]
    var = rows[0:1, 512:1024]
    p.ts(mu, ps_a[0:1, 0:512], 1.0 / D, w=[(s_t, s_t.n)], i=(s_v, 1))
    p.ts(var, ps_a[0:1, 512:1024], 1.0 / D, i=(s_v, 1))
    p.tt(env["lnscr"][0:1, 0:512], mu, mu, AluOpType.mult, i=(s_v, 1))
    p.tt(var, var, env["lnscr"][0:1, 0:512], AluOpType.subtract, i=(s_v, 1))
    p.act(out=var, in_=var, func=AF.Ln, bias=env["eps1"][:], w=[(s_v, s_v.n)], i=(s_s, 1))
    p.act(out=var, in_=var, func=AF.Exp, scale=-0.5, i=(s_s, 1))
    p.tt(mu, mu, var, AluOpType.mult, w=[(s_s, s_s.n)], i=(s_v, 1))
    p.mm(ps_bc[:, 0:512], env["ones1f"][:], var, w=[(s_v, s_v.n)], i=(s_t, 1))
    p.mm(ps_bc[:, 512:1024], env["ones1f"][:], mu, i=(s_t, 1))
    for c in range(KC):
        xc = xbuf[:, c * 512:(c + 1) * 512]
        p.tt(env["lnscr"][:], xc, ps_bc[:, 0:512], AluOpType.mult,
             w=[(s_t, s_t.n)], i=(s_v, 1))
        p.tt(xln_slice_fn(c), env["lnscr"][:], ps_bc[:, 512:1024],
             AluOpType.subtract, i=(s_v, 1))



def build_attn_b(kind, P=None, env=None, xap=None, xnap=None, nm=None):
    p = P if P is not None else Prog()
    nm = nm or {}
    g = lambda n: nm.get(n, n)
    if xap is None:
        x = p.dram("x", [D, T], F32, kind="ExternalInput").ap()
        xn = p.dram("xn", [D, T], F32, kind="ExternalOutput").ap()
    else:
        x, xn = xap, xnap
    cos_t = p.dram("cos_t", [128, SEQ], F32, kind="ExternalInput").ap()
    sin_t = p.dram("sin_t", [128, SEQ], F32, kind="ExternalInput").ap()
    e2sel = p.dram("e2sel", [128, 2], F32, kind="ExternalInput").ap()
    if kind == "l0":
        wqkv_d = p.dram(g("w_qkv"), [D, 3 * D], F32, kind="ExternalInput").ap()
        wo_d = p.dram(g("w_o"), [D, D], F32, kind="ExternalInput").ap()
    else:
        wq_d = p.dram(g("Wq"), [D, D], F32, kind="ExternalInput").ap()
        wk_d = p.dram(g("Wk"), [D, D], F32, kind="ExternalInput").ap()
        wv_d = p.dram(g("Wv"), [D, D], F32, kind="ExternalInput").ap()
        wo_d = p.dram(g("Wo"), [D, D], F32, kind="ExternalInput").ap()
        qnw_d = p.dram(g("q_norm_w"), [1, 64], F32, kind="ExternalInput").ap()
        knw_d = p.dram(g("k_norm_w"), [1, 64], F32, kind="ExternalInput").ap()
        cmask_d = p.dram("cmask", [128, 4 * 512], F32, kind="ExternalInput").ap()

    if env is None:
        env = make_env(p)
    s_x, s_v, s_s, s_t, s_st, s_w = (env[k] for k in
                                     ("s_x", "s_v", "s_s", "s_t", "s_st", "s_w"))
    p.scope_enter()

    xln = p.sbuf("xln", [128, KC * SEQ], BF16)
    wqkv = p.sbuf("wqkv", [128, KC * 2 * D], BF16)
    wo = p.sbuf("wo", [128, KC * D], BF16)
    vtm = p.sbuf("vtm", [128, 16 * D], BF16)
    vplus = p.sbuf("vplus", [128, 16 * 65], BF16)
    cos2 = p.sbuf("cos2", [128, SEQ], BF16)
    sin2 = p.sbuf("sin2", [128, SEQ], BF16)
    e2s = p.sbuf("e2s", [128, 2], BF16)
    negb = p.sbuf("negb", [128, 2], F32)
    ptb = p.sbuf("ptb", [128, 2 * 512], BF16)
    rden = p.sbuf("rden", [1, 512], F32)
    bcast = p.sbuf("bcast", [64, 512], BF16)
    attn = p.sbuf("attn", [128, KC * SEQ], BF16)
    outsc = p.sbuf("outsc", [128, 2 * 512], F32)
    qh = env["xbf"][:, 0:SEQ]
    kh = env["xbf"][:, SEQ:2 * SEQ]
    sqs = env["sqb"][:, 0:SEQ]
    rot = env["sqb"][:, SEQ:2 * SEQ]
    if kind == "l1":
        qnw = p.sbuf("qnw", [1, 64], F32)
        knw = p.sbuf("knw", [1, 64], F32)
        cmask = p.sbuf("cmask_s", [128, 4 * 512], BF16)

    ps_a, ps_o, ps_c = env["ps_a"], env["ps_o"], env["ps_c"]

    if kind == "l0":
        for c in range(KC):
            p.gdma(out=wqkv[:, c * 2 * D: c * 2 * D + D],
                   in_=wqkv_d[c * 128:(c + 1) * 128, 0:D], i=(s_w, 16))
            p.gdma(out=wqkv[:, c * 2 * D + D: c * 2 * D + 2 * D],
                   in_=wqkv_d[c * 128:(c + 1) * 128, D:2 * D], i=(s_w, 16))
    else:
        for c in range(KC):
            p.gdma(out=wqkv[:, c * 2 * D: c * 2 * D + D],
                   in_=wq_d[c * 128:(c + 1) * 128, :], i=(s_w, 16))
            p.gdma(out=wqkv[:, c * 2 * D + D: c * 2 * D + 2 * D],
                   in_=wk_d[c * 128:(c + 1) * 128, :], i=(s_w, 16))
    for c in range(KC):
        p.gdma(out=wo[:, c * D:(c + 1) * D],
               in_=wo_d[c * 128:(c + 1) * 128, :], i=(s_w, 16))
    p.gdma(out=cos2[:], in_=cos_t, i=(s_w, 16))
    p.gdma(out=sin2[:], in_=sin_t, i=(s_w, 16))
    p.gdma(out=e2s[:], in_=e2sel, i=(s_w, 16))
    if kind == "l1":
        p.gdma(out=qnw[:], in_=qnw_d, i=(s_w, 16))
        p.gdma(out=knw[:], in_=knw_d, i=(s_w, 16))
        p.gdma(out=cmask[:], in_=cmask_d, i=(s_w, 16))
    wload_total = s_w.n

    emit_env_init(p, env)

    HOFF = {"q": 0, "k": D}

    exp_bank = {0: 0, 1: 0}
    pv_bank = {0: 0, 1: 0}

    for b in range(NB):
        for c in range(KC):
            p.gdma(out=attn[:, c * D:(c + 1) * D],
                   in_=(wqkv_d[c * 128:(c + 1) * 128, 2 * D:3 * D] if kind == "l0"
                        else wv_d[c * 128:(c + 1) * 128, :]),
                   w=[(s_t, s_t.n), (s_v, s_v.n)], i=(s_w, 16))
        vw_loaded = s_w.n
        for blk in range(4):
            emit_ln_block(p, env, x,
                          lambda c, blk=blk: xln[:, c * SEQ + blk * 512:
                                                 c * SEQ + blk * 512 + 512],
                          b, blk)

        p.wait("tensor", s_w, vw_loaded)
        for tc in range(16):
            for half in range(2):
                bank = (2 * tc + half) % 2
                for c in range(KC):
                    p.mm(ps_o[:, bank * 512: bank * 512 + 512],
                         xln[:, c * SEQ + tc * 128: c * SEQ + (tc + 1) * 128],
                         attn[:, c * D + half * 512: c * D + (half + 1) * 512],
                         start=(c == 0), stop=(c == KC - 1),
                         w=([(s_v, s_v.n)] if c == 0 else ()),
                         i=((s_t, 1) if c == KC - 1 else None))
                p.vcopy(out=vtm[:, tc * D + half * 512: tc * D + (half + 1) * 512],
                        in_=ps_o[:, bank * 512: bank * 512 + 512],
                        w=[(s_t, s_t.n)], i=(s_v, 1))

        for hp in range(8):
            for (dst, key) in ((qh, "q"), (kh, "k")):
                for qb in range(4):
                    bank = qb % 2
                    for c in range(KC):
                        p.mm(ps_o[:, bank * 512: bank * 512 + 512],
                             wqkv[:, c * 2 * D + HOFF[key] + hp * 128:
                                  c * 2 * D + HOFF[key] + (hp + 1) * 128],
                             xln[:, c * SEQ + qb * 512: c * SEQ + (qb + 1) * 512],
                             start=(c == 0), stop=(c == KC - 1),
                             w=([(s_v, s_v.n)] if c == 0 else ()),
                             i=((s_t, 1) if c == KC - 1 else None))
                    p.vcopy(out=dst[:, qb * 512:(qb + 1) * 512],
                            in_=ps_o[:, bank * 512: bank * 512 + 512],
                            w=[(s_t, s_t.n)], i=(s_v, 1))

            if kind == "l1":
                for (dst, nw) in ((qh, qnw), (kh, knw)):
                    p.act(out=sqs[:], in_=dst[:], func=AF.Square,
                          w=[(s_v, s_v.n)], i=(s_s, 1))
                    for j in range(2):
                        for qb in range(4):
                            p.mm(ps_a[0:1, (qb % 2) * 512:(qb % 2) * 512 + 512],
                                 e2s[:, j:j + 1],
                                 sqs[:, qb * 512:(qb + 1) * 512],
                                 start=True, stop=True,
                                 w=[(s_s, s_s.n), (s_v, s_v.n)],
                                 i=(s_t, 1))
                            p.act(out=env["rows"][0:1, 0:512],
                                  in_=ps_a[0:1, (qb % 2) * 512:(qb % 2) * 512 + 512],
                                  func=AF.Ln, bias=env["eps6"][:], scale=1.0 / 64,
                                  w=[(s_t, s_t.n)], i=(s_s, 1))
                            p.act(out=env["rows"][0:1, 0:512], in_=env["rows"][0:1, 0:512],
                                  func=AF.Exp, scale=-0.5, i=(s_s, 1))
                            p.mm(ps_c[j * 64:(j + 1) * 64, 0:512], nw[:],
                                 env["rows"][0:1, 0:512],
                                 w=[(s_s, s_s.n), (s_v, s_v.n)], i=(s_t, 1))
                            p.tt(dst[j * 64:(j + 1) * 64, qb * 512:(qb + 1) * 512],
                                 dst[j * 64:(j + 1) * 64, qb * 512:(qb + 1) * 512],
                                 ps_c[j * 64:(j + 1) * 64, 0:512], AluOpType.mult,
                                 w=[(s_t, s_t.n)], i=(s_v, 1))

            for dst in (qh, kh):
                p.ts(rot[0:32, :], dst[32:64, :], -1.0, w=[(s_v, s_v.n), (s_s, s_s.n)], i=(s_v, 1))
                p.vcopy(rot[32:64, :], dst[0:32, :], i=(s_v, 1))
                p.ts(rot[64:96, :], dst[96:128, :], -1.0, i=(s_v, 1))
                p.vcopy(rot[96:128, :], dst[64:96, :], i=(s_v, 1))
                p.tt(rot[:], rot[:], sin2[:], AluOpType.mult, i=(s_v, 1))
                p.tt(dst[:], dst[:], cos2[:], AluOpType.mult, i=(s_v, 1))
                p.tt(dst[:], dst[:], rot[:], AluOpType.add, i=(s_v, 1))

            for (src, col) in ((qh, 0), (kh, 1)):
                p.act(out=sqs[:], in_=src[:], func=AF.Square,
                      w=[(s_v, s_v.n)], i=(s_s, 1))
                for j in range(2):
                    slot = 8 + 2 * col + j
                    for qb in range(4):
                        p.mm(ps_a[0:1, (qb % 2) * 512:(qb % 2) * 512 + 512],
                             e2s[:, j:j + 1], sqs[:, qb * 512:(qb + 1) * 512],
                             start=True, stop=True,
                             w=[(s_s, s_s.n), (s_v, s_v.n)], i=(s_t, 1))
                        p.vredmax(env["small"][0:1, slot:slot + 1]
                                  if qb == 0 else env["small"][0:1, 12:13],
                                  ps_a[0:1, (qb % 2) * 512:(qb % 2) * 512 + 512],
                                  w=[(s_t, s_t.n)], i=(s_v, 1))
                        if qb > 0:
                            p.tt(env["small"][0:1, slot:slot + 1],
                                 env["small"][0:1, slot:slot + 1],
                                 env["small"][0:1, 12:13], AluOpType.max, i=(s_v, 1))
            for j in range(2):
                pj = env["small"][0:1, 0 + j:1 + j]
                p.tt(pj, env["small"][0:1, 8 + j:9 + j],
                     env["small"][0:1, 10 + j:11 + j], AluOpType.mult,
                     w=[(s_v, s_v.n)], i=(s_v, 1))
                p.act(out=pj, in_=pj, func=AF.Ln, w=[(s_v, s_v.n)], i=(s_s, 1))
                p.act(out=pj, in_=pj, func=AF.Exp, scale=0.5, i=(s_s, 1))
                p.ts(pj, pj, -0.125, w=[(s_s, s_s.n)], i=(s_v, 1))
                p.mm(ps_c[:, 0:1], env["ones1f"][:], pj, w=[(s_v, s_v.n)], i=(s_t, 1))
                p.vcopy(negb[:, j:j + 1], ps_c[:, 0:1], w=[(s_t, s_t.n)], i=(s_v, 1))

            for j in range(2):
                hg = hp * 2 + j
                p.vmemset(vplus[:], 1.0, w=[(s_t, s_t.n)], i=(s_v, 1))
                for tc in range(16):
                    p.vcopy(vplus[:, tc * 65: tc * 65 + 64],
                            vtm[:, tc * D + hg * 64: tc * D + (hg + 1) * 64],
                            i=(s_v, 1))
                vplus_ready = s_v.n
                for qc in range(4):
                    nkt = 16 if kind == "l0" else (qc + 1) * 4
                    for kt in range(nkt):
                        par = kt % 2
                        wv = [(s_v, vplus_ready)]
                        if exp_bank[par]:
                            wv.append((s_s, exp_bank[par]))
                        p.mm(ps_a[:, par * 512:(par + 1) * 512],
                             kh[j * 64:(j + 1) * 64, kt * 128:(kt + 1) * 128],
                             qh[j * 64:(j + 1) * 64, qc * 512:(qc + 1) * 512],
                             w=wv, i=(s_t, 1))
                        smm_done = s_t.n
                        wv2 = [(s_t, smm_done)]
                        if pv_bank[par]:
                            wv2.append((s_t, pv_bank[par]))
                        p.act(out=ptb[:, par * 512:(par + 1) * 512],
                              in_=ps_a[:, par * 512:(par + 1) * 512],
                              func=AF.Exp, scale=0.125, bias=negb[:, j:j + 1],
                              w=wv2, i=(s_s, 1))
                        exp_bank[par] = s_s.n
                        wv3 = [(s_s, exp_bank[par])]
                        if kind == "l1" and kt >= 4 * qc:
                            p.tt(ptb[:, par * 512:(par + 1) * 512],
                                 ptb[:, par * 512:(par + 1) * 512],
                                 cmask[:, (kt - 4 * qc) * 512:(kt - 4 * qc + 1) * 512],
                                 AluOpType.mult, w=[(s_s, exp_bank[par])], i=(s_v, 1))
                            wv3 = [(s_v, s_v.n)]
                        if kt == 0:
                            wv3.append((s_v, s_v.n))
                        p.mm(ps_o[0:65, (qc % 2) * 512:(qc % 2) * 512 + 512],
                             vplus[:, kt * 65:(kt + 1) * 65],
                             ptb[:, par * 512:(par + 1) * 512],
                             start=(kt == 0), stop=(kt == nkt - 1),
                             w=wv3, i=(s_t, 1))
                        pv_bank[par] = s_t.n
                    p.vrecip(rden[:], ps_o[64:65, (qc % 2) * 512:(qc % 2) * 512 + 512],
                             w=[(s_t, s_t.n)], i=(s_v, 1))
                    p.mm(ps_c[0:64, 0:512], env["ones1f"][0:1, 0:64], rden[:],
                         w=[(s_v, s_v.n)], i=(s_t, 1))
                    p.scopy(out=bcast[:], in_=ps_c[0:64, 0:512],
                            w=[(s_t, s_t.n)], i=(s_s, 1))
                    p.tt(attn[(hg % 2) * 64:(hg % 2) * 64 + 64,
                              (hg // 2) * SEQ + qc * 512:(hg // 2) * SEQ + (qc + 1) * 512],
                         ps_o[0:64, (qc % 2) * 512:(qc % 2) * 512 + 512],
                         bcast[:], AluOpType.mult,
                         w=[(s_s, s_s.n)], i=(s_v, 1))

        for blk in range(4):
            tof = b * SEQ + blk * 512
            p.gdma(out=env["xbuf"][:].rearrange("p (c t) -> p c t", c=KC),
                   in_=x[:, tof:tof + 512].rearrange("(c p) t -> p c t", p=128),
                   w=[(s_v, s_v.n), (s_st, s_st.n)], i=(s_x, 16))
            store_done = {}
            for oc in range(KC):
                bank = oc % 2
                for c in range(KC):
                    p.mm(ps_o[:, bank * 512: bank * 512 + 512],
                         wo[:, c * D + oc * 128: c * D + (oc + 1) * 128],
                         attn[:, c * SEQ + blk * 512: c * SEQ + (blk + 1) * 512],
                         start=(c == 0), stop=(c == KC - 1),
                         w=([(s_v, s_v.n)] if c == 0 else ()),
                         i=((s_t, 1) if c == KC - 1 else None))
                wv = [(s_t, s_t.n), (s_x, s_x.n)]
                if oc - 2 in store_done:
                    wv.append((s_st, store_done[oc - 2]))
                p.tt(outsc[:, bank * 512: bank * 512 + 512],
                     ps_o[:, bank * 512: bank * 512 + 512],
                     env["xbuf"][:, oc * 512:(oc + 1) * 512], AluOpType.add,
                     w=wv, i=(s_v, 1))
                p.sydma(out=xn[oc * 128:(oc + 1) * 128, tof:tof + 512],
                        in_=outsc[:, bank * 512: bank * 512 + 512],
                        w=[(s_v, s_v.n)], i=(s_st, 16))
                store_done[oc] = s_st.n

    p.wait("sync", s_st, s_st.n)
    p.scope_exit()
    return p



def build_attn_a_l0(P=None, env=None, xap=None, xnap=None, nm=None):
    p = P if P is not None else Prog()
    nm = nm or {}
    g = lambda n: nm.get(n, n)
    if xap is None:
        x = p.dram("x", [D, T], F32, kind="ExternalInput").ap()
        xn = p.dram("xn", [D, T], F32, kind="ExternalOutput").ap()
    else:
        x, xn = xap, xnap
    qw_d = p.dram(g("q_w"), [D, D], F32, kind="ExternalInput").ap()
    kw_d = p.dram(g("k_w"), [D, D], F32, kind="ExternalInput").ap()
    vw_d = p.dram(g("v_w"), [D, D], F32, kind="ExternalInput").ap()
    ow_d = p.dram(g("out_w"), [D, D], F32, kind="ExternalInput").ap()
    qkb_d = p.dram("qkbT", [128, 16], F32, kind="ExternalInput").ap()
    vb_d = p.dram("aa0_vb", [1, D], F32, kind="ExternalInput").ap()
    gbo_d = p.dram("gboT", [128, 24], F32, kind="ExternalInput").ap()
    pja_d = p.dram("projT_aug", [65, 256], F32, kind="ExternalInput").ap()
    ut_d = p.dram("utmask", [128, 128], F32, kind="ExternalInput").ap()
    e2sel = p.dram("e2sel", [128, 2], F32, kind="ExternalInput").ap()

    if env is None:
        env = make_env(p)
    s_x, s_v, s_s, s_t, s_st, s_w = (env[k] for k in
                                     ("s_x", "s_v", "s_s", "s_t", "s_st", "s_w"))
    p.scope_enter()

    xln = p.sbuf("xln", [128, KC * SEQ], BF16)
    attn = p.sbuf("attn", [128, KC * SEQ], BF16)
    vtm = p.sbuf("vtm", [128, 16 * D], BF16)
    wqh = p.sbuf("wqh", [128, 2 * KC * 128], BF16)
    wkh = p.sbuf("wkh", [128, 2 * KC * 128], BF16)
    qaug = p.sbuf("qaug", [128, SEQ], BF16)
    kaug = p.sbuf("kaug", [128, SEQ], BF16)
    qpf = p.sbuf("qpf", [128, 2 * SEQ], BF16)
    kpf = p.sbuf("kpf", [128, 2 * SEQ], BF16)
    kptm = p.sbuf("kptm", [128, 16 * 256], BF16)
    pja = p.sbuf("pja", [128, 256], BF16)
    utm = p.sbuf("utm", [128, 128], BF16)
    e2s = p.sbuf("e2s", [128, 2], BF16)
    vplus = p.sbuf("vplus", [128, 16 * 65], BF16)
    Ss = p.sbuf("Ss", [128, 2 * 65], F32)
    Sb = p.sbuf("Sb", [128, 2 * 65], BF16)
    abuf = p.sbuf("abuf", [128, 2 * 128], BF16)
    qkb = p.sbuf("qkb", [128, 16], F32)
    gbo = p.sbuf("gbo", [128, 24], F32)
    vbt = p.sbuf("vbt", [128, 2 * 512], F32)
    rden = p.sbuf("rdenA", [1, 128], F32)
    bca = p.sbuf("bca", [64, 128], BF16)
    outsc = qpf[:, 0:2048].bitcast(F32)
    ones128f = p.sbuf("ones128f", [128, 1], F32)
    qh_pair = env["xbf"][:, 0:SEQ]
    kh_pair = env["xbf"][:, SEQ:2 * SEQ]
    sqs = env["sqb"][:, 0:SEQ]
    resbuf = env["xbuf"]

    ps_a, ps_o, ps_c, ps_bc = env["ps_a"], env["ps_o"], env["ps_c"], env["ps_bc"]

    for (dst, srcd) in ((pja[0:65, :], pja_d), (utm[:], ut_d), (e2s[:], e2sel),
                        (qkb[:], qkb_d), (gbo[:], gbo_d)):
        p.gdma(out=dst, in_=srcd, i=(s_w, 16))
    wload_total = s_w.n
    emit_env_init(p, env)
    p.vmemset(ones128f[:], 1.0, i=(s_v, 1))
    vbrow = p.sbuf("vbrow", [1, D], F32)
    p.gdma(out=vbrow[:], in_=vb_d, i=(s_w, 16))
    p.wait("tensor", s_w, s_w.n)
    for half in range(2):
        p.mm(ps_c[:, 0:512], env["ones1f"][:], vbrow[0:1, half * 512:(half + 1) * 512],
             w=[(s_v, s_v.n)], i=(s_t, 1))
        p.vcopy(vbt[:, half * 512:(half + 1) * 512], ps_c[:, 0:512],
                w=[(s_t, s_t.n)], i=(s_v, 1))

    at_bank = {0: 0, 1: 0}
    nd_bank = {0: 0, 1: 0}

    for b in range(NB):
        for c in range(KC):
            p.gdma(out=attn[:, c * D:(c + 1) * D],
                   in_=vw_d[c * 128:(c + 1) * 128, :],
                   w=[(s_t, s_t.n), (s_v, s_v.n)], i=(s_w, 16))
        vw_loaded = s_w.n

        for blk in range(4):
            emit_ln_block(p, env, x,
                          lambda c, blk=blk: xln[:, c * SEQ + blk * 512:
                                                 c * SEQ + blk * 512 + 512],
                          b, blk)

        p.wait("tensor", s_w, vw_loaded)
        for tc in range(16):
            for half in range(2):
                bank = half
                for c in range(KC):
                    p.mm(ps_o[:, bank * 512: bank * 512 + 512],
                         xln[:, c * SEQ + tc * 128: c * SEQ + (tc + 1) * 128],
                         attn[:, c * D + half * 512: c * D + (half + 1) * 512],
                         start=(c == 0), stop=(c == KC - 1),
                         w=([(s_v, s_v.n)] if c == 0 else ()),
                         i=((s_t, 1) if c == KC - 1 else None))
                p.tt(vtm[:, tc * D + half * 512: tc * D + (half + 1) * 512],
                     ps_o[:, bank * 512: bank * 512 + 512],
                     vbt[:, half * 512:(half + 1) * 512], AluOpType.add,
                     w=[(s_t, s_t.n)], i=(s_v, 1))

        wh_done = {0: 0, 1: 0}
        for hp in range(8):
            hpar = hp % 2
            for (whdst, wsrcd) in ((wqh, qw_d), (wkh, kw_d)):
                p.gdma(out=whdst[:, hpar * KC * 128:(hpar + 1) * KC * 128].rearrange(
                           "p (c o) -> p c o", c=KC),
                       in_=wsrcd[:, hp * 128:(hp + 1) * 128].rearrange(
                           "(c p) o -> p c o", p=128),
                       w=([(s_t, wh_done[hpar])] if wh_done[hpar] else ()),
                       i=(s_w, 16))
            wh_ld = s_w.n
            p.wait("tensor", s_w, wh_ld)
            for (dst, wsrc, bcol) in ((qh_pair, wqh, hp), (kh_pair, wkh, 8 + hp)):
                for qb in range(4):
                    bank = qb % 2
                    for c in range(KC):
                        p.mm(ps_o[:, bank * 512: bank * 512 + 512],
                             wsrc[:, hpar * KC * 128 + c * 128:
                                  hpar * KC * 128 + (c + 1) * 128],
                             xln[:, c * SEQ + qb * 512: c * SEQ + (qb + 1) * 512],
                             start=(c == 0), stop=(c == KC - 1),
                             w=([(s_v, s_v.n)] if c == 0 else ()),
                             i=((s_t, 1) if c == KC - 1 else None))
                    p.ts(dst[:, qb * 512:(qb + 1) * 512],
                         ps_o[:, bank * 512: bank * 512 + 512],
                         qkb[:, bcol:bcol + 1], op0=AluOpType.add,
                         w=[(s_t, s_t.n)], i=(s_v, 1))
            wh_done[hpar] = s_t.n
            for j in range(2):
                h = hp * 2 + j
                for (aug, pair) in ((qaug, qh_pair), (kaug, kh_pair)):
                    p.vcopy(aug[0:64, :], pair[j * 64:(j + 1) * 64, :],
                            w=[(s_v, s_v.n)], i=(s_v, 1))
                    p.act(out=sqs[:], in_=pair[:], func=AF.Square,
                          w=[(s_v, s_v.n)], i=(s_s, 1))
                    for qb in range(4):
                        p.mm(ps_c[0:1, 0:512], e2s[:, j:j + 1],
                             sqs[:, qb * 512:(qb + 1) * 512],
                             w=[(s_s, s_s.n), (s_v, s_v.n)], i=(s_t, 1))
                        p.ts(aug[64:65, qb * 512:(qb + 1) * 512], ps_c[0:1, 0:512],
                             -1.0 / 16, op0=AluOpType.mult,
                             w=[(s_t, s_t.n)], i=(s_v, 1))
                for (src, dst) in ((qaug, qpf), (kaug, kpf)):
                    for mc in range(2):
                        for qb in range(4):
                            bank = qb % 2
                            p.mm(ps_o[:, bank * 512: bank * 512 + 512],
                                 pja[0:65, mc * 128:(mc + 1) * 128],
                                 src[0:65, qb * 512:(qb + 1) * 512],
                                 w=[(s_v, s_v.n), (s_s, s_s.n)], i=(s_t, 1))
                            p.act(out=dst[:, mc * SEQ + qb * 512: mc * SEQ + (qb + 1) * 512],
                                  in_=ps_o[:, bank * 512: bank * 512 + 512],
                                  func=AF.Exp, scale=0.125,
                                  w=[(s_t, s_t.n)], i=(s_s, 1))
                for tc in range(16):
                    bank = tc % 2
                    p.mm(ps_o[:, bank * 512: bank * 512 + 256],
                         kaug[0:65, tc * 128:(tc + 1) * 128],
                         pja[0:65, :],
                         w=[(s_s, s_s.n)], i=(s_t, 1))
                    p.act(out=kptm[:, tc * 256:(tc + 1) * 256],
                          in_=ps_o[:, bank * 512: bank * 512 + 256],
                          func=AF.Exp, scale=0.125,
                          w=[(s_t, s_t.n)], i=(s_s, 1))
                p.vmemset(vplus[:], 1.0, w=[(s_t, s_t.n)], i=(s_v, 1))
                for tc in range(16):
                    p.vcopy(vplus[:, tc * 65: tc * 65 + 64],
                            vtm[:, tc * D + h * 64: tc * D + (h + 1) * 64],
                            i=(s_v, 1))
                p.vmemset(Ss[:], 0.0, i=(s_v, 1))
                p.vmemset(Sb[:], 0.0, i=(s_v, 1))
                ready = s_v.n
                p.wait("scalar", s_v, ready)
                for tc in range(16):
                    par = tc % 2
                    wv = [(s_s, s_s.n), (s_v, max(ready, at_bank[par]))]
                    for mc in range(2):
                        p.mm(ps_a[:, par * 512: par * 512 + 128],
                             kpf[:, mc * SEQ + tc * 128: mc * SEQ + (tc + 1) * 128],
                             qpf[:, mc * SEQ + tc * 128: mc * SEQ + (tc + 1) * 128],
                             start=(mc == 0), stop=(mc == 1),
                             w=(wv if mc == 0 else ()),
                             i=((s_t, 1) if mc == 1 else None))
                    p.tt(abuf[:, par * 128:(par + 1) * 128],
                         ps_a[:, par * 512: par * 512 + 128], utm[:],
                         AluOpType.mult, w=[(s_t, s_t.n)], i=(s_v, 1))
                    at_bank[par] = s_v.n
                    wv2 = [(s_v, s_v.n)]
                    if nd_bank[par]:
                        wv2.append((s_v, nd_bank[par]))
                    p.mm(ps_o[0:65, par * 512: par * 512 + 128],
                         Sb[:, 0:65], qpf[:, 0 * SEQ + tc * 128:(tc + 1) * 128],
                         start=True, stop=False, w=wv2, i=None)
                    p.mm(ps_o[0:65, par * 512: par * 512 + 128],
                         Sb[:, 65:130], qpf[:, SEQ + tc * 128: SEQ + (tc + 1) * 128],
                         start=False, stop=False, i=None)
                    p.mm(ps_o[0:65, par * 512: par * 512 + 128],
                         vplus[:, tc * 65:(tc + 1) * 65],
                         abuf[:, par * 128:(par + 1) * 128],
                         start=False, stop=True, i=(s_t, 1))
                    numden_done = s_t.n
                    for mc in range(2):
                        p.mm(ps_bc[:, mc * 512: mc * 512 + 65],
                             kptm[:, tc * 256 + mc * 128: tc * 256 + (mc + 1) * 128],
                             vplus[:, tc * 65:(tc + 1) * 65],
                             w=([(s_v, s_v.n)] if mc == 0 else ()), i=(s_t, 1))
                    p.wait("vector", s_t, s_t.n)
                    for mc in range(2):
                        p.tt(Ss[:, mc * 65:(mc + 1) * 65], Ss[:, mc * 65:(mc + 1) * 65],
                             ps_bc[:, mc * 512: mc * 512 + 65], AluOpType.add,
                             i=(s_v, 1))
                        p.vcopy(Sb[:, mc * 65:(mc + 1) * 65],
                                Ss[:, mc * 65:(mc + 1) * 65], i=(s_v, 1))
                    p.vrecip(rden[:], ps_o[64:65, par * 512: par * 512 + 128],
                             w=[(s_t, numden_done)], i=(s_v, 1))
                    p.mm(ps_c[0:64, 0:128], env["ones1f"][0:1, 0:64], rden[:],
                         w=[(s_v, s_v.n)], i=(s_t, 1))
                    p.scopy(out=bca[:], in_=ps_c[0:64, 0:128],
                            w=[(s_t, s_t.n)], i=(s_s, 1))
                    p.tt(attn[(h % 2) * 64:(h % 2) * 64 + 64,
                              (h // 2) * SEQ + tc * 128:(h // 2) * SEQ + (tc + 1) * 128],
                         ps_o[0:64, par * 512: par * 512 + 128],
                         bca[:], AluOpType.mult,
                         w=[(s_s, s_s.n)], i=(s_v, 1))
                    nd_bank[par] = s_v.n

        for c in range(KC):
            p.gdma(out=vtm[:, c * D:(c + 1) * D], in_=ow_d[c * 128:(c + 1) * 128, :],
                   w=[(s_t, s_t.n), (s_v, s_v.n)], i=(s_w, 16))
        ow_loaded = s_w.n
        p.wait("tensor", s_w, ow_loaded)
        for blk in range(4):
            tof = b * SEQ + blk * 512
            for oc in range(KC):
                bank = oc % 2
                for c in range(KC):
                    p.mm(ps_o[:, bank * 512: bank * 512 + 512],
                         vtm[:, c * D + oc * 128: c * D + (oc + 1) * 128],
                         attn[:, c * SEQ + blk * 512: c * SEQ + (blk + 1) * 512],
                         start=(c == 0), stop=(c == KC - 1),
                         w=([(s_v, s_v.n)] if c == 0 else ()),
                         i=((s_t, 1) if c == KC - 1 else None))
                p.ts(resbuf[:, oc * 512:(oc + 1) * 512],
                     ps_o[:, bank * 512: bank * 512 + 512],
                     gbo[:, 16 + oc:16 + oc + 1], op0=AluOpType.add,
                     w=[(s_t, s_t.n)], i=(s_v, 1))
                p.tt(resbuf[:, oc * 512:(oc + 1) * 512],
                     resbuf[:, oc * 512:(oc + 1) * 512],
                     xln[:, oc * SEQ + blk * 512: oc * SEQ + (blk + 1) * 512],
                     AluOpType.add, i=(s_v, 1))
            p.act(out=env["sqb"][:], in_=resbuf[:], func=AF.Square,
                  w=[(s_v, s_v.n)], i=(s_s, 1))
            for c in range(KC):
                p.mm(ps_a[0:1, 0:512], ones128f[:], resbuf[:, c * 512:(c + 1) * 512],
                     start=(c == 0), stop=(c == KC - 1),
                     w=([(s_v, s_v.n)] if c == 0 else ()),
                     i=((s_t, 1) if c == KC - 1 else None))
            for c in range(KC):
                p.mm(ps_a[0:1, 512:1024], env["ones128"][:],
                     env["sqb"][:, c * 512:(c + 1) * 512],
                     start=(c == 0), stop=(c == KC - 1),
                     w=([(s_s, s_s.n)] if c == 0 else ()),
                     i=((s_t, 1) if c == KC - 1 else None))
            mu = env["rows"][0:1, 0:512]
            var = env["rows"][0:1, 512:1024]
            p.ts(mu, ps_a[0:1, 0:512], 1.0 / D, w=[(s_t, s_t.n)], i=(s_v, 1))
            p.ts(var, ps_a[0:1, 512:1024], 1.0 / D, i=(s_v, 1))
            p.tt(env["lnscr"][0:1, 0:512], mu, mu, AluOpType.mult, i=(s_v, 1))
            p.tt(var, var, env["lnscr"][0:1, 0:512], AluOpType.subtract, i=(s_v, 1))
            p.act(out=var, in_=var, func=AF.Ln, bias=env["eps6"][:],
                  w=[(s_v, s_v.n)], i=(s_s, 1))
            p.act(out=var, in_=var, func=AF.Exp, scale=-0.5, i=(s_s, 1))
            p.tt(mu, mu, var, AluOpType.mult, w=[(s_s, s_s.n)], i=(s_v, 1))
            p.mm(ps_bc[:, 0:512], env["ones1f"][:], var, w=[(s_v, s_v.n)], i=(s_t, 1))
            p.mm(ps_bc[:, 512:1024], env["ones1f"][:], mu, i=(s_t, 1))
            store_done = {}
            for oc in range(KC):
                par2 = oc % 2
                p.gdma(out=env["lnscr"][:],
                       in_=x[:, tof:tof + 512].rearrange("(c p) t -> p c t", p=128)[
                           :, oc, :] if False else
                           x[oc * 128:(oc + 1) * 128, tof:tof + 512],
                       w=[(s_v, s_v.n)], i=(s_x, 16))
                o = outsc[:, par2 * 512:(par2 + 1) * 512]
                wv = [(s_t, s_t.n), (s_x, s_x.n)]
                if oc - 2 in store_done:
                    wv.append((s_st, store_done[oc - 2]))
                p.tt(o, resbuf[:, oc * 512:(oc + 1) * 512], ps_bc[:, 0:512],
                     AluOpType.mult, w=wv, i=(s_v, 1))
                p.tt(o, o, ps_bc[:, 512:1024], AluOpType.subtract, i=(s_v, 1))
                p.ts(o, o, gbo[:, oc:oc + 1], gbo[:, 8 + oc:8 + oc + 1],
                     op0=AluOpType.mult, op1=AluOpType.add, i=(s_v, 1))
                p.tt(o, o, env["lnscr"][:], AluOpType.add, i=(s_v, 1))
                p.sydma(out=xn[oc * 128:(oc + 1) * 128, tof:tof + 512],
                        in_=o, w=[(s_v, s_v.n)], i=(s_st, 16))
                store_done[oc] = s_st.n

    p.wait("sync", s_st, s_st.n)
    p.scope_exit()
    return p


def run_attn_a_l0(x_shards, q_w, q_b, k_w, k_b, v_w, v_b, out_w, out_b, proj,
                  gamma, beta):
    consts = _host_consts_attn()
    nc = _get_prog("attn_a_l0", lambda: build_attn_a_l0())
    qkbT = np.zeros((128, 16), np.float32)
    qkbT[:, 0:8] = np.asarray(q_b, np.float32).reshape(8, 128).T
    qkbT[:, 8:16] = np.asarray(k_b, np.float32).reshape(8, 128).T
    gboT = np.zeros((128, 24), np.float32)
    gboT[:, 0:8] = np.asarray(gamma, np.float32).reshape(8, 128).T
    gboT[:, 8:16] = np.asarray(beta, np.float32).reshape(8, 128).T
    gboT[:, 16:24] = np.asarray(out_b, np.float32).reshape(8, 128).T
    pja = np.ones((65, 256), np.float32)
    pja[0:64, :] = np.asarray(proj, np.float32).T
    ut = (np.arange(128)[:, None] <= np.arange(128)[None, :]).astype(np.float32)
    in_maps = []
    for c in range(NCORES):
        m = {"x": x_shards[c], "q_w": np.asarray(q_w, np.float32),
             "k_w": np.asarray(k_w, np.float32), "v_w": np.asarray(v_w, np.float32),
             "out_w": np.asarray(out_w, np.float32),
             "qkbT": qkbT, "aa0_vb": np.asarray(v_b, np.float32).reshape(1, D),
             "gboT": gboT, "projT_aug": pja, "utmask": ut,
             "e2sel": consts["e2sel"]}
        in_maps.append(m)
    res = run_bass_kernel_spmd(nc, in_maps, core_ids=list(range(NCORES)))
    return [np.asarray(r["xn"]) for r in res.results]



def build_ffn_l0(mode="full", P=None, env=None, xap=None, xnap=None, nm=None):
    p = P if P is not None else Prog()
    nm = nm or {}
    g = lambda n: nm.get(n, n)
    dbg = (mode == "wdump")
    if xap is None:
        x = p.dram("x", [D, T], F32, kind="ExternalInput").ap()
        xn = p.dram("xn", [D, T], F32, kind="ExternalOutput").ap()
    else:
        x, xn = xap, xnap
    wg_d = p.dram(g("Wg"), [D, 8], F32, kind="ExternalInput").ap()
    w1_d = p.dram(g("W1"), [8, D, 4096], F32, kind="ExternalInput").ap()
    b1_d = p.dram("b1T", [128, 256], F32, kind="ExternalInput").ap()
    w2_d = p.dram(g("W2"), [8, 4096, D], F32, kind="ExternalInput").ap()
    b2_d = p.dram("b2T", [128, 64], F32, kind="ExternalInput").ap()
    e8s_d = p.dram("E8sel", [128, 8], F32, kind="ExternalInput").ap()
    e8t_d = p.dram("E8T", [8, 128], F32, kind="ExternalInput").ap()

    cc_in = p.dram("cc_in", [8, T], F32)
    cc_out = p.dram("cc_out", [64, T], F32)
    xlnd = p.dram("xlnd", [D, T], BF16)
    wdram0 = p.dram("wdram0", [8, T], BF16)
    if dbg:
        wout = p.dram("wout", [8, T], BF16, kind="ExternalOutput").ap()
        thout = p.dram("thout", [8, 8], F32, kind="ExternalOutput").ap()
        xlout = p.dram("xlout", [D, 512], BF16, kind="ExternalOutput").ap()
        scout = p.dram("scout", [64, T], F32, kind="ExternalOutput").ap()

    if env is None:
        env = make_env(p)
    s_x, s_v, s_s, s_t, s_st, s_w = (env[k] for k in
                                     ("s_x", "s_v", "s_s", "s_t", "s_st", "s_w"))
    s_ac = env["s_ac"]
    s_wr = env["s_wr"]
    s_cc = env["s_cc"]
    p.scope_enter()

    w1r = p.sbuf("w1r", [128, KC * 4096], BF16)
    w2r = p.sbuf("w2r", [128, 32 * D], BF16)
    hbuf = p.sbuf("hbuf", [128, 32 * 512], BF16)
    wgb = p.sbuf("wgb", [128, KC * 8], BF16)
    b1T = p.sbuf("b1Ts", [128, 256], F32)
    b2T = p.sbuf("b2Ts", [128, 64], F32)
    e8s = p.sbuf("e8s", [128, 8], F32)
    e8t = p.sbuf("e8t", [8, 128], F32)
    scstage = p.sbuf("scstage", [8, 1024], F32)
    th = p.sbuf("th", [8, 8], F32)
    rowcnt = p.sbuf("rowcnt", [128, 1], F32)
    wrow = p.sbuf("wrow", [1, 2 * 512], BF16)
    xlnb = hbuf[:, 0:4096]
    sc128 = hbuf[0:64, 0:8192].bitcast(F32)
    sloc = hbuf[0:8, 8192:16384].bitcast(F32)
    w8 = hbuf[0:8, 4096:8192]
    xt2a = env["xbuf"][:, 0:2048].bitcast(BF16)
    xt2b = env["xbuf"][:, 2048:4096].bitcast(BF16)
    ysc0 = env["xbf"][:, 0:1024].bitcast(F32)
    ysc1 = env["xbf"][:, 1024:2048].bitcast(F32)
    wtile = env["sqb"][:, 0:1024].bitcast(F32)

    ps_a, ps_o, ps_c, ps_bc = env["ps_a"], env["ps_o"], env["ps_c"], env["ps_bc"]

    for c in range(KC):
        p.gdma(out=wgb[:, c * 8:(c + 1) * 8], in_=wg_d[c * 128:(c + 1) * 128, :],
               i=(s_w, 16))
    for (dst, srcd) in ((b1T, b1_d), (b2T, b2_d), (e8s, e8s_d), (e8t, e8t_d)):
        p.gdma(out=dst[:], in_=srcd, i=(s_w, 16))
    p.gdma(out=xn, in_=x, i=(s_ac, 16))
    xn_init = s_ac.n
    wload_total = s_w.n
    emit_env_init(p, env)
    p.wait("vector", s_w, wload_total)
    p.wait("tensor", s_w, wload_total)

    for gblk in range(8):
        b, blk = divmod(gblk, 4)
        emit_ln_block(p, env, x,
                      lambda c: xlnb[:, c * 512:(c + 1) * 512],
                      b, blk)
        tof = gblk * 512
        p.gdma(out=xlnd[:, tof:tof + 512].rearrange("(c p) t -> p c t", p=128),
               in_=xlnb.rearrange("p (c t) -> p c t", c=KC),
               w=[(s_v, s_v.n)], i=(s_x, 16))
        for c in range(KC):
            p.mm(ps_c[0:8, 0:512], wgb[:, c * 8:(c + 1) * 8],
                 xlnb[:, c * 512:(c + 1) * 512],
                 start=(c == 0), stop=(c == KC - 1),
                 w=([(s_v, s_v.n), (s_s, s_s.n)] if c == 0 else ()),
                 i=((s_t, 1) if c == KC - 1 else None))
        par = gblk % 2
        p.act(out=scstage[0:8, par * 512:(par + 1) * 512], in_=ps_c[0:8, 0:512],
              func=AF.Sigmoid, w=[(s_t, s_t.n), (s_x, s_x.n)], i=(s_s, 1))
        p.gdma(out=cc_in.ap()[0:8, tof:tof + 512],
               in_=scstage[0:8, par * 512:(par + 1) * 512],
               w=[(s_s, s_s.n)], i=(s_x, 16))

    p.collective("AllGather", AluOpType.bypass, [list(range(8))],
                 [cc_in.ap().opt()], [cc_out.ap().opt()],
                 w=[(s_x, s_x.n)], i=(s_cc, 1))
    p.wait("gpsimd", s_cc, s_cc.n)
    p.gdma(out=sc128, in_=cc_out.ap(),
           w=[(s_t, s_t.n), (s_v, s_v.n), (s_s, s_s.n)], i=(s_x, 16))
    sc_ready = s_x.n

    cmpscr = env["xbuf"][0:64, :]

    def VS(fn):
        fn()
        p.wait("vector", s_v, s_v.n)
        p.wait("scalar", s_s, s_s.n)
        p.wait("vector", s_s, s_s.n)
        p.wait("scalar", s_v, s_v.n)
        p.wait("tensor", s_v, s_v.n)
        p.wait("vector", s_t, s_t.n)

    lo = th[0:8, 0:1]
    hi = th[0:8, 1:2]
    mid = th[0:8, 2:3]
    cge = th[0:8, 3:4]
    kk = th[0:8, 4:5]
    VS(lambda: p.vmemset(lo, 0.0, w=[(s_x, sc_ready)], i=(s_v, 1)))
    VS(lambda: p.vmemset(hi, 1.0, i=(s_v, 1)))
    VS(lambda: p.vmemset(mid, 0.5, i=(s_v, 1)))
    VS(lambda: p.vmemset(kk, 8192.0, i=(s_v, 1)))
    for it in range(44):
        VS(lambda: p.mm(ps_c[0:64, 0:1], e8t[0:8, 0:64], mid, i=(s_t, 1)))
        VS(lambda: p.ts(cmpscr, sc128, ps_c[0:64, 0:1],
                        s2=0.0, op0=AluOpType.is_ge, op1=AluOpType.add,
                        accum=rowcnt[0:64, :], i=(s_v, 1)))
        VS(lambda: p.mm(ps_bc[0:8, 0:1], e8s[0:64, :], rowcnt[0:64, :], i=(s_t, 1)))
        VS(lambda: p.ts(cge, ps_bc[0:8, 0:1], kk, op0=AluOpType.is_ge, i=(s_v, 1)))
        t1 = th[0:8, 5:6]
        VS(lambda: p.tt(t1, mid, lo, AluOpType.subtract, i=(s_v, 1)))
        VS(lambda: p.tt(t1, cge, t1, AluOpType.mult, i=(s_v, 1)))
        VS(lambda: p.tt(lo, lo, t1, AluOpType.add, i=(s_v, 1)))
        VS(lambda: p.tt(t1, hi, mid, AluOpType.subtract, i=(s_v, 1)))
        VS(lambda: p.tt(t1, cge, t1, AluOpType.mult, i=(s_v, 1)))
        VS(lambda: p.tt(hi, mid, t1, AluOpType.add, i=(s_v, 1)))
        VS(lambda: p.ts(mid, lo, hi, 0.5, op0=AluOpType.add, op1=AluOpType.mult,
                        i=(s_v, 1)))
    slocal = hbuf[0:8, 8192:16384].bitcast(F32)
    mk = hbuf[0:8, 0:4096].bitcast(F32)
    VS(lambda: p.gdma(out=slocal, in_=cc_in.ap(), w=[(s_v, s_v.n)], i=(s_x, 16)))
    for half in range(2):
        sl_h = hbuf[0:8, 8192 + half * 4096: 8192 + (half + 1) * 4096].bitcast(F32)
        w8_h = hbuf[0:8, 4096 + half * 2048: 4096 + (half + 1) * 2048]
        VS(lambda sl_h=sl_h: p.ts(mk, sl_h, lo, op0=AluOpType.is_ge,
                                  w=[(s_x, s_x.n)], i=(s_v, 1)))
        VS(lambda sl_h=sl_h, w8_h=w8_h: p.tt(w8_h, mk, sl_h,
                                             AluOpType.mult, i=(s_v, 1)))
    VS(lambda: p.gdma(out=wdram0.ap(), in_=w8, w=[(s_v, s_v.n)], i=(s_wr, 16)))
    wdram_ready = s_wr.n
    if dbg:
        p.gdma(out=wout, in_=wdram0.ap(), w=[(s_wr, wdram_ready)], i=(s_wr, 16))
        p.gdma(out=thout, in_=th[:], w=[(s_v, s_v.n)], i=(s_wr, 16))
        p.gdma(out=xlout, in_=xlnd.ap()[:, 0:512], i=(s_wr, 16))
        p.gdma(out=scout, in_=cc_out.ap(), i=(s_wr, 16))

    ysc_dma = {0: 0, 1: 0}
    xt_done = {0: 0, 1: 0}
    wpar = 0
    for e in range(8):
        wv = [(s_t, wpar)] if wpar else [(s_v, s_v.n), (s_s, s_s.n)]
        for c in range(KC):
            p.gdma(out=w1r[:, c * 4096:(c + 1) * 4096],
                   in_=w1_d[e, c * 128:(c + 1) * 128, :],
                   w=(wv if c == 0 else ()), i=(s_w, 16))
        for q in range(4):
            p.gdma(out=w2r[:, q * 8 * D:(q + 1) * 8 * D].rearrange(
                       "p (c o) -> p c o", c=8),
                   in_=w2_d[e, q * 8 * 128:(q + 1) * 8 * 128, :].rearrange(
                       "(c p) o -> p c o", p=128), i=(s_w, 16))
        p.wait("tensor", s_w, s_w.n)
        for tb in range(8):
            par = tb % 2
            xt2 = xt2a if par == 0 else xt2b
            p.gdma(out=xt2.rearrange("p (c t) -> p c t", c=KC),
                   in_=xlnd.ap()[:, tb * 512:(tb + 1) * 512].rearrange(
                       "(c p) t -> p c t", p=128),
                   w=[(s_t, xt_done[par])] if xt_done[par] else [(s_x, s_x.n)],
                   i=(s_x, 16))
            xt_ld = s_x.n
            p.gdma(out=wrow[0:1, par * 512:(par + 1) * 512],
                   in_=wdram0.ap()[e:e + 1, tb * 512:(tb + 1) * 512],
                   w=[(s_wr, wdram_ready), (s_t, s_t.n)], i=(s_wr, 16))
            p.wait("tensor", s_wr, s_wr.n)
            p.mm(ps_bc[:, 0:512], env["ones1b"][:],
                 wrow[0:1, par * 512:(par + 1) * 512],
                 w=[(s_s, s_s.n)], i=(s_t, 1))
            p.scopy(out=wtile, in_=ps_bc[:, 0:512], w=[(s_t, s_t.n)], i=(s_s, 1))
            p.wait("tensor", s_x, xt_ld)
            for fs in range(32):
                bank = fs % 2
                for c in range(KC):
                    p.mm(ps_a[:, bank * 512: bank * 512 + 512],
                         w1r[:, c * 4096 + fs * 128: c * 4096 + (fs + 1) * 128],
                         xt2[:, c * 512:(c + 1) * 512],
                         start=(c == 0), stop=(c == KC - 1),
                         w=([(s_s, s_s.n)] if c == 0 else ()),
                         i=((s_t, 1) if c == KC - 1 else None))
                p.act(out=hbuf[:, fs * 512:(fs + 1) * 512],
                      in_=ps_a[:, bank * 512: bank * 512 + 512],
                      func=AF.Gelu, bias=b1T[:, e * 32 + fs: e * 32 + fs + 1],
                      w=[(s_t, s_t.n)], i=(s_s, 1))
            xt_done[par] = s_t.n
            for oc in range(KC):
                bank = oc % 2
                for fc in range(32):
                    p.mm(ps_o[:, bank * 512: bank * 512 + 512],
                         w2r[:, fc * D + oc * 128: fc * D + (oc + 1) * 128],
                         hbuf[:, fc * 512:(fc + 1) * 512],
                         start=(fc == 0), stop=(fc == 31),
                         w=([(s_s, s_s.n), (s_v, s_v.n)] if fc == 0 else ()),
                         i=((s_t, 1) if fc == 31 else None))
                ysc = ysc0 if bank == 0 else ysc1
                p.ts(ysc, ps_o[:, bank * 512: bank * 512 + 512],
                     b2T[:, e * 8 + oc: e * 8 + oc + 1], op0=AluOpType.add,
                     w=[(s_t, s_t.n), (s_ac, max(xn_init, ysc_dma[bank]))],
                     i=(s_v, 1))
                p.tt(ysc, ysc, wtile, AluOpType.mult, i=(s_v, 1))
                p.gdma(out=xn[oc * 128:(oc + 1) * 128, tb * 512:(tb + 1) * 512],
                       in_=ysc, accum_op=AluOpType.add,
                       w=[(s_v, s_v.n)], i=(s_ac, 16))
                ysc_dma[bank] = s_ac.n
        wpar = s_t.n

    p.wait("gpsimd", s_ac, s_ac.n)
    p.scope_exit()
    return p


def run_ffn_l0(x_shards, Wg, W1, b1, W2, b2, mode="full"):
    nc = _get_prog("ffn_l0_" + mode, lambda: build_ffn_l0(mode))
    b1T = np.ascontiguousarray(np.asarray(b1, np.float32).reshape(8, 32, 128)
                               .transpose(2, 0, 1).reshape(128, 256))
    b2T = np.ascontiguousarray(np.asarray(b2, np.float32).reshape(8, 8, 128)
                               .transpose(2, 0, 1).reshape(128, 64))
    E8sel = np.zeros((128, 8), np.float32)
    for pp in range(64):
        E8sel[pp, pp % 8] = 1.0
    E8T = np.ascontiguousarray(E8sel.T)
    in_maps = []
    for c in range(NCORES):
        m = {"x": x_shards[c], "Wg": np.asarray(Wg, np.float32),
             "W1": np.asarray(W1, np.float32), "b1T": b1T,
             "W2": np.asarray(W2, np.float32), "b2T": b2T,
             "E8sel": E8sel, "E8T": E8T}
        in_maps.append(m)
    res = run_bass_kernel_spmd(nc, in_maps, core_ids=list(range(NCORES)))
    if mode == "wdump":
        return ([np.asarray(r["xn"]) for r in res.results],
                [(np.asarray(r["wout"]), np.asarray(r["thout"]),
                  np.asarray(r["xlout"]), np.asarray(r["scout"])) for r in res.results])
    return [np.asarray(r["xn"]) for r in res.results]



def build_attn_a_l1(P=None, env=None, xap=None, xnap=None, nm=None):
    p = P if P is not None else Prog()
    nm = nm or {}
    g = lambda n: nm.get(n, n)
    if xap is None:
        x = p.dram("x", [D, T], F32, kind="ExternalInput").ap()
        xn = p.dram("xn", [D, T], F32, kind="ExternalOutput").ap()
    else:
        x, xn = xap, xnap
    wq_d = p.dram(g("W_q"), [D, D], F32, kind="ExternalInput").ap()
    wk_d = p.dram(g("W_k"), [D, D], F32, kind="ExternalInput").ap()
    wv_d = p.dram(g("W_v"), [D, D], F32, kind="ExternalInput").ap()
    wo_d = p.dram(g("W_o"), [D, D], F32, kind="ExternalInput").ap()
    wdn_d = p.dram(g("W_down_kv"), [D, 256], F32, kind="ExternalInput").ap()
    wuk_d = p.dram(g("W_up_k"), [256, D], F32, kind="ExternalInput").ap()
    wuv_d = p.dram(g("W_up_v"), [256, D], F32, kind="ExternalInput").ap()
    e2sel = p.dram("e2sel", [128, 2], F32, kind="ExternalInput").ap()
    dec_d = p.dram("dec28", [128, 28 * 512], F32, kind="ExternalInput").ap()

    if env is None:
        env = make_env(p)
    s_x, s_v, s_s, s_t, s_st, s_w = (env[k] for k in
                                     ("s_x", "s_v", "s_s", "s_t", "s_st", "s_w"))
    p.scope_enter()

    xln = p.sbuf("xln", [128, KC * SEQ], BF16)
    wq = p.sbuf("wq", [128, KC * D], BF16)
    wdn = p.sbuf("wdn", [128, KC * 256], BF16)
    wuk = p.sbuf("wuk", [128, 2 * D], BF16)
    wuv = p.sbuf("wuv", [128, 2 * D], BF16)
    latent = p.sbuf("latent", [128, 2 * SEQ], BF16)
    vtm = p.sbuf("vtm", [128, 16 * D], BF16)
    vplus = p.sbuf("vplus", [128, 16 * 65], BF16)
    e2s = p.sbuf("e2s", [128, 2], BF16)
    negb = p.sbuf("negb", [128, 2], F32)
    ptb = p.sbuf("ptb", [128, 2 * 512], BF16)
    rden = p.sbuf("rden", [1, 512], F32)
    bcast = p.sbuf("bcast", [64, 512], BF16)
    attn = p.sbuf("attn", [128, KC * SEQ], BF16)
    R = p.sbuf("Rbuf", [128, 28 * 512], BF16)
    qh = env["xbf"][:, 0:SEQ]
    kh = env["xbf"][:, SEQ:2 * SEQ]
    sqs = env["sqb"][:, 0:SEQ]
    sdec = env["xbuf"][:, 0:1024]

    ps_a, ps_o, ps_c = env["ps_a"], env["ps_o"], env["ps_c"]

    for c in range(KC):
        p.gdma(out=wq[:, c * D:(c + 1) * D], in_=wq_d[c * 128:(c + 1) * 128, :], i=(s_w, 16))
        p.gdma(out=wdn[:, c * 256:(c + 1) * 256], in_=wdn_d[c * 128:(c + 1) * 128, :],
               i=(s_w, 16))
    for lc in range(2):
        p.gdma(out=wuk[:, lc * D:(lc + 1) * D], in_=wuk_d[lc * 128:(lc + 1) * 128, :],
               i=(s_w, 16))
        p.gdma(out=wuv[:, lc * D:(lc + 1) * D], in_=wuv_d[lc * 128:(lc + 1) * 128, :],
               i=(s_w, 16))
    p.gdma(out=e2s[:], in_=e2sel, i=(s_w, 16))
    wload_total = s_w.n
    emit_env_init(p, env)

    exp_bank = {0: 0, 1: 0}
    pv_bank = {0: 0, 1: 0}
    dec_bank = {0: 0, 1: 0}

    for b in range(NB):
        for c in range(KC):
            p.gdma(out=attn[:, c * D:(c + 1) * D],
                   in_=wk_d[c * 128:(c + 1) * 128, :],
                   w=[(s_t, s_t.n), (s_v, s_v.n)], i=(s_w, 16))
            p.gdma(out=attn[:, KC * D + c * D:KC * D + (c + 1) * D],
                   in_=wv_d[c * 128:(c + 1) * 128, :], i=(s_w, 16))
        for q4 in range(4):
            p.gdma(out=R[:, q4 * 7 * 512:(q4 + 1) * 7 * 512],
                   in_=dec_d[:, q4 * 7 * 512:(q4 + 1) * 7 * 512],
                   w=[(s_t, s_t.n)], i=(s_w, 16))
        vw_loaded = s_w.n

        for blk in range(4):
            emit_ln_block(p, env, x,
                          lambda c, blk=blk: xln[:, c * SEQ + blk * 512:
                                                 c * SEQ + blk * 512 + 512],
                          b, blk)

        p.wait("tensor", s_w, vw_loaded)
        for blk in range(4):
            for c in range(KC):
                for (woff, bank) in ((0, 0), (KC * D, 1)):
                    for cc in range(KC):
                        p.mm(ps_o[:, bank * 512: bank * 512 + 512],
                             attn[:, woff + cc * D + c * 128: woff + cc * D + (c + 1) * 128],
                             xln[:, cc * SEQ + blk * 512: cc * SEQ + (blk + 1) * 512],
                             start=(cc == 0), stop=(cc == KC - 1),
                             w=([(s_v, s_v.n)] if cc == 0 else ()),
                             i=((s_t, 1) if cc == KC - 1 else None))
                p.vcopy(out=env["lnscr"][:], in_=ps_o[:, 0:512],
                        w=[(s_t, s_t.n)], i=(s_v, 1))
                p.tt(vtm[:, c * SEQ + blk * 512: c * SEQ + (blk + 1) * 512],
                     env["lnscr"][:], ps_o[:, 512:1024], AluOpType.add,
                     i=(s_v, 1))
        for blk in range(4):
            for lc in range(2):
                bank = lc % 2
                for c in range(KC):
                    p.mm(ps_a[:, bank * 512: bank * 512 + 512],
                         wdn[:, c * 256 + lc * 128: c * 256 + (lc + 1) * 128],
                         vtm[:, c * SEQ + blk * 512: c * SEQ + (blk + 1) * 512],
                         start=(c == 0), stop=(c == KC - 1),
                         w=([(s_v, s_v.n)] if c == 0 else ()),
                         i=((s_t, 1) if c == KC - 1 else None))
                p.vcopy(out=latent[:, lc * SEQ + blk * 512: lc * SEQ + (blk + 1) * 512],
                        in_=ps_a[:, bank * 512: bank * 512 + 512],
                        w=[(s_t, s_t.n)], i=(s_v, 1))
        for tc in range(16):
            for half in range(2):
                bank = half % 2
                for lc in range(2):
                    p.mm(ps_o[:, bank * 512: bank * 512 + 512],
                         latent[:, lc * SEQ + tc * 128: lc * SEQ + (tc + 1) * 128],
                         wuv[:, lc * D + half * 512: lc * D + (half + 1) * 512],
                         start=(lc == 0), stop=(lc == 1),
                         w=([(s_v, s_v.n)] if lc == 0 else ()),
                         i=((s_t, 1) if lc == 1 else None))
                p.vcopy(out=vtm[:, tc * D + half * 512: tc * D + (half + 1) * 512],
                        in_=ps_o[:, bank * 512: bank * 512 + 512],
                        w=[(s_t, s_t.n)], i=(s_v, 1))

        for hp in range(8):
            for qb in range(4):
                bank = qb % 2
                for c in range(KC):
                    p.mm(ps_o[:, bank * 512: bank * 512 + 512],
                         wq[:, c * D + hp * 128: c * D + (hp + 1) * 128],
                         xln[:, c * SEQ + qb * 512: c * SEQ + (qb + 1) * 512],
                         start=(c == 0), stop=(c == KC - 1),
                         w=([(s_v, s_v.n)] if c == 0 else ()),
                         i=((s_t, 1) if c == KC - 1 else None))
                p.vcopy(out=qh[:, qb * 512:(qb + 1) * 512],
                        in_=ps_o[:, bank * 512: bank * 512 + 512],
                        w=[(s_t, s_t.n)], i=(s_v, 1))
            for qb in range(4):
                bank = qb % 2
                for lc in range(2):
                    p.mm(ps_o[:, bank * 512: bank * 512 + 512],
                         wuk[:, lc * D + hp * 128: lc * D + (hp + 1) * 128],
                         latent[:, lc * SEQ + qb * 512: lc * SEQ + (qb + 1) * 512],
                         start=(lc == 0), stop=(lc == 1),
                         w=([(s_v, s_v.n)] if lc == 0 else ()),
                         i=((s_t, 1) if lc == 1 else None))
                p.vcopy(out=kh[:, qb * 512:(qb + 1) * 512],
                        in_=ps_o[:, bank * 512: bank * 512 + 512],
                        w=[(s_t, s_t.n)], i=(s_v, 1))

            for (src, col) in ((qh, 0), (kh, 1)):
                p.act(out=sqs[:], in_=src[:], func=AF.Square,
                      w=[(s_v, s_v.n)], i=(s_s, 1))
                for j in range(2):
                    slot = 8 + 2 * col + j
                    for qb in range(4):
                        p.mm(ps_a[0:1, (qb % 2) * 512:(qb % 2) * 512 + 512],
                             e2s[:, j:j + 1], sqs[:, qb * 512:(qb + 1) * 512],
                             start=True, stop=True,
                             w=[(s_s, s_s.n), (s_v, s_v.n)], i=(s_t, 1))
                        p.vredmax(env["small"][0:1, slot:slot + 1]
                                  if qb == 0 else env["small"][0:1, 12:13],
                                  ps_a[0:1, (qb % 2) * 512:(qb % 2) * 512 + 512],
                                  w=[(s_t, s_t.n)], i=(s_v, 1))
                        if qb > 0:
                            p.tt(env["small"][0:1, slot:slot + 1],
                                 env["small"][0:1, slot:slot + 1],
                                 env["small"][0:1, 12:13], AluOpType.max, i=(s_v, 1))
            for j in range(2):
                pj = env["small"][0:1, 0 + j:1 + j]
                p.tt(pj, env["small"][0:1, 8 + j:9 + j],
                     env["small"][0:1, 10 + j:11 + j], AluOpType.mult,
                     w=[(s_v, s_v.n)], i=(s_v, 1))
                p.act(out=pj, in_=pj, func=AF.Ln, w=[(s_v, s_v.n)], i=(s_s, 1))
                p.act(out=pj, in_=pj, func=AF.Exp, scale=0.5, i=(s_s, 1))
                p.ts(pj, pj, -1.0 / 32, w=[(s_s, s_s.n)], i=(s_v, 1))
                p.mm(ps_c[:, 0:1], env["ones1f"][:], pj, w=[(s_v, s_v.n)], i=(s_t, 1))
                p.vcopy(negb[:, j:j + 1], ps_c[:, 0:1], w=[(s_t, s_t.n)], i=(s_v, 1))

            for j in range(2):
                hg = hp * 2 + j
                p.vmemset(vplus[:], 1.0, w=[(s_t, s_t.n)], i=(s_v, 1))
                for tc in range(16):
                    p.vcopy(vplus[:, tc * 65: tc * 65 + 64],
                            vtm[:, tc * D + hg * 64: tc * D + (hg + 1) * 64],
                            i=(s_v, 1))
                vplus_ready = s_v.n
                for qc in range(4):
                    for kt in range(16):
                        par = kt % 2
                        wv = [(s_v, max(vplus_ready, dec_bank[par]))]
                        p.mm(ps_a[:, par * 512:(par + 1) * 512],
                             kh[j * 64:(j + 1) * 64, kt * 128:(kt + 1) * 128],
                             qh[j * 64:(j + 1) * 64, qc * 512:(qc + 1) * 512],
                             w=wv, i=(s_t, 1))
                        smm_done = s_t.n
                        p.tt(sdec[:, par * 512:(par + 1) * 512],
                             ps_a[:, par * 512:(par + 1) * 512],
                             R[:, (kt - 4 * qc + 12) * 512:(kt - 4 * qc + 13) * 512],
                             AluOpType.mult,
                             w=[(s_t, smm_done)] + ([(s_s, exp_bank[par])] if exp_bank[par] else []),
                             i=(s_v, 1))
                        dec_bank[par] = s_v.n
                        dec_done = s_v.n
                        wv2 = [(s_v, dec_done)]
                        if pv_bank[par]:
                            wv2.append((s_t, pv_bank[par]))
                        p.act(out=ptb[:, par * 512:(par + 1) * 512],
                              in_=sdec[:, par * 512:(par + 1) * 512],
                              func=AF.Exp, scale=1.0 / 32, bias=negb[:, j:j + 1],
                              w=wv2, i=(s_s, 1))
                        exp_bank[par] = s_s.n
                        wv3 = [(s_s, exp_bank[par])]
                        if kt == 0:
                            wv3.append((s_v, s_v.n))
                        p.mm(ps_o[0:65, (qc % 2) * 512:(qc % 2) * 512 + 512],
                             vplus[:, kt * 65:(kt + 1) * 65],
                             ptb[:, par * 512:(par + 1) * 512],
                             start=(kt == 0), stop=(kt == 15),
                             w=wv3, i=(s_t, 1))
                        pv_bank[par] = s_t.n
                    p.vrecip(rden[:], ps_o[64:65, (qc % 2) * 512:(qc % 2) * 512 + 512],
                             w=[(s_t, s_t.n)], i=(s_v, 1))
                    p.mm(ps_c[0:64, 0:512], env["ones1f"][0:1, 0:64], rden[:],
                         w=[(s_v, s_v.n)], i=(s_t, 1))
                    p.scopy(out=bcast[:], in_=ps_c[0:64, 0:512],
                            w=[(s_t, s_t.n)], i=(s_s, 1))
                    p.tt(attn[(hg % 2) * 64:(hg % 2) * 64 + 64,
                              (hg // 2) * SEQ + qc * 512:(hg // 2) * SEQ + (qc + 1) * 512],
                         ps_o[0:64, (qc % 2) * 512:(qc % 2) * 512 + 512],
                         bcast[:], AluOpType.mult,
                         w=[(s_s, s_s.n)], i=(s_v, 1))

        for c in range(KC):
            p.gdma(out=R[:, c * D:(c + 1) * D], in_=wo_d[c * 128:(c + 1) * 128, :],
                   w=[(s_s, s_s.n), (s_v, s_v.n)], i=(s_w, 16))
        wo_loaded = s_w.n
        p.wait("tensor", s_w, wo_loaded)
        for blk in range(4):
            tof = b * SEQ + blk * 512
            p.gdma(out=env["xbuf"][:].rearrange("p (c t) -> p c t", c=KC),
                   in_=x[:, tof:tof + 512].rearrange("(c p) t -> p c t", p=128),
                   w=[(s_v, s_v.n), (s_st, s_st.n)], i=(s_x, 16))
            store_done = {}
            for oc in range(KC):
                bank = oc % 2
                for c in range(KC):
                    p.mm(ps_o[:, bank * 512: bank * 512 + 512],
                         R[:, c * D + oc * 128: c * D + (oc + 1) * 128],
                         attn[:, c * SEQ + blk * 512: c * SEQ + (blk + 1) * 512],
                         start=(c == 0), stop=(c == KC - 1),
                         w=([(s_v, s_v.n)] if c == 0 else ()),
                         i=((s_t, 1) if c == KC - 1 else None))
                wv = [(s_t, s_t.n), (s_x, s_x.n)]
                if oc - 2 in store_done:
                    wv.append((s_st, store_done[oc - 2]))
                oslot = latent[:, bank * 1024:(bank + 1) * 1024].bitcast(F32)
                p.tt(oslot,
                     ps_o[:, bank * 512: bank * 512 + 512],
                     env["xbuf"][:, oc * 512:(oc + 1) * 512], AluOpType.add,
                     w=wv, i=(s_v, 1))
                p.sydma(out=xn[oc * 128:(oc + 1) * 128, tof:tof + 512],
                        in_=oslot,
                        w=[(s_v, s_v.n)], i=(s_st, 16))
                store_done[oc] = s_st.n

    p.wait("sync", s_st, s_st.n)
    p.scope_exit()
    return p



def build_ffn_l1(mode="full", P=None, env=None, xap=None, xnap=None, nm=None):
    p = P if P is not None else Prog()
    nm = nm or {}
    g = lambda n: nm.get(n, n)
    if xap is None:
        x = p.dram("x", [D, T], F32, kind="ExternalInput").ap()
        xn = p.dram("xn", [D, T], F32, kind="ExternalOutput").ap()
    else:
        x, xn = xap, xnap
    wout = p.dram("wout", [8, T], BF16, kind="ExternalOutput").ap() if mode in ("wdump", "route", "routemin") else None
    aout = p.dram("aout", [128, 32 * 16], F32, kind="ExternalOutput").ap() if mode in ("wdump", "route", "routemin") else None
    adbg = p.sbuf("adbg", [128, 32 * 16], F32) if mode in ("wdump", "route", "routemin") else None
    mout = p.dram("mout", [128, 1024 + 64 + 8], BF16, kind="ExternalOutput").ap() if mode in ("wdump", "route", "routemin") else None
    dout = p.dram("dscr", [128, 16], F32) if mode in ("wdump", "route", "routemin") else None
    dout2 = (p.dram("dout", [128, 16], F32, kind="ExternalOutput").ap()
             if mode in ("wdump", "route", "routemin") else None)
    dout3 = (p.dram("dout3", [128, 8], BF16, kind="ExternalOutput").ap()
             if mode in ("wdump", "route", "routemin") else None)
    dout4 = (p.dram("dout4", [8, 512], BF16, kind="ExternalOutput").ap()
             if mode in ("wdump", "route", "routemin") else None)
    mdbg = p.sbuf("mdbg", [128, 1024 + 64 + 8], BF16) if mode in ("wdump", "route", "routemin") else None
    rwT_d = p.dram("routing_WT", [D, 8], F32, kind="ExternalInput").ap()
    eb_d = p.dram("expert_bias128", [128, 8], F32, kind="ExternalInput").ap()
    sw1_d = p.dram(g("sW1"), [D, 512], F32, kind="ExternalInput").ap()
    sb1_d = p.dram("sb1T", [128, 4], F32, kind="ExternalInput").ap()
    sw2_d = p.dram(g("sW2"), [512, D], F32, kind="ExternalInput").ap()
    sb2_d = p.dram("sb2T", [128, 8], F32, kind="ExternalInput").ap()
    rw1_d = p.dram(g("rW1"), [8, D, 512], F32, kind="ExternalInput").ap()
    rb1_d = p.dram("rb1T", [128, 32], F32, kind="ExternalInput").ap()
    rw2_d = p.dram(g("rW2"), [8, 512, D], F32, kind="ExternalInput").ap()
    rb2_d = p.dram("rb2T", [128, 64], F32, kind="ExternalInput").ap()
    id_d = p.dram("ident", [128, 128], F32, kind="ExternalInput").ap()
    wdram = p.dram("wdram", [8, T], BF16)

    if env is None:
        env = make_env(p)
    s_x, s_v, s_s, s_t, s_st, s_w = (env[k] for k in
                                     ("s_x", "s_v", "s_s", "s_t", "s_st", "s_w"))
    s_ac = env["s_ac"]
    s_wr = env["s_wr"]
    p.scope_enter()

    xln = p.sbuf("xln", [128, KC * T], BF16)
    sw1b = p.sbuf("sw1b", [128, KC * 512], BF16)
    sw2b = p.sbuf("sw2b", [128, 4 * D], BF16)
    rw1b = p.sbuf("rw1b", [128, 2 * KC * 512], BF16)
    rw2b = p.sbuf("rw2b", [128, 2 * 4 * D], BF16)
    rwT = p.sbuf("rwT", [128, KC * 8], BF16)
    hb = p.sbuf("hb", [128, 4 * 512], BF16)
    wfm = p.sbuf("wfm", [8, T], BF16)
    identb = p.sbuf("identb", [128, 128], BF16)
    sb1 = p.sbuf("sb1", [128, 4], F32)
    sb2 = p.sbuf("sb2", [128, 8], F32)
    rb1 = p.sbuf("rb1", [128, 32], F32)
    rb2 = p.sbuf("rb2", [128, 64], F32)
    ebias = p.sbuf("ebias", [128, 8], F32)
    onecol = p.sbuf("onecol", [128, 1], F32)
    lg8 = p.sbuf("lg8", [128, 48], F32)
    t_lg = p.sbuf("t_lg", [128, 8], F32)
    t_a = p.sbuf("t_a", [128, 8], F32)
    t_eq = p.sbuf("t_eq", [128, 8], F32)
    t_am = p.sbuf("t_am", [128, 8], F32)
    t_m1 = p.sbuf("t_m1", [128, 1], F32)
    t_m2 = p.sbuf("t_m2", [128, 1], F32)
    t_sc = p.sbuf("t_sc", [128, 8], F32)
    t_sc2 = p.sbuf("t_sc2", [128, 8], F32)
    wtm8 = p.sbuf("wtm8", [128, 8], BF16)
    wrow = p.sbuf("wrow", [1, 2 * 512], BF16)
    wtile = env["xbuf"][:, 0:512]
    ysc0 = env["xbuf"][:, 512:1024]
    ysc1 = env["xbuf"][:, 1024:1536]
    ysc2 = env["xbuf"][:, 1536:2048]

    ps_a, ps_o, ps_c, ps_bc = env["ps_a"], env["ps_o"], env["ps_c"], env["ps_bc"]
    ps_t = p.psum("ps_t", [8, 128], BF16)

    for c in range(KC):
        p.gdma(out=sw1b[:, c * 512:(c + 1) * 512], in_=sw1_d[c * 128:(c + 1) * 128, :],
               i=(s_w, 16))
        p.gdma(out=rwT[:, c * 8:(c + 1) * 8], in_=rwT_d[c * 128:(c + 1) * 128, :],
               i=(s_w, 16))
    for fc in range(4):
        p.gdma(out=sw2b[:, fc * D:(fc + 1) * D], in_=sw2_d[fc * 128:(fc + 1) * 128, :],
               i=(s_w, 16))
    for (dst, srcd) in ((sb1, sb1_d), (sb2, sb2_d), (rb1, rb1_d), (rb2, rb2_d)):
        p.gdma(out=dst[:], in_=srcd, i=(s_w, 16))
    p.gdma(out=identb[:], in_=id_d, i=(s_w, 16))
    p.gdma(out=xn, in_=x, i=(s_ac, 16))
    xn_init = s_ac.n
    p.gdma(out=ebias[:], in_=eb_d, i=(s_w, 16))
    wload_total = s_w.n
    emit_env_init(p, env)
    p.vmemset(onecol[:], 1.0, i=(s_v, 1))
    p.wait("vector", s_w, wload_total)

    NGB = 1 if mode == "routemin" else 8
    tr_done = 0
    for gblk in range(NGB):
        b, blk = divmod(gblk, 4)

        emit_ln_block(p, env, x,
                      lambda c, gblk=gblk: xln[:, c * T + gblk * 512:
                                               c * T + gblk * 512 + 512],
                      b, blk)
        lgbank = [ps_c[:, 0:8], ps_a[:, 0:8], ps_a[:, 512:520], ps_bc[:, 0:8]]
        for s4 in range(4):
            for c in range(KC):
                p.mm(lgbank[s4],
                     xln[:, c * T + gblk * 512 + s4 * 128: c * T + gblk * 512 + (s4 + 1) * 128],
                     rwT[:, c * 8:(c + 1) * 8],
                     start=(c == 0), stop=(c == KC - 1),
                     w=([(s_v, s_v.n), (s_s, s_s.n)] if c == 0 else ()),
                     i=((s_t, 1) if c == KC - 1 else None))
        if mdbg is not None and gblk == 0:
            for c in range(KC):
                p.vcopy(mdbg[:, c * 128:(c + 1) * 128],
                        xln[:, c * T: c * T + 128], w=[(s_t, s_t.n)], i=(s_v, 1))
            p.vcopy(mdbg[:, 1024:1088], rwT[:, 0:64], i=(s_v, 1))
            p.vcopy(mdbg[:, 1088:1096], ps_c[:, 0:8], i=(s_v, 1))
            p.vcopy(mdbg[:, 1060:1068], ebias[:], i=(s_v, 1))
        for s4 in range(4 if mode != "routemin" else 1):
            lg = t_lg[:]
            a = t_a[:]
            eq = t_eq[:]
            am = t_am[:]
            m1 = t_m1[:]
            m2 = t_m2[:]
            sc1 = t_sc[:]
            def VS(fn):
                fn()
                p.wait("vector", s_v, s_v.n)
                p.wait("scalar", s_s, s_s.n)
                p.wait("vector", s_s, s_s.n)
                p.wait("scalar", s_v, s_v.n)
            VS(lambda: p.tt(lg, lgbank[s4], ebias[:], AluOpType.add,
                            w=[(s_t, s_t.n), (s_s, s_s.n)], i=(s_v, 1)))
            VS(lambda: p.act(out=sc1, in_=lg, func=AF.Abs, i=(s_s, 1)))
            VS(lambda: p.act(out=t_sc2[:], in_=sc1, func=AF.Exp, scale=-1.0, i=(s_s, 1)))
            VS(lambda: p.act(out=sc1, in_=t_sc2[:], func=AF.Ln, bias=onecol[:], i=(s_s, 1)))
            VS(lambda: p.ts(a, lg, 0.0, op0=AluOpType.max, i=(s_v, 1)))
            VS(lambda: p.tt(am, a, sc1, AluOpType.add, i=(s_v, 1)))
            VS(lambda: p.act(out=t_sc2[:], in_=am, func=AF.Ln, i=(s_s, 1)))
            VS(lambda: p.act(out=a, in_=t_sc2[:], func=AF.Exp, scale=0.5, i=(s_s, 1)))
            VS(lambda: p.vredmax(m1, a, i=(s_v, 1)))
            VS(lambda: p.ts(eq, a, m1, op0=AluOpType.is_equal, i=(s_v, 1)))
            VS(lambda: p.ts(am, eq, 1e30, op0=AluOpType.mult, i=(s_v, 1)))
            VS(lambda: p.tt(am, a, am, AluOpType.subtract, i=(s_v, 1)))
            VS(lambda: p.vredmax(m2, am, i=(s_v, 1)))
            VS(lambda: p.ts(eq, a, m2, op0=AluOpType.is_ge, i=(s_v, 1)))
            VS(lambda: p.tt(wtm8[:], a, eq, AluOpType.mult, i=(s_v, 1)))
            if adbg is not None:
                p.vcopy(adbg[:, (gblk * 4 + s4) * 16 + 8:(gblk * 4 + s4) * 16 + 16], a,
                        i=(s_v, 1))
                if gblk == 1 and s4 == 0:
                    p.vcopy(adbg[:, 100 * 4:100 * 4 + 8], eq, i=(s_v, 1))
                    p.vcopy(adbg[:, 103 * 4:103 * 4 + 8], am, i=(s_v, 1))
                    p.vcopy(adbg[:, 106 * 4:106 * 4 + 1], m1, i=(s_v, 1))
                    p.vcopy(adbg[:, 107 * 4:107 * 4 + 1], m2, i=(s_v, 1))
            if mode == "routemin":
                continue
            p.transpose(ps_t[:], wtm8[:], identb[:],
                        w=[(s_v, s_v.n), (s_v, tr_done)], i=(s_t, 1))
            p.vcopy(wfm[0:8, gblk * 512 + s4 * 128: gblk * 512 + (s4 + 1) * 128],
                    ps_t[:], w=[(s_t, s_t.n)], i=(s_v, 1))
            tr_done = s_v.n
    if mode != "routemin":
        p.gdma(out=wdram.ap(), in_=wfm[:], w=[(s_v, s_v.n)], i=(s_w, 16))
    wdram_ready = s_w.n

    ysc_dma = {0: 0, 1: 0}

    def ffn_pass(widx1, widx2, b1ap, b2ap, wexp):
        for tb in range(8):
            if wexp is not None:
                par = tb % 2
                p.gdma(out=wrow[0:1, par * 512:(par + 1) * 512],
                       in_=wdram.ap()[wexp:wexp + 1, tb * 512:(tb + 1) * 512],
                       w=[(s_w, wdram_ready), (s_t, s_t.n)], i=(s_wr, 16))
                p.wait("tensor", s_wr, s_wr.n)
                p.mm(ps_bc[:, 0:512], env["ones1b"][:],
                     wrow[0:1, par * 512:(par + 1) * 512],
                     w=[(s_s, s_s.n)], i=(s_t, 1))
                p.scopy(out=wtile, in_=ps_bc[:, 0:512], w=[(s_t, s_t.n)], i=(s_s, 1))
            for fs in range(4):
                bank = fs % 2
                for c in range(KC):
                    p.mm(ps_a[:, bank * 512: bank * 512 + 512],
                         widx1[:, c * 512 + fs * 128: c * 512 + (fs + 1) * 128],
                         xln[:, c * T + tb * 512: c * T + (tb + 1) * 512],
                         start=(c == 0), stop=(c == KC - 1),
                         w=([(s_s, s_s.n), (s_v, s_v.n)] if c == 0 else ()),
                         i=((s_t, 1) if c == KC - 1 else None))
                p.act(out=hb[:, fs * 512:(fs + 1) * 512],
                      in_=ps_a[:, bank * 512: bank * 512 + 512],
                      func=AF.Silu, bias=b1ap[:, fs:fs + 1],
                      w=[(s_t, s_t.n)], i=(s_s, 1))
            for oc in range(KC):
                bank = oc % 2
                for fc in range(4):
                    p.mm(ps_o[:, bank * 512: bank * 512 + 512],
                         widx2[:, fc * D + oc * 128: fc * D + (oc + 1) * 128],
                         hb[:, fc * 512:(fc + 1) * 512],
                         start=(fc == 0), stop=(fc == 3),
                         w=([(s_s, s_s.n), (s_v, s_v.n)] if fc == 0 else ()),
                         i=((s_t, 1) if fc == 3 else None))
                ysc = (ysc0 if (oc % 2) == 0 else ysc1)
                p.ts(ysc, ps_o[:, bank * 512: bank * 512 + 512], b2ap[:, oc:oc + 1],
                     op0=AluOpType.add,
                     w=[(s_t, s_t.n), (s_ac, max(xn_init, ysc_dma[bank]))], i=(s_v, 1))
                if wexp is not None:
                    p.tt(ysc, ysc, wtile, AluOpType.mult, i=(s_v, 1))
                wv = [(s_v, s_v.n)]
                tof = tb * 512
                p.gdma(out=xn[oc * 128:(oc + 1) * 128, tof:tof + 512],
                       in_=ysc, accum_op=AluOpType.add, w=wv, i=(s_ac, 16))
                ysc_dma[bank] = s_ac.n

    p.wait("tensor", s_w, wload_total)
    if mode not in ("route", "routemin"):
        ffn_pass(sw1b, sw2b, sb1, sb2, None)

    if mode in ("wdump", "route", "routemin"):
        if mode != "routemin":
            p.gdma(out=wout, in_=wdram.ap(), w=[(s_w, wdram_ready)], i=(s_ac, 16))
        else:
            p.gdma(out=wout, in_=wfm[:], w=[(s_v, s_v.n)], i=(s_ac, 16))
        p.gdma(out=aout, in_=adbg[:], w=[(s_v, s_v.n)], i=(s_ac, 16))
        p.gdma(out=mout, in_=mdbg[:], i=(s_ac, 16))
        p.gdma(out=dout.ap()[0:128, 0:8], in_=t_a[:], i=(s_ac, 16))
        p.gdma(out=dout.ap()[0:128, 8:16], in_=t_eq[:], i=(s_ac, 16))
        p.gdma(out=dout2, in_=dout.ap(), w=[(s_ac, s_ac.n)], i=(s_ac, 16))
        p.gdma(out=dout3, in_=wtm8[:], i=(s_ac, 16))
        p.gdma(out=dout4, in_=wfm[0:8, 0:512], i=(s_ac, 16))
    wpar_done = {0: 0, 1: 0}
    for e in range(8) if mode in ("full", "wdump") else []:
        par = e % 2
        wv = [(s_t, wpar_done[par])] if wpar_done[par] else []
        for c in range(KC):
            p.gdma(out=rw1b[:, par * KC * 512 + c * 512: par * KC * 512 + (c + 1) * 512],
                   in_=rw1_d[e, c * 128:(c + 1) * 128, :],
                   w=(wv if c == 0 else ()), i=(s_w, 16))
        for fc in range(4):
            p.gdma(out=rw2b[:, par * 4 * D + fc * D: par * 4 * D + (fc + 1) * D],
                   in_=rw2_d[e, fc * 128:(fc + 1) * 128, :], i=(s_w, 16))
        p.wait("tensor", s_w, s_w.n)
        ffn_pass(rw1b[:, par * KC * 512:(par + 1) * KC * 512],
                 rw2b[:, par * 4 * D:(par + 1) * 4 * D],
                 rb1[:, e * 4:(e + 1) * 4], rb2[:, e * 8:(e + 1) * 8], e)
        wpar_done[par] = s_t.n

    p.wait("gpsimd", s_ac, s_ac.n)
    p.scope_exit()
    return p


def run_ffn_l1(x_shards, routing_W, expert_bias, sW1, sb1, sW2, sb2, rW1, rb1, rW2, rb2,
               mode="full"):
    nc = _get_prog("ffn_l1_" + mode, lambda: build_ffn_l1(mode))
    rwT = np.ascontiguousarray(np.asarray(routing_W, np.float32).T)
    sb1T = np.ascontiguousarray(np.asarray(sb1, np.float32).reshape(4, 128).T)
    sb2T = np.ascontiguousarray(np.asarray(sb2, np.float32).reshape(8, 128).T)
    rb1T = np.ascontiguousarray(np.asarray(rb1, np.float32).reshape(8, 4, 128)
                                .transpose(2, 0, 1).reshape(128, 32))
    rb2T = np.ascontiguousarray(np.asarray(rb2, np.float32).reshape(8, 8, 128)
                                .transpose(2, 0, 1).reshape(128, 64))
    ident = np.eye(128, dtype=np.float32)
    in_maps = []
    for c in range(NCORES):
        m = {"x": x_shards[c], "routing_WT": rwT,
             "expert_bias128": np.ascontiguousarray(
                 np.broadcast_to(np.asarray(expert_bias, np.float32).reshape(1, 8), (128, 8))),
             "sW1": np.asarray(sW1, np.float32), "sb1T": sb1T,
             "sW2": np.asarray(sW2, np.float32), "sb2T": sb2T,
             "rW1": np.asarray(rW1, np.float32), "rb1T": rb1T,
             "rW2": np.asarray(rW2, np.float32), "rb2T": rb2T,
             "ident": ident}
        in_maps.append(m)
    res = run_bass_kernel_spmd(nc, in_maps, core_ids=list(range(NCORES)))
    if mode in ("wdump", "route", "routemin"):
        return ([np.asarray(r["xn"]) for r in res.results],
                [(np.asarray(r["wout"]), np.asarray(r["aout"]), np.asarray(r["mout"]),
                  np.asarray(r["dout"]), np.asarray(r["dout3"]),
                  np.asarray(r["dout4"])) for r in res.results])
    return [np.asarray(r["xn"]) for r in res.results]


def _host_consts_attn_a():
    lam = 0.01
    dec = np.zeros((128, 28 * 512), np.float32)
    p_i = np.arange(128)[:, None]
    f_i = np.arange(512)[None, :]
    for im, m in enumerate(range(-12, 16)):
        dec[:, im * 512:(im + 1) * 512] = np.exp(-lam * np.abs(m * 128 + p_i - f_i))
    return dec


def run_attn_a_l1(x_shards, Wq, Wk, Wv, Wo, Wdn, Wuk, Wuv):
    consts = _host_consts_attn()
    dec28 = _host_consts_attn_a()
    nc = _get_prog("attn_a_l1", lambda: build_attn_a_l1())
    in_maps = []
    for c in range(NCORES):
        m = {"x": x_shards[c], "W_q": Wq, "W_k": Wk, "W_v": Wv, "W_o": Wo,
             "W_down_kv": Wdn, "W_up_k": Wuk, "W_up_v": Wuv,
             "e2sel": consts["e2sel"], "dec28": dec28}
        in_maps.append(m)
    res = run_bass_kernel_spmd(nc, in_maps, core_ids=list(range(NCORES)))
    return [np.asarray(r["xn"]) for r in res.results]


def _host_consts_attn():
    inv = 1.0 / (10000.0 ** (np.arange(0, 64, 2, dtype=np.float64) / 64))
    t = np.arange(SEQ, dtype=np.float64)
    fr = np.outer(t, inv)
    cos = np.cos(fr).astype(np.float32)
    sin = np.sin(fr).astype(np.float32)
    cos64 = np.concatenate([cos.T, cos.T], 0)
    sin64 = np.concatenate([sin.T, sin.T], 0)
    cos128 = np.concatenate([cos64, cos64], 0)
    sin128 = np.concatenate([sin64, sin64], 0)
    e2sel = np.zeros((128, 2), np.float32)
    e2sel[0:64, 0] = 1
    e2sel[64:128, 1] = 1
    cmask = np.zeros((128, 4 * 512), np.float32)
    for r in range(4):
        ki = np.arange(128)[:, None] + r * 128
        qi = np.arange(512)[None, :]
        cmask[:, r * 512:(r + 1) * 512] = (ki <= qi).astype(np.float32)
    return dict(cos_t=np.ascontiguousarray(cos128), sin_t=np.ascontiguousarray(sin128),
                e2sel=e2sel, cmask=cmask)



_PROG_CACHE = {}


def _get_prog(name, builder):
    if name not in _PROG_CACHE:
        p = builder()
        _PROG_CACHE[name] = p.build()
    return _PROG_CACHE[name]


def _shard_x(full):
    out = []
    for c in range(NCORES):
        xc = np.asarray(full[2 * c:2 * c + 2], np.float32)
        out.append(np.ascontiguousarray(xc.transpose(2, 0, 1).reshape(D, T)))
    return out


def _unshard_x(shards):
    full = np.empty((16, SEQ, D), np.float32)
    for c in range(NCORES):
        full[2 * c:2 * c + 2] = np.asarray(shards[c]).reshape(D, NB, SEQ).transpose(1, 2, 0)
    return full


def run_attn_b_l0(x_shards, w_qkv, w_o):
    consts = _host_consts_attn()
    nc = _get_prog("attn_b_l0", lambda: build_attn_b("l0"))
    in_maps = []
    for c in range(NCORES):
        m = {"x": x_shards[c], "w_qkv": w_qkv, "w_o": w_o,
             "cos_t": consts["cos_t"], "sin_t": consts["sin_t"],
             "e2sel": consts["e2sel"]}
        in_maps.append(m)
    res = run_bass_kernel_spmd(nc, in_maps, core_ids=list(range(NCORES)))
    return [np.asarray(r["xn"]) for r in res.results]


def run_attn_b_l1(x_shards, Wq, Wk, Wv, Wo, qnw, knw):
    consts = _host_consts_attn()
    nc = _get_prog("attn_b_l1", lambda: build_attn_b("l1"))
    in_maps = []
    for c in range(NCORES):
        m = {"x": x_shards[c], "Wq": Wq, "Wk": Wk, "Wv": Wv, "Wo": Wo,
             "q_norm_w": np.asarray(qnw, np.float32).reshape(1, 64),
             "k_norm_w": np.asarray(knw, np.float32).reshape(1, 64)}
        m.update(consts)
        in_maps.append(m)
    res = run_bass_kernel_spmd(nc, in_maps, core_ids=list(range(NCORES)))
    return [np.asarray(r["xn"]) for r in res.results]


def build_mega():
    p = Prog()
    x0 = p.dram("x0", [D, T], F32, kind="ExternalInput").ap()
    xn = p.dram("xn", [D, T], F32, kind="ExternalOutput").ap()
    mids = [p.dram("xmid", [D, T], F32).ap() for _ in range(7)]
    env = make_env(p)
    chain = [x0] + mids + [xn]
    specs = [
        (build_attn_a_l0, dict(nm={"q_w": "L0_attn_a_q_w", "k_w": "L0_attn_a_k_w",
                                   "v_w": "L0_attn_a_v_w", "out_w": "L0_attn_a_out_w"})),
        (build_ffn_l0, dict(nm={"Wg": "L0_ffn_Wg", "W1": "L0_ffn_W1",
                                "W2": "L0_ffn_W2"})),
        (lambda **kw: build_attn_b("l0", **kw),
         dict(nm={"w_qkv": "L0_attn_b_w_qkv", "w_o": "L0_attn_b_w_o"})),
        (build_ffn_l0, dict(nm={"Wg": "L0_ffn_Wg", "W1": "L0_ffn_W1",
                                "W2": "L0_ffn_W2"})),
        (build_attn_a_l1, dict(nm={"W_q": "L1_attn_a_W_q", "W_k": "L1_attn_a_W_k",
                                   "W_v": "L1_attn_a_W_v", "W_o": "L1_attn_a_W_o",
                                   "W_down_kv": "L1_attn_a_W_down_kv",
                                   "W_up_k": "L1_attn_a_W_up_k",
                                   "W_up_v": "L1_attn_a_W_up_v"})),
        (build_ffn_l1, dict(nm={"sW1": "L1_ffn_sW1", "sW2": "L1_ffn_sW2",
                                "rW1": "L1_ffn_rW1", "rW2": "L1_ffn_rW2"})),
        (lambda **kw: build_attn_b("l1", **kw),
         dict(nm={"Wq": "L1_attn_b_Wq", "Wk": "L1_attn_b_Wk", "Wv": "L1_attn_b_Wv",
                  "Wo": "L1_attn_b_Wo", "q_norm_w": "L1_attn_b_q_norm_w",
                  "k_norm_w": "L1_attn_b_k_norm_w"})),
        (build_ffn_l1, dict(nm={"sW1": "L1_ffn_sW1", "sW2": "L1_ffn_sW2",
                                "rW1": "L1_ffn_rW1", "rW2": "L1_ffn_rW2"})),
    ]
    for i, (fn, kw) in enumerate(specs):
        fn(P=p, env=env, xap=chain[i], xnap=chain[i + 1], **kw)
        emit_barrier(p, env)
    return p


def _host_all_arrays(inputs):
    g = lambda k: np.asarray(inputs[k], np.float32)
    m = {}
    consts = _host_consts_attn()
    m["cos_t"] = consts["cos_t"]
    m["sin_t"] = consts["sin_t"]
    m["e2sel"] = consts["e2sel"]
    m["cmask"] = consts["cmask"]
    m["dec28"] = _host_consts_attn_a()
    for k in ("L0_attn_a_q_w", "L0_attn_a_k_w", "L0_attn_a_v_w", "L0_attn_a_out_w",
              "L0_ffn_Wg", "L0_ffn_W1", "L0_ffn_W2",
              "L0_attn_b_w_qkv", "L0_attn_b_w_o",
              "L1_attn_a_W_q", "L1_attn_a_W_k", "L1_attn_a_W_v", "L1_attn_a_W_o",
              "L1_attn_a_W_down_kv", "L1_attn_a_W_up_k", "L1_attn_a_W_up_v",
              "L1_ffn_sW1", "L1_ffn_sW2", "L1_ffn_rW1", "L1_ffn_rW2",
              "L1_attn_b_Wq", "L1_attn_b_Wk", "L1_attn_b_Wv", "L1_attn_b_Wo"):
        m[k] = g(k)
    m["L1_attn_b_q_norm_w"] = g("L1_attn_b_q_norm_w").reshape(1, 64)
    m["L1_attn_b_k_norm_w"] = g("L1_attn_b_k_norm_w").reshape(1, 64)
    qkbT = np.zeros((128, 16), np.float32)
    qkbT[:, 0:8] = g("L0_attn_a_q_b").reshape(8, 128).T
    qkbT[:, 8:16] = g("L0_attn_a_k_b").reshape(8, 128).T
    m["qkbT"] = qkbT
    gboT = np.zeros((128, 24), np.float32)
    gboT[:, 0:8] = g("L0_attn_a_gamma").reshape(8, 128).T
    gboT[:, 8:16] = g("L0_attn_a_beta").reshape(8, 128).T
    gboT[:, 16:24] = g("L0_attn_a_out_b").reshape(8, 128).T
    m["gboT"] = gboT
    m["aa0_vb"] = g("L0_attn_a_v_b").reshape(1, D)
    pja = np.ones((65, 256), np.float32)
    pja[0:64, :] = g("L0_attn_a_proj").T
    m["projT_aug"] = np.ascontiguousarray(pja)
    m["utmask"] = (np.arange(128)[:, None] <= np.arange(128)[None, :]).astype(np.float32)
    m["b1T"] = np.ascontiguousarray(g("L0_ffn_b1").reshape(8, 32, 128)
                                    .transpose(2, 0, 1).reshape(128, 256))
    m["b2T"] = np.ascontiguousarray(g("L0_ffn_b2").reshape(8, 8, 128)
                                    .transpose(2, 0, 1).reshape(128, 64))
    E8sel = np.zeros((128, 8), np.float32)
    for pp in range(64):
        E8sel[pp, pp % 8] = 1.0
    m["E8sel"] = E8sel
    m["E8T"] = np.ascontiguousarray(E8sel.T)
    m["routing_WT"] = np.ascontiguousarray(g("L1_ffn_routing_W").T)
    m["expert_bias128"] = np.ascontiguousarray(
        np.broadcast_to(g("L1_ffn_expert_bias").reshape(1, 8), (128, 8)))
    m["sb1T"] = np.ascontiguousarray(g("L1_ffn_sb1").reshape(4, 128).T)
    m["sb2T"] = np.ascontiguousarray(g("L1_ffn_sb2").reshape(8, 128).T)
    m["rb1T"] = np.ascontiguousarray(g("L1_ffn_rb1").reshape(8, 4, 128)
                                     .transpose(2, 0, 1).reshape(128, 32))
    m["rb2T"] = np.ascontiguousarray(g("L1_ffn_rb2").reshape(8, 8, 128)
                                     .transpose(2, 0, 1).reshape(128, 64))
    m["ident"] = np.eye(128, dtype=np.float32)
    return m


def kernel(**inputs):
    x = np.asarray(inputs["L0_attn_a_x"], np.float32)
    shards = _shard_x(x)
    arrs = _host_all_arrays(inputs)
    nc = _get_prog("mega", lambda: build_mega())
    in_maps = []
    for c in range(NCORES):
        mm_ = dict(arrs)
        mm_["x0"] = shards[c]
        in_maps.append(mm_)
    res = run_bass_kernel_spmd(nc, in_maps, core_ids=list(range(NCORES)))
    return _unshard_x([np.asarray(r["xn"]) for r in res.results])


# version 04
# speedup vs baseline: 1.4392x; 1.4392x over previous
import math
import numpy as np
from contextlib import ExitStack

import concourse.bass as bass
import concourse.mybir as mybir
from concourse.alu_op_type import AluOpType
from concourse.bass_utils import run_bass_kernel_spmd

F32 = mybir.dt.float32
BF16 = mybir.dt.bfloat16
AF = mybir.ActivationFunctionType
AX = mybir.AxisListType

NCORES = 8
D = 1024
T = 4096
SEQ = 2048
NB = 2
KC = D // 128
LN_EPS = 1e-5



class Sem:
    def __init__(self, handle):
        self.h = handle
        self.n = 0

    def inc(self, k=1):
        self.n += k
        return self.n


class Prog:

    ENGS = ("tensor", "vector", "scalar", "gpsimd", "sync")

    def __init__(self):
        self.nc = bass.Bass("TRN2", target_bir_lowering=False, debug=False)
        self.es = ExitStack()
        self.stacks = [self.es]
        self.steps = []
        self._dram_cache = {}
        self._dram_n = 0

    def dram(self, name, shape, dtype=F32, kind=None):
        if kind is None:
            self._dram_n += 1
            return self.nc.dram_tensor(f"{name}__{self._dram_n}", shape, dtype)
        if name in self._dram_cache:
            return self._dram_cache[name]
        t = self.nc.dram_tensor(name, shape, dtype, kind=kind)
        self._dram_cache[name] = t
        return t

    def scope_enter(self):
        self.stacks.append(ExitStack())

    def scope_exit(self):
        self.stacks.pop().close()

    def sbuf(self, name, shape, dtype):
        return self.stacks[-1].enter_context(
            self.nc.sbuf_tensor(name, shape, dtype, allow_name_mangling=True))

    def psum(self, name, shape, dtype=F32):
        return self.stacks[-1].enter_context(
            self.nc.psum_tensor(name, shape, dtype, allow_name_mangling=True))

    def sem(self, name):
        return Sem(self.es.enter_context(self.nc.semaphore(name)))

    def _step(self, eng, fn, w, i):
        iv = None
        if i is not None:
            s, k = i
            s.inc(k)
            iv = (s.h, k)
        wv = [(s.h, v) for (s, v) in w if v > 0]

        def run(E, fn=fn, wv=wv, iv=iv):
            for h, v in wv:
                E.wait_ge(h, v)
            inst = fn(E)
            if iv is not None:
                if inst is None:
                    inst = E.nop()
                inst.then_inc(iv[0], iv[1])
        self.steps.append((eng, run))

    def mm(self, out, lhsT, rhs, start=True, stop=True, w=(), i=None):
        self._step("tensor",
                   lambda E: E.matmul(out, lhsT, rhs, start=start, stop=stop), w, i)

    def transpose(self, out, in_, ident, w=(), i=None):
        self._step("tensor", lambda E: E.transpose(out, in_, ident), w, i)

    def tt(self, out, in0, in1, op, w=(), i=None):
        self._step("vector", lambda E: E.tensor_tensor(out=out, in0=in0, in1=in1, op=op), w, i)

    def ts(self, out, in0, s1, s2=None, op0=AluOpType.mult, op1=None, accum=None, w=(), i=None):
        kw = {}
        if op1 is not None:
            kw["op1"] = op1
        if accum is not None:
            kw["accum_out"] = accum
        self._step("vector",
                   lambda E: E.tensor_scalar(out=out, in0=in0, scalar1=s1, scalar2=s2,
                                             op0=op0, **kw), w, i)

    def vcopy(self, out, in_, w=(), i=None):
        self._step("vector", lambda E: E.tensor_copy(out=out, in_=in_), w, i)

    def stt(self, out, in0, scalar, in1, op0, op1, w=(), i=None):
        self._step("vector",
                   lambda E: E.scalar_tensor_tensor(out=out, in0=in0, scalar=scalar,
                                                    in1=in1, op0=op0, op1=op1), w, i)

    def vmemset(self, ap, val, w=(), i=None):
        self._step("vector", lambda E: E.memset(ap, val), w, i)

    def vrecip(self, out, in_, w=(), i=None):
        self._step("vector", lambda E: E.reciprocal(out=out, in_=in_), w, i)

    def vredmax(self, out, in_, w=(), i=None):
        self._step("vector", lambda E: E.reduce_max(out=out, in_=in_, axis=AX.X), w, i)

    def vselect(self, out, mask, on_true, on_false, w=(), i=None):
        self._step("vector",
                   lambda E: E.select(out=out, mask=mask, on_true=on_true, on_false=on_false),
                   w, i)

    def act(self, out, in_, func, bias=None, scale=None, accum=None, w=(), i=None):
        kw = {}
        if bias is not None:
            kw["bias"] = bias
        if scale is not None:
            kw["scale"] = scale
        if accum is not None:
            kw["accum_out"] = accum
        self._step("scalar", lambda E: E.activation(out=out, in_=in_, func=func, **kw), w, i)

    def scopy(self, out, in_, w=(), i=None):
        self._step("scalar", lambda E: E.copy(out=out, in_=in_), w, i)

    def gdma(self, out, in_, accum_op=None, w=(), i=None):
        if accum_op is None:
            self._step("gpsimd", lambda E: E.dma_start(out=out, in_=in_), w, i)
        else:
            self._step("gpsimd",
                       lambda E: E.dma_start(out=out, in_=in_, accum_op=accum_op), w, i)

    def sydma(self, out, in_, w=(), i=None):
        self._step("sync", lambda E: E.dma_start(out=out, in_=in_), w, i)

    def collective(self, kind, op, groups, ins, outs, w=(), i=None):
        self._step("gpsimd",
                   lambda E: E.collective_compute(kind, op, replica_groups=groups,
                                                  ins=ins, outs=outs), w, i)

    def wait(self, eng, sem, val):
        self._step(eng, lambda E: None, [(sem, val)], None)

    def build(self):
        with self.nc.Block() as block:
            for eng in self.ENGS:
                deco = getattr(block, eng)

                def body(E, eng=eng):
                    for e2, fn in self.steps:
                        if e2 == eng:
                            fn(E)
                deco(body)
        self.es.close()
        return self.nc



def make_env(p):
    env = {}
    env["xbuf"] = p.sbuf("xbuf", [128, KC * 512], F32)
    env["xbf"] = p.sbuf("xbf", [128, KC * 512], BF16)
    env["sqb"] = p.sbuf("sqb", [128, KC * 512], BF16)
    env["rows"] = p.sbuf("rows", [1, 1024], F32)
    env["small"] = p.sbuf("small", [1, 32], F32)
    env["eps1"] = p.sbuf("eps1", [1, 1], F32)
    env["eps6"] = p.sbuf("eps6", [1, 1], F32)
    env["ones128"] = p.sbuf("ones128", [128, 1], BF16)
    env["ones1f"] = p.sbuf("ones1f", [1, 128], F32)
    env["ones1b"] = p.sbuf("ones1b", [1, 128], BF16)
    env["lnscr"] = p.sbuf("lnscr", [128, 512], F32)
    env["ps_a"] = p.psum("ps_a", [128, 1024])
    env["ps_o"] = p.psum("ps_o", [128, 1024])
    env["ps_c"] = p.psum("ps_c", [128, 512])
    env["ps_bc"] = p.psum("ps_bc", [128, 1024])
    for s in ("s_x", "s_v", "s_s", "s_t", "s_st", "s_w", "s_g", "s_ac", "s_wr",
              "s_cc"):
        env[s] = p.sem(s)
    return env


def emit_barrier(p, env):
    sems = [env[k] for k in ("s_x", "s_v", "s_s", "s_t", "s_st", "s_w", "s_ac",
                             "s_wr", "s_cc")]
    for eng in Prog.ENGS:
        for s in sems:
            if s.n:
                p.wait(eng, s, s.n)


def emit_env_init(p, env):
    s_v = env["s_v"]
    p.vmemset(env["ones128"][:], 1.0, i=(s_v, 1))
    p.vmemset(env["ones1f"][:], 1.0, i=(s_v, 1))
    p.vmemset(env["ones1b"][:], 1.0, i=(s_v, 1))
    p.vmemset(env["eps1"][:], LN_EPS, i=(s_v, 1))
    p.vmemset(env["eps6"][:], 1e-6, i=(s_v, 1))


def emit_ln_block(p, env, x_ap, xln_slice_fn, b, blk):
    xbuf, xbf, sqb = env["xbuf"], env["xbf"], env["sqb"]
    rows = env["rows"]
    ps_a, ps_bc = env["ps_a"], env["ps_bc"]
    s_x, s_v, s_s, s_t = env["s_x"], env["s_v"], env["s_s"], env["s_t"]

    tof = b * SEQ + blk * 512
    p.gdma(out=xbuf[:].rearrange("p (c t) -> p c t", c=KC),
           in_=x_ap[:, tof:tof + 512].rearrange("(c p) t -> p c t", p=128),
           w=[(s_v, s_v.n), (s_t, s_t.n)], i=(s_x, 16))
    p.vcopy(out=xbf[:], in_=xbuf[:], w=[(s_x, s_x.n), (s_s, s_s.n)], i=(s_v, 1))
    p.act(out=sqb[:], in_=xbf[:], func=AF.Square, w=[(s_v, s_v.n)], i=(s_s, 1))
    for c in range(KC):
        p.mm(ps_a[0:1, 0:512], env["ones128"][:], xbf[:, c * 512:(c + 1) * 512],
             start=(c == 0), stop=(c == KC - 1),
             w=([(s_v, s_v.n)] if c == 0 else ()),
             i=((s_t, 1) if c == KC - 1 else None))
    for c in range(KC):
        p.mm(ps_a[0:1, 512:1024], env["ones128"][:], sqb[:, c * 512:(c + 1) * 512],
             start=(c == 0), stop=(c == KC - 1),
             w=([(s_s, s_s.n)] if c == 0 else ()),
             i=((s_t, 1) if c == KC - 1 else None))
    mu = rows[0:1, 0:512]
    var = rows[0:1, 512:1024]
    p.ts(mu, ps_a[0:1, 0:512], 1.0 / D, w=[(s_t, s_t.n)], i=(s_v, 1))
    p.ts(var, ps_a[0:1, 512:1024], 1.0 / D, i=(s_v, 1))
    p.tt(env["lnscr"][0:1, 0:512], mu, mu, AluOpType.mult, i=(s_v, 1))
    p.tt(var, var, env["lnscr"][0:1, 0:512], AluOpType.subtract, i=(s_v, 1))
    p.act(out=var, in_=var, func=AF.Ln, bias=env["eps1"][:], w=[(s_v, s_v.n)], i=(s_s, 1))
    p.act(out=var, in_=var, func=AF.Exp, scale=-0.5, i=(s_s, 1))
    p.tt(mu, mu, var, AluOpType.mult, w=[(s_s, s_s.n)], i=(s_v, 1))
    p.mm(ps_bc[:, 0:512], env["ones1f"][:], var, w=[(s_v, s_v.n)], i=(s_t, 1))
    p.mm(ps_bc[:, 512:1024], env["ones1f"][:], mu, i=(s_t, 1))
    for c in range(KC):
        xc = xbuf[:, c * 512:(c + 1) * 512]
        p.tt(env["lnscr"][:], xc, ps_bc[:, 0:512], AluOpType.mult,
             w=[(s_t, s_t.n)], i=(s_v, 1))
        p.tt(xln_slice_fn(c), env["lnscr"][:], ps_bc[:, 512:1024],
             AluOpType.subtract, i=(s_v, 1))



def build_attn_b(kind, P=None, env=None, xap=None, xnap=None, nm=None):
    p = P if P is not None else Prog()
    nm = nm or {}
    g = lambda n: nm.get(n, n)
    if xap is None:
        x = p.dram("x", [D, T], F32, kind="ExternalInput").ap()
        xn = p.dram("xn", [D, T], F32, kind="ExternalOutput").ap()
    else:
        x, xn = xap, xnap
    cos_t = p.dram("cos_t", [128, SEQ], F32, kind="ExternalInput").ap()
    sin_t = p.dram("sin_t", [128, SEQ], F32, kind="ExternalInput").ap()
    e2sel = p.dram("e2sel", [128, 2], F32, kind="ExternalInput").ap()
    if kind == "l0":
        wqkv_d = p.dram(g("w_qkv"), [D, 3 * D], F32, kind="ExternalInput").ap()
        wo_d = p.dram(g("w_o"), [D, D], F32, kind="ExternalInput").ap()
    else:
        wq_d = p.dram(g("Wq"), [D, D], F32, kind="ExternalInput").ap()
        wk_d = p.dram(g("Wk"), [D, D], F32, kind="ExternalInput").ap()
        wv_d = p.dram(g("Wv"), [D, D], F32, kind="ExternalInput").ap()
        wo_d = p.dram(g("Wo"), [D, D], F32, kind="ExternalInput").ap()
        qnw_d = p.dram(g("q_norm_w"), [1, 64], F32, kind="ExternalInput").ap()
        knw_d = p.dram(g("k_norm_w"), [1, 64], F32, kind="ExternalInput").ap()
        cmask_d = p.dram("cmask", [128, 4 * 512], F32, kind="ExternalInput").ap()

    if env is None:
        env = make_env(p)
    s_x, s_v, s_s, s_t, s_st, s_w = (env[k] for k in
                                     ("s_x", "s_v", "s_s", "s_t", "s_st", "s_w"))
    p.scope_enter()

    xln = p.sbuf("xln", [128, KC * SEQ], BF16)
    wqkv = p.sbuf("wqkv", [128, KC * 2 * D], BF16)
    wo = p.sbuf("wo", [128, KC * D], BF16)
    vtm = p.sbuf("vtm", [128, 16 * D], BF16)
    vplus = p.sbuf("vplus", [128, 2 * 16 * 65], BF16)
    cos2 = p.sbuf("cos2", [128, SEQ], BF16)
    sin2 = p.sbuf("sin2", [128, SEQ], BF16)
    e2s = p.sbuf("e2s", [128, 2], BF16)
    negb = p.sbuf("negb", [128, 2], F32)
    ptb = p.sbuf("ptb", [128, 4 * 512], BF16)
    rden = p.sbuf("rden", [1, 512], F32)
    bcast = p.sbuf("bcast", [64, 512], BF16)
    attn = p.sbuf("attn", [128, KC * SEQ], BF16)
    outsc = ptb[:, 0:2048].bitcast(F32)
    qh = env["xbf"][:, 0:SEQ]
    kh = env["xbf"][:, SEQ:2 * SEQ]
    sqs = env["sqb"][:, 0:SEQ]
    rot = env["sqb"][:, SEQ:2 * SEQ]
    if kind == "l1":
        qnw = p.sbuf("qnw", [1, 64], F32)
        knw = p.sbuf("knw", [1, 64], F32)
        cmask = p.sbuf("cmask_s", [128, 4 * 512], BF16)

    ps_a, ps_o, ps_c, ps_bc = (env["ps_a"], env["ps_o"], env["ps_c"], env["ps_bc"])

    if kind == "l0":
        for c in range(KC):
            p.gdma(out=wqkv[:, c * 2 * D: c * 2 * D + D],
                   in_=wqkv_d[c * 128:(c + 1) * 128, 0:D], i=(s_w, 16))
            p.gdma(out=wqkv[:, c * 2 * D + D: c * 2 * D + 2 * D],
                   in_=wqkv_d[c * 128:(c + 1) * 128, D:2 * D], i=(s_w, 16))
    else:
        for c in range(KC):
            p.gdma(out=wqkv[:, c * 2 * D: c * 2 * D + D],
                   in_=wq_d[c * 128:(c + 1) * 128, :], i=(s_w, 16))
            p.gdma(out=wqkv[:, c * 2 * D + D: c * 2 * D + 2 * D],
                   in_=wk_d[c * 128:(c + 1) * 128, :], i=(s_w, 16))
    for c in range(KC):
        p.gdma(out=wo[:, c * D:(c + 1) * D],
               in_=wo_d[c * 128:(c + 1) * 128, :], i=(s_w, 16))
    p.gdma(out=cos2[:], in_=cos_t, i=(s_w, 16))
    p.gdma(out=sin2[:], in_=sin_t, i=(s_w, 16))
    p.gdma(out=e2s[:], in_=e2sel, i=(s_w, 16))
    if kind == "l1":
        p.gdma(out=qnw[:], in_=qnw_d, i=(s_w, 16))
        p.gdma(out=knw[:], in_=knw_d, i=(s_w, 16))
        p.gdma(out=cmask[:], in_=cmask_d, i=(s_w, 16))
    wload_total = s_w.n

    emit_env_init(p, env)

    HOFF = {"q": 0, "k": D}

    exp_bank = {0: 0, 1: 0, 2: 0, 3: 0}
    pv_bank = {0: 0, 1: 0, 2: 0, 3: 0}

    for b in range(NB):
        for c in range(KC):
            p.gdma(out=attn[:, c * D:(c + 1) * D],
                   in_=(wqkv_d[c * 128:(c + 1) * 128, 2 * D:3 * D] if kind == "l0"
                        else wv_d[c * 128:(c + 1) * 128, :]),
                   w=[(s_t, s_t.n), (s_v, s_v.n)], i=(s_w, 16))
        vw_loaded = s_w.n
        for blk in range(4):
            emit_ln_block(p, env, x,
                          lambda c, blk=blk: xln[:, c * SEQ + blk * 512:
                                                 c * SEQ + blk * 512 + 512],
                          b, blk)

        p.wait("tensor", s_w, vw_loaded)
        evrel = {0: s_v.n, 1: s_v.n}
        for tc in range(16):
            for half in range(2):
                bank = (2 * tc + half) % 2
                for c in range(KC):
                    p.mm(ps_o[:, bank * 512: bank * 512 + 512],
                         xln[:, c * SEQ + tc * 128: c * SEQ + (tc + 1) * 128],
                         attn[:, c * D + half * 512: c * D + (half + 1) * 512],
                         start=(c == 0), stop=(c == KC - 1),
                         w=([(s_v, evrel[bank])] if c == 0 else ()),
                         i=((s_t, 1) if c == KC - 1 else None))
                p.vcopy(out=vtm[:, tc * D + half * 512: tc * D + (half + 1) * 512],
                        in_=ps_o[:, bank * 512: bank * 512 + 512],
                        w=[(s_t, s_t.n)], i=(s_v, 1))
                evrel[bank] = s_v.n

        for hp in range(8):
            evrel = {0: s_v.n, 1: s_v.n}
            for (dst, key) in ((qh, "q"), (kh, "k")):
                for qb in range(4):
                    bank = qb % 2
                    for c in range(KC):
                        p.mm(ps_o[:, bank * 512: bank * 512 + 512],
                             wqkv[:, c * 2 * D + HOFF[key] + hp * 128:
                                  c * 2 * D + HOFF[key] + (hp + 1) * 128],
                             xln[:, c * SEQ + qb * 512: c * SEQ + (qb + 1) * 512],
                             start=(c == 0), stop=(c == KC - 1),
                             w=([(s_v, evrel[bank])] if c == 0 else ()),
                             i=((s_t, 1) if c == KC - 1 else None))
                    p.vcopy(out=dst[:, qb * 512:(qb + 1) * 512],
                            in_=ps_o[:, bank * 512: bank * 512 + 512],
                            w=[(s_t, s_t.n)], i=(s_v, 1))
                    evrel[bank] = s_v.n

            if kind == "l1":
                for (dst, nw) in ((qh, qnw), (kh, knw)):
                    p.act(out=sqs[:], in_=dst[:], func=AF.Square,
                          w=[(s_v, s_v.n)], i=(s_s, 1))
                    for j in range(2):
                        for qb in range(4):
                            p.mm(ps_a[0:1, (qb % 2) * 512:(qb % 2) * 512 + 512],
                                 e2s[:, j:j + 1],
                                 sqs[:, qb * 512:(qb + 1) * 512],
                                 start=True, stop=True,
                                 w=[(s_s, s_s.n), (s_v, s_v.n)],
                                 i=(s_t, 1))
                            p.act(out=env["rows"][0:1, 0:512],
                                  in_=ps_a[0:1, (qb % 2) * 512:(qb % 2) * 512 + 512],
                                  func=AF.Ln, bias=env["eps6"][:], scale=1.0 / 64,
                                  w=[(s_t, s_t.n)], i=(s_s, 1))
                            p.act(out=env["rows"][0:1, 0:512], in_=env["rows"][0:1, 0:512],
                                  func=AF.Exp, scale=-0.5, i=(s_s, 1))
                            p.mm(ps_c[j * 64:(j + 1) * 64, 0:512], nw[:],
                                 env["rows"][0:1, 0:512],
                                 w=[(s_s, s_s.n), (s_v, s_v.n)], i=(s_t, 1))
                            p.tt(dst[j * 64:(j + 1) * 64, qb * 512:(qb + 1) * 512],
                                 dst[j * 64:(j + 1) * 64, qb * 512:(qb + 1) * 512],
                                 ps_c[j * 64:(j + 1) * 64, 0:512], AluOpType.mult,
                                 w=[(s_t, s_t.n)], i=(s_v, 1))

            for dst in (qh, kh):
                p.ts(rot[0:32, :], dst[32:64, :], -1.0, w=[(s_v, s_v.n), (s_s, s_s.n)], i=(s_v, 1))
                p.vcopy(rot[32:64, :], dst[0:32, :], i=(s_v, 1))
                p.ts(rot[64:96, :], dst[96:128, :], -1.0, i=(s_v, 1))
                p.vcopy(rot[96:128, :], dst[64:96, :], i=(s_v, 1))
                p.tt(rot[:], rot[:], sin2[:], AluOpType.mult, i=(s_v, 1))
                p.tt(dst[:], dst[:], cos2[:], AluOpType.mult, i=(s_v, 1))
                p.tt(dst[:], dst[:], rot[:], AluOpType.add, i=(s_v, 1))

            for (src, col) in ((qh, 0), (kh, 1)):
                p.act(out=sqs[:], in_=src[:], func=AF.Square,
                      w=[(s_v, s_v.n)], i=(s_s, 1))
                for j in range(2):
                    slot = 8 + 2 * col + j
                    for qb in range(4):
                        p.mm(ps_a[0:1, (qb % 2) * 512:(qb % 2) * 512 + 512],
                             e2s[:, j:j + 1], sqs[:, qb * 512:(qb + 1) * 512],
                             start=True, stop=True,
                             w=[(s_s, s_s.n), (s_v, s_v.n)], i=(s_t, 1))
                        p.vredmax(env["small"][0:1, slot:slot + 1]
                                  if qb == 0 else env["small"][0:1, 12:13],
                                  ps_a[0:1, (qb % 2) * 512:(qb % 2) * 512 + 512],
                                  w=[(s_t, s_t.n)], i=(s_v, 1))
                        if qb > 0:
                            p.tt(env["small"][0:1, slot:slot + 1],
                                 env["small"][0:1, slot:slot + 1],
                                 env["small"][0:1, 12:13], AluOpType.max, i=(s_v, 1))
            for j in range(2):
                pj = env["small"][0:1, 0 + j:1 + j]
                p.tt(pj, env["small"][0:1, 8 + j:9 + j],
                     env["small"][0:1, 10 + j:11 + j], AluOpType.mult,
                     w=[(s_v, s_v.n)], i=(s_v, 1))
                p.act(out=pj, in_=pj, func=AF.Ln, w=[(s_v, s_v.n)], i=(s_s, 1))
                p.act(out=pj, in_=pj, func=AF.Exp, scale=0.5, i=(s_s, 1))
                p.ts(pj, pj, -0.125, w=[(s_s, s_s.n)], i=(s_v, 1))
                p.mm(ps_c[:, 0:1], env["ones1f"][:], pj, w=[(s_v, s_v.n)], i=(s_t, 1))
                p.vcopy(negb[:, j:j + 1], ps_c[:, 0:1], w=[(s_t, s_t.n)], i=(s_v, 1))

            for j in range(2):
                hg = hp * 2 + j
                p.vmemset(vplus[:, j * 1040:(j + 1) * 1040], 1.0,
                          w=[(s_t, s_t.n)], i=(s_v, 1))
                for tc in range(16):
                    p.vcopy(vplus[:, j * 1040 + tc * 65: j * 1040 + tc * 65 + 64],
                            vtm[:, tc * D + hg * 64: tc * D + (hg + 1) * 64],
                            i=(s_v, 1))
            vplus_ready = s_v.n
            evac_done = {0: 0, 1: 0}
            for qc in range(4):
                nkt = 16 if kind == "l0" else (qc + 1) * 4
                pend = []

                def emit_pv(j, kt, slot, wv3):
                    if kt == 0 and evac_done[j]:
                        wv3 = wv3 + [(s_v, evac_done[j])]
                    p.mm(ps_o[0:65, j * 512: j * 512 + 512],
                         vplus[:, j * 1040 + kt * 65: j * 1040 + (kt + 1) * 65],
                         ptb[:, slot * 512:(slot + 1) * 512],
                         start=(kt == 0), stop=(kt == nkt - 1),
                         w=wv3, i=(s_t, 1))
                    pv_bank[slot] = s_t.n

                for kt in range(nkt):
                    for j in range(2):
                        slot = j * 2 + (kt % 2)
                        sbank = (ps_a[:, (kt % 2) * 512:(kt % 2) * 512 + 512]
                                 if j == 0 else
                                 ps_bc[:, (kt % 2) * 512:(kt % 2) * 512 + 512])
                        wv = [(s_v, vplus_ready)]
                        if exp_bank[slot]:
                            wv.append((s_s, exp_bank[slot]))
                        p.mm(sbank,
                             kh[j * 64:(j + 1) * 64, kt * 128:(kt + 1) * 128],
                             qh[j * 64:(j + 1) * 64, qc * 512:(qc + 1) * 512],
                             w=wv, i=(s_t, 1))
                        smm_done = s_t.n
                        wv2 = [(s_t, smm_done)]
                        if pv_bank[slot]:
                            wv2.append((s_t, pv_bank[slot]))
                        p.act(out=ptb[:, slot * 512:(slot + 1) * 512],
                              in_=sbank,
                              func=AF.Exp, scale=0.125, bias=negb[:, j:j + 1],
                              w=wv2, i=(s_s, 1))
                        exp_bank[slot] = s_s.n
                        wv3 = [(s_s, exp_bank[slot])]
                        if kind == "l1" and kt >= 4 * qc:
                            p.tt(ptb[:, slot * 512:(slot + 1) * 512],
                                 ptb[:, slot * 512:(slot + 1) * 512],
                                 cmask[:, (kt - 4 * qc) * 512:(kt - 4 * qc + 1) * 512],
                                 AluOpType.mult, w=[(s_s, exp_bank[slot])], i=(s_v, 1))
                            wv3 = [(s_v, s_v.n)]
                        pend.append((j, kt, slot, wv3))
                    if len(pend) > 2:
                        for args in pend[:-2]:
                            emit_pv(*args)
                        pend = pend[-2:]
                for args in pend:
                    emit_pv(*args)
                for j in range(2):
                    hg = hp * 2 + j
                    p.vrecip(rden[:], ps_o[64:65, j * 512: j * 512 + 512],
                             w=[(s_t, s_t.n)], i=(s_v, 1))
                    p.mm(ps_c[0:64, 0:512], env["ones1f"][0:1, 0:64], rden[:],
                         w=[(s_v, s_v.n)], i=(s_t, 1))
                    p.scopy(out=bcast[:], in_=ps_c[0:64, 0:512],
                            w=[(s_t, s_t.n)], i=(s_s, 1))
                    p.tt(attn[(hg % 2) * 64:(hg % 2) * 64 + 64,
                              (hg // 2) * SEQ + qc * 512:(hg // 2) * SEQ + (qc + 1) * 512],
                         ps_o[0:64, j * 512: j * 512 + 512],
                         bcast[:], AluOpType.mult,
                         w=[(s_s, s_s.n)], i=(s_v, 1))
                    evac_done[j] = s_v.n

        for blk in range(4):
            tof = b * SEQ + blk * 512
            p.gdma(out=env["xbuf"][:].rearrange("p (c t) -> p c t", c=KC),
                   in_=x[:, tof:tof + 512].rearrange("(c p) t -> p c t", p=128),
                   w=[(s_v, s_v.n), (s_st, s_st.n)], i=(s_x, 16))
            store_done = {}
            evrel = {0: s_v.n, 1: s_v.n}
            for oc in range(KC):
                bank = oc % 2
                for c in range(KC):
                    p.mm(ps_o[:, bank * 512: bank * 512 + 512],
                         wo[:, c * D + oc * 128: c * D + (oc + 1) * 128],
                         attn[:, c * SEQ + blk * 512: c * SEQ + (blk + 1) * 512],
                         start=(c == 0), stop=(c == KC - 1),
                         w=([(s_v, evrel[bank])] if c == 0 else ()),
                         i=((s_t, 1) if c == KC - 1 else None))
                wv = [(s_t, s_t.n), (s_x, s_x.n)]
                if oc - 2 in store_done:
                    wv.append((s_st, store_done[oc - 2]))
                p.tt(outsc[:, bank * 512: bank * 512 + 512],
                     ps_o[:, bank * 512: bank * 512 + 512],
                     env["xbuf"][:, oc * 512:(oc + 1) * 512], AluOpType.add,
                     w=wv, i=(s_v, 1))
                evrel[bank] = s_v.n
                p.sydma(out=xn[oc * 128:(oc + 1) * 128, tof:tof + 512],
                        in_=outsc[:, bank * 512: bank * 512 + 512],
                        w=[(s_v, s_v.n)], i=(s_st, 16))
                store_done[oc] = s_st.n

    p.wait("sync", s_st, s_st.n)
    p.scope_exit()
    return p



def build_attn_a_l0(P=None, env=None, xap=None, xnap=None, nm=None):
    p = P if P is not None else Prog()
    nm = nm or {}
    g = lambda n: nm.get(n, n)
    if xap is None:
        x = p.dram("x", [D, T], F32, kind="ExternalInput").ap()
        xn = p.dram("xn", [D, T], F32, kind="ExternalOutput").ap()
    else:
        x, xn = xap, xnap
    qw_d = p.dram(g("q_w"), [D, D], F32, kind="ExternalInput").ap()
    kw_d = p.dram(g("k_w"), [D, D], F32, kind="ExternalInput").ap()
    vw_d = p.dram(g("v_w"), [D, D], F32, kind="ExternalInput").ap()
    ow_d = p.dram(g("out_w"), [D, D], F32, kind="ExternalInput").ap()
    qkb_d = p.dram("qkbT", [128, 16], F32, kind="ExternalInput").ap()
    vb_d = p.dram("aa0_vb", [1, D], F32, kind="ExternalInput").ap()
    gbo_d = p.dram("gboT", [128, 24], F32, kind="ExternalInput").ap()
    pja_d = p.dram("projT_aug", [65, 256], F32, kind="ExternalInput").ap()
    ut_d = p.dram("utmask", [128, 128], F32, kind="ExternalInput").ap()
    e2sel = p.dram("e2sel", [128, 2], F32, kind="ExternalInput").ap()

    if env is None:
        env = make_env(p)
    s_x, s_v, s_s, s_t, s_st, s_w = (env[k] for k in
                                     ("s_x", "s_v", "s_s", "s_t", "s_st", "s_w"))
    p.scope_enter()

    xln = p.sbuf("xln", [128, KC * SEQ], BF16)
    attn = p.sbuf("attn", [128, KC * SEQ], BF16)
    vtm = p.sbuf("vtm", [128, 16 * D], BF16)
    wqh = p.sbuf("wqh", [128, 2 * KC * 128], BF16)
    wkh = p.sbuf("wkh", [128, 2 * KC * 128], BF16)
    qaug = p.sbuf("qaug", [128, SEQ], BF16)
    kaug = p.sbuf("kaug", [128, SEQ], BF16)
    qpf = p.sbuf("qpf", [128, 2 * SEQ], BF16)
    kpf = p.sbuf("kpf", [128, 2 * SEQ], BF16)
    kptm = p.sbuf("kptm", [128, 16 * 256], BF16)
    pja = p.sbuf("pja", [128, 256], BF16)
    utm = p.sbuf("utm", [128, 128], BF16)
    e2s = p.sbuf("e2s", [128, 2], BF16)
    vplus = p.sbuf("vplus", [128, 16 * 65], BF16)
    Ss = p.sbuf("Ss", [128, 2 * 65], F32)
    Sb = p.sbuf("Sb", [128, 2 * 65], BF16)
    abuf = p.sbuf("abuf", [128, 2 * 128], BF16)
    qkb = p.sbuf("qkb", [128, 16], F32)
    gbo = p.sbuf("gbo", [128, 24], F32)
    vbt = p.sbuf("vbt", [128, 2 * 512], F32)
    rden = p.sbuf("rdenA", [1, 128], F32)
    bca = p.sbuf("bca", [64, 128], BF16)
    outsc = qpf[:, 0:2048].bitcast(F32)
    ones128f = p.sbuf("ones128f", [128, 1], F32)
    qh_pair = env["xbf"][:, 0:SEQ]
    kh_pair = env["xbf"][:, SEQ:2 * SEQ]
    sqs = env["sqb"][:, 0:SEQ]
    resbuf = env["xbuf"]

    ps_a, ps_o, ps_c, ps_bc = env["ps_a"], env["ps_o"], env["ps_c"], env["ps_bc"]

    for (dst, srcd) in ((pja[0:65, :], pja_d), (utm[:], ut_d), (e2s[:], e2sel),
                        (qkb[:], qkb_d), (gbo[:], gbo_d)):
        p.gdma(out=dst, in_=srcd, i=(s_w, 16))
    wload_total = s_w.n
    emit_env_init(p, env)
    p.vmemset(ones128f[:], 1.0, i=(s_v, 1))
    vbrow = p.sbuf("vbrow", [1, D], F32)
    p.gdma(out=vbrow[:], in_=vb_d, i=(s_w, 16))
    p.wait("tensor", s_w, s_w.n)
    for half in range(2):
        p.mm(ps_c[:, 0:512], env["ones1f"][:], vbrow[0:1, half * 512:(half + 1) * 512],
             w=[(s_v, s_v.n)], i=(s_t, 1))
        p.vcopy(vbt[:, half * 512:(half + 1) * 512], ps_c[:, 0:512],
                w=[(s_t, s_t.n)], i=(s_v, 1))

    at_bank = {0: 0, 1: 0}
    nd_bank = {0: 0, 1: 0}

    for b in range(NB):
        for c in range(KC):
            p.gdma(out=attn[:, c * D:(c + 1) * D],
                   in_=vw_d[c * 128:(c + 1) * 128, :],
                   w=[(s_t, s_t.n), (s_v, s_v.n)], i=(s_w, 16))
        vw_loaded = s_w.n

        for blk in range(4):
            emit_ln_block(p, env, x,
                          lambda c, blk=blk: xln[:, c * SEQ + blk * 512:
                                                 c * SEQ + blk * 512 + 512],
                          b, blk)

        p.wait("tensor", s_w, vw_loaded)
        evrelV = {0: s_v.n, 1: s_v.n}
        for tc in range(16):
            for half in range(2):
                bank = half
                for c in range(KC):
                    p.mm(ps_o[:, bank * 512: bank * 512 + 512],
                         xln[:, c * SEQ + tc * 128: c * SEQ + (tc + 1) * 128],
                         attn[:, c * D + half * 512: c * D + (half + 1) * 512],
                         start=(c == 0), stop=(c == KC - 1),
                         w=([(s_v, evrelV[bank])] if c == 0 else ()),
                         i=((s_t, 1) if c == KC - 1 else None))
                p.tt(vtm[:, tc * D + half * 512: tc * D + (half + 1) * 512],
                     ps_o[:, bank * 512: bank * 512 + 512],
                     vbt[:, half * 512:(half + 1) * 512], AluOpType.add,
                     w=[(s_t, s_t.n)], i=(s_v, 1))
                evrelV[bank] = s_v.n

        wh_done = {0: 0, 1: 0}
        for hp in range(8):
            hpar = hp % 2
            for (whdst, wsrcd) in ((wqh, qw_d), (wkh, kw_d)):
                p.gdma(out=whdst[:, hpar * KC * 128:(hpar + 1) * KC * 128].rearrange(
                           "p (c o) -> p c o", c=KC),
                       in_=wsrcd[:, hp * 128:(hp + 1) * 128].rearrange(
                           "(c p) o -> p c o", p=128),
                       w=([(s_t, wh_done[hpar])] if wh_done[hpar] else ()),
                       i=(s_w, 16))
            wh_ld = s_w.n
            p.wait("tensor", s_w, wh_ld)
            evrelQ = {0: s_v.n, 1: s_v.n}
            for (dst, wsrc, bcol) in ((qh_pair, wqh, hp), (kh_pair, wkh, 8 + hp)):
                for qb in range(4):
                    bank = qb % 2
                    for c in range(KC):
                        p.mm(ps_o[:, bank * 512: bank * 512 + 512],
                             wsrc[:, hpar * KC * 128 + c * 128:
                                  hpar * KC * 128 + (c + 1) * 128],
                             xln[:, c * SEQ + qb * 512: c * SEQ + (qb + 1) * 512],
                             start=(c == 0), stop=(c == KC - 1),
                             w=([(s_v, evrelQ[bank])] if c == 0 else ()),
                             i=((s_t, 1) if c == KC - 1 else None))
                    p.ts(dst[:, qb * 512:(qb + 1) * 512],
                         ps_o[:, bank * 512: bank * 512 + 512],
                         qkb[:, bcol:bcol + 1], op0=AluOpType.add,
                         w=[(s_t, s_t.n)], i=(s_v, 1))
                    evrelQ[bank] = s_v.n
            wh_done[hpar] = s_t.n
            for j in range(2):
                h = hp * 2 + j
                for (aug, pair) in ((qaug, qh_pair), (kaug, kh_pair)):
                    p.vcopy(aug[0:64, :], pair[j * 64:(j + 1) * 64, :],
                            w=[(s_v, s_v.n)], i=(s_v, 1))
                    p.act(out=sqs[:], in_=pair[:], func=AF.Square,
                          w=[(s_v, s_v.n)], i=(s_s, 1))
                    for qb in range(4):
                        p.mm(ps_c[0:1, 0:512], e2s[:, j:j + 1],
                             sqs[:, qb * 512:(qb + 1) * 512],
                             w=[(s_s, s_s.n), (s_v, s_v.n)], i=(s_t, 1))
                        p.ts(aug[64:65, qb * 512:(qb + 1) * 512], ps_c[0:1, 0:512],
                             -1.0 / 16, op0=AluOpType.mult,
                             w=[(s_t, s_t.n)], i=(s_v, 1))
                exprel = {0: 0, 1: 0}
                first_map = [True]
                for (src, dst) in ((qaug, qpf), (kaug, kpf)):
                    for mc in range(2):
                        for qb in range(4):
                            bank = qb % 2
                            wvm = []
                            if first_map[0]:
                                wvm = [(s_v, s_v.n), (s_s, s_s.n)]
                                first_map[0] = False
                            elif exprel[bank]:
                                wvm = [(s_s, exprel[bank])]
                            p.mm(ps_o[:, bank * 512: bank * 512 + 512],
                                 pja[0:65, mc * 128:(mc + 1) * 128],
                                 src[0:65, qb * 512:(qb + 1) * 512],
                                 w=wvm, i=(s_t, 1))
                            p.act(out=dst[:, mc * SEQ + qb * 512: mc * SEQ + (qb + 1) * 512],
                                  in_=ps_o[:, bank * 512: bank * 512 + 512],
                                  func=AF.Exp, scale=0.125,
                                  w=[(s_t, s_t.n)], i=(s_s, 1))
                            exprel[bank] = s_s.n
                for tc in range(16):
                    bank = tc % 2
                    p.mm(ps_o[:, bank * 512: bank * 512 + 256],
                         kaug[0:65, tc * 128:(tc + 1) * 128],
                         pja[0:65, :],
                         w=[(s_s, exprel[bank])], i=(s_t, 1))
                    p.act(out=kptm[:, tc * 256:(tc + 1) * 256],
                          in_=ps_o[:, bank * 512: bank * 512 + 256],
                          func=AF.Exp, scale=0.125,
                          w=[(s_t, s_t.n)], i=(s_s, 1))
                    exprel[bank] = s_s.n
                p.vmemset(vplus[:], 1.0, w=[(s_t, s_t.n)], i=(s_v, 1))
                for tc in range(16):
                    p.vcopy(vplus[:, tc * 65: tc * 65 + 64],
                            vtm[:, tc * D + h * 64: tc * D + (h + 1) * 64],
                            i=(s_v, 1))
                p.vmemset(Ss[:], 0.0, i=(s_v, 1))
                p.vmemset(Sb[:], 0.0, i=(s_v, 1))
                ready = s_v.n
                p.wait("scalar", s_v, ready)
                for tc in range(16):
                    par = tc % 2
                    wv = [(s_s, s_s.n), (s_v, max(ready, at_bank[par]))]
                    for mc in range(2):
                        p.mm(ps_a[:, par * 512: par * 512 + 128],
                             kpf[:, mc * SEQ + tc * 128: mc * SEQ + (tc + 1) * 128],
                             qpf[:, mc * SEQ + tc * 128: mc * SEQ + (tc + 1) * 128],
                             start=(mc == 0), stop=(mc == 1),
                             w=(wv if mc == 0 else ()),
                             i=((s_t, 1) if mc == 1 else None))
                    p.tt(abuf[:, par * 128:(par + 1) * 128],
                         ps_a[:, par * 512: par * 512 + 128], utm[:],
                         AluOpType.mult, w=[(s_t, s_t.n)], i=(s_v, 1))
                    at_bank[par] = s_v.n
                    wv2 = [(s_v, s_v.n)]
                    if nd_bank[par]:
                        wv2.append((s_v, nd_bank[par]))
                    p.mm(ps_o[0:65, par * 512: par * 512 + 128],
                         Sb[:, 0:65], qpf[:, 0 * SEQ + tc * 128:(tc + 1) * 128],
                         start=True, stop=False, w=wv2, i=None)
                    p.mm(ps_o[0:65, par * 512: par * 512 + 128],
                         Sb[:, 65:130], qpf[:, SEQ + tc * 128: SEQ + (tc + 1) * 128],
                         start=False, stop=False, i=None)
                    p.mm(ps_o[0:65, par * 512: par * 512 + 128],
                         vplus[:, tc * 65:(tc + 1) * 65],
                         abuf[:, par * 128:(par + 1) * 128],
                         start=False, stop=True, i=(s_t, 1))
                    numden_done = s_t.n
                    for mc in range(2):
                        p.mm(ps_bc[:, mc * 512: mc * 512 + 65],
                             kptm[:, tc * 256 + mc * 128: tc * 256 + (mc + 1) * 128],
                             vplus[:, tc * 65:(tc + 1) * 65],
                             w=([(s_v, s_v.n)] if mc == 0 else ()), i=(s_t, 1))
                    p.wait("vector", s_t, s_t.n)
                    for mc in range(2):
                        p.tt(Ss[:, mc * 65:(mc + 1) * 65], Ss[:, mc * 65:(mc + 1) * 65],
                             ps_bc[:, mc * 512: mc * 512 + 65], AluOpType.add,
                             i=(s_v, 1))
                        p.vcopy(Sb[:, mc * 65:(mc + 1) * 65],
                                Ss[:, mc * 65:(mc + 1) * 65], i=(s_v, 1))
                    p.vrecip(rden[:], ps_o[64:65, par * 512: par * 512 + 128],
                             w=[(s_t, numden_done)], i=(s_v, 1))
                    p.mm(ps_c[0:64, 0:128], env["ones1f"][0:1, 0:64], rden[:],
                         w=[(s_v, s_v.n)], i=(s_t, 1))
                    p.scopy(out=bca[:], in_=ps_c[0:64, 0:128],
                            w=[(s_t, s_t.n)], i=(s_s, 1))
                    p.tt(attn[(h % 2) * 64:(h % 2) * 64 + 64,
                              (h // 2) * SEQ + tc * 128:(h // 2) * SEQ + (tc + 1) * 128],
                         ps_o[0:64, par * 512: par * 512 + 128],
                         bca[:], AluOpType.mult,
                         w=[(s_s, s_s.n)], i=(s_v, 1))
                    nd_bank[par] = s_v.n

        for c in range(KC):
            p.gdma(out=vtm[:, c * D:(c + 1) * D], in_=ow_d[c * 128:(c + 1) * 128, :],
                   w=[(s_t, s_t.n), (s_v, s_v.n)], i=(s_w, 16))
        ow_loaded = s_w.n
        p.wait("tensor", s_w, ow_loaded)
        for blk in range(4):
            tof = b * SEQ + blk * 512
            evrelO = {0: s_v.n, 1: s_v.n}
            for oc in range(KC):
                bank = oc % 2
                for c in range(KC):
                    p.mm(ps_o[:, bank * 512: bank * 512 + 512],
                         vtm[:, c * D + oc * 128: c * D + (oc + 1) * 128],
                         attn[:, c * SEQ + blk * 512: c * SEQ + (blk + 1) * 512],
                         start=(c == 0), stop=(c == KC - 1),
                         w=([(s_v, evrelO[bank])] if c == 0 else ()),
                         i=((s_t, 1) if c == KC - 1 else None))
                p.ts(resbuf[:, oc * 512:(oc + 1) * 512],
                     ps_o[:, bank * 512: bank * 512 + 512],
                     gbo[:, 16 + oc:16 + oc + 1], op0=AluOpType.add,
                     w=[(s_t, s_t.n)], i=(s_v, 1))
                p.tt(resbuf[:, oc * 512:(oc + 1) * 512],
                     resbuf[:, oc * 512:(oc + 1) * 512],
                     xln[:, oc * SEQ + blk * 512: oc * SEQ + (blk + 1) * 512],
                     AluOpType.add, i=(s_v, 1))
                evrelO[bank] = s_v.n
            p.act(out=env["sqb"][:], in_=resbuf[:], func=AF.Square,
                  w=[(s_v, s_v.n)], i=(s_s, 1))
            for c in range(KC):
                p.mm(ps_a[0:1, 0:512], ones128f[:], resbuf[:, c * 512:(c + 1) * 512],
                     start=(c == 0), stop=(c == KC - 1),
                     w=([(s_v, s_v.n)] if c == 0 else ()),
                     i=((s_t, 1) if c == KC - 1 else None))
            for c in range(KC):
                p.mm(ps_a[0:1, 512:1024], env["ones128"][:],
                     env["sqb"][:, c * 512:(c + 1) * 512],
                     start=(c == 0), stop=(c == KC - 1),
                     w=([(s_s, s_s.n)] if c == 0 else ()),
                     i=((s_t, 1) if c == KC - 1 else None))
            mu = env["rows"][0:1, 0:512]
            var = env["rows"][0:1, 512:1024]
            p.ts(mu, ps_a[0:1, 0:512], 1.0 / D, w=[(s_t, s_t.n)], i=(s_v, 1))
            p.ts(var, ps_a[0:1, 512:1024], 1.0 / D, i=(s_v, 1))
            p.tt(env["lnscr"][0:1, 0:512], mu, mu, AluOpType.mult, i=(s_v, 1))
            p.tt(var, var, env["lnscr"][0:1, 0:512], AluOpType.subtract, i=(s_v, 1))
            p.act(out=var, in_=var, func=AF.Ln, bias=env["eps6"][:],
                  w=[(s_v, s_v.n)], i=(s_s, 1))
            p.act(out=var, in_=var, func=AF.Exp, scale=-0.5, i=(s_s, 1))
            p.tt(mu, mu, var, AluOpType.mult, w=[(s_s, s_s.n)], i=(s_v, 1))
            p.mm(ps_bc[:, 0:512], env["ones1f"][:], var, w=[(s_v, s_v.n)], i=(s_t, 1))
            p.mm(ps_bc[:, 512:1024], env["ones1f"][:], mu, i=(s_t, 1))
            store_done = {}
            for oc in range(KC):
                par2 = oc % 2
                p.gdma(out=env["lnscr"][:],
                       in_=x[:, tof:tof + 512].rearrange("(c p) t -> p c t", p=128)[
                           :, oc, :] if False else
                           x[oc * 128:(oc + 1) * 128, tof:tof + 512],
                       w=[(s_v, s_v.n)], i=(s_x, 16))
                o = outsc[:, par2 * 512:(par2 + 1) * 512]
                wv = [(s_t, s_t.n), (s_x, s_x.n)]
                if oc - 2 in store_done:
                    wv.append((s_st, store_done[oc - 2]))
                p.tt(o, resbuf[:, oc * 512:(oc + 1) * 512], ps_bc[:, 0:512],
                     AluOpType.mult, w=wv, i=(s_v, 1))
                p.tt(o, o, ps_bc[:, 512:1024], AluOpType.subtract, i=(s_v, 1))
                p.ts(o, o, gbo[:, oc:oc + 1], gbo[:, 8 + oc:8 + oc + 1],
                     op0=AluOpType.mult, op1=AluOpType.add, i=(s_v, 1))
                p.tt(o, o, env["lnscr"][:], AluOpType.add, i=(s_v, 1))
                p.sydma(out=xn[oc * 128:(oc + 1) * 128, tof:tof + 512],
                        in_=o, w=[(s_v, s_v.n)], i=(s_st, 16))
                store_done[oc] = s_st.n

    p.wait("sync", s_st, s_st.n)
    p.scope_exit()
    return p


def run_attn_a_l0(x_shards, q_w, q_b, k_w, k_b, v_w, v_b, out_w, out_b, proj,
                  gamma, beta):
    consts = _host_consts_attn()
    nc = _get_prog("attn_a_l0", lambda: build_attn_a_l0())
    qkbT = np.zeros((128, 16), np.float32)
    qkbT[:, 0:8] = np.asarray(q_b, np.float32).reshape(8, 128).T
    qkbT[:, 8:16] = np.asarray(k_b, np.float32).reshape(8, 128).T
    gboT = np.zeros((128, 24), np.float32)
    gboT[:, 0:8] = np.asarray(gamma, np.float32).reshape(8, 128).T
    gboT[:, 8:16] = np.asarray(beta, np.float32).reshape(8, 128).T
    gboT[:, 16:24] = np.asarray(out_b, np.float32).reshape(8, 128).T
    pja = np.ones((65, 256), np.float32)
    pja[0:64, :] = np.asarray(proj, np.float32).T
    ut = (np.arange(128)[:, None] <= np.arange(128)[None, :]).astype(np.float32)
    in_maps = []
    for c in range(NCORES):
        m = {"x": x_shards[c], "q_w": np.asarray(q_w, np.float32),
             "k_w": np.asarray(k_w, np.float32), "v_w": np.asarray(v_w, np.float32),
             "out_w": np.asarray(out_w, np.float32),
             "qkbT": qkbT, "aa0_vb": np.asarray(v_b, np.float32).reshape(1, D),
             "gboT": gboT, "projT_aug": pja, "utmask": ut,
             "e2sel": consts["e2sel"]}
        in_maps.append(m)
    res = run_bass_kernel_spmd(nc, in_maps, core_ids=list(range(NCORES)))
    return [np.asarray(r["xn"]) for r in res.results]



def build_ffn_l0(mode="full", P=None, env=None, xap=None, xnap=None, nm=None):
    p = P if P is not None else Prog()
    nm = nm or {}
    g = lambda n: nm.get(n, n)
    dbg = (mode == "wdump")
    if xap is None:
        x = p.dram("x", [D, T], F32, kind="ExternalInput").ap()
        xn = p.dram("xn", [D, T], F32, kind="ExternalOutput").ap()
    else:
        x, xn = xap, xnap
    wg_d = p.dram(g("Wg"), [D, 8], F32, kind="ExternalInput").ap()
    w1_d = p.dram(g("W1"), [8, D, 4096], F32, kind="ExternalInput").ap()
    b1_d = p.dram("b1T", [128, 256], F32, kind="ExternalInput").ap()
    w2_d = p.dram(g("W2"), [8, 4096, D], F32, kind="ExternalInput").ap()
    b2_d = p.dram("b2T", [128, 64], F32, kind="ExternalInput").ap()
    e8s_d = p.dram("E8sel", [128, 8], F32, kind="ExternalInput").ap()
    e8t_d = p.dram("E8T", [8, 128], F32, kind="ExternalInput").ap()

    cc_in = p.dram("cc_in", [8, T], F32)
    cc_out = p.dram("cc_out", [64, T], F32)
    xlnd = p.dram("xlnd", [D, T], BF16)
    wdram0 = p.dram("wdram0", [8, T], BF16)
    if dbg:
        wout = p.dram("wout", [8, T], BF16, kind="ExternalOutput").ap()
        thout = p.dram("thout", [8, 8], F32, kind="ExternalOutput").ap()
        xlout = p.dram("xlout", [D, 512], BF16, kind="ExternalOutput").ap()
        scout = p.dram("scout", [64, T], F32, kind="ExternalOutput").ap()

    if env is None:
        env = make_env(p)
    s_x, s_v, s_s, s_t, s_st, s_w = (env[k] for k in
                                     ("s_x", "s_v", "s_s", "s_t", "s_st", "s_w"))
    s_ac = env["s_ac"]
    s_wr = env["s_wr"]
    s_cc = env["s_cc"]
    p.scope_enter()

    w1r = p.sbuf("w1r", [128, KC * 4096], BF16)
    w2r = p.sbuf("w2r", [128, 32 * D], BF16)
    hbuf = p.sbuf("hbuf", [128, 32 * 512], BF16)
    wgb = p.sbuf("wgb", [128, KC * 8], BF16)
    b1T = p.sbuf("b1Ts", [128, 256], F32)
    b2T = p.sbuf("b2Ts", [128, 64], F32)
    e8s = p.sbuf("e8s", [128, 8], F32)
    e8t = p.sbuf("e8t", [8, 128], F32)
    scstage = p.sbuf("scstage", [8, 1024], F32)
    th = p.sbuf("th", [8, 8], F32)
    rowcnt = p.sbuf("rowcnt", [128, 1], F32)
    wrow = p.sbuf("wrow", [1, 2 * 512], BF16)
    xlnb = hbuf[:, 0:4096]
    sc128 = hbuf[0:64, 0:8192].bitcast(F32)
    sloc = hbuf[0:8, 8192:16384].bitcast(F32)
    w8 = hbuf[0:8, 4096:8192]
    xt2a = env["xbuf"][:, 0:2048].bitcast(BF16)
    xt2b = env["xbuf"][:, 2048:4096].bitcast(BF16)
    ysc0 = env["xbf"][:, 0:1024].bitcast(F32)
    ysc1 = env["xbf"][:, 1024:2048].bitcast(F32)
    wtile = env["sqb"][:, 0:1024].bitcast(F32)

    ps_a, ps_o, ps_c, ps_bc = env["ps_a"], env["ps_o"], env["ps_c"], env["ps_bc"]

    for c in range(KC):
        p.gdma(out=wgb[:, c * 8:(c + 1) * 8], in_=wg_d[c * 128:(c + 1) * 128, :],
               i=(s_w, 16))
    for (dst, srcd) in ((b1T, b1_d), (b2T, b2_d), (e8s, e8s_d), (e8t, e8t_d)):
        p.gdma(out=dst[:], in_=srcd, i=(s_w, 16))
    p.gdma(out=xn, in_=x, i=(s_ac, 16))
    xn_init = s_ac.n
    wload_total = s_w.n
    emit_env_init(p, env)
    p.wait("vector", s_w, wload_total)
    p.wait("tensor", s_w, wload_total)

    for gblk in range(8):
        b, blk = divmod(gblk, 4)
        emit_ln_block(p, env, x,
                      lambda c: xlnb[:, c * 512:(c + 1) * 512],
                      b, blk)
        tof = gblk * 512
        p.gdma(out=xlnd[:, tof:tof + 512].rearrange("(c p) t -> p c t", p=128),
               in_=xlnb.rearrange("p (c t) -> p c t", c=KC),
               w=[(s_v, s_v.n)], i=(s_x, 16))
        for c in range(KC):
            p.mm(ps_c[0:8, 0:512], wgb[:, c * 8:(c + 1) * 8],
                 xlnb[:, c * 512:(c + 1) * 512],
                 start=(c == 0), stop=(c == KC - 1),
                 w=([(s_v, s_v.n), (s_s, s_s.n)] if c == 0 else ()),
                 i=((s_t, 1) if c == KC - 1 else None))
        par = gblk % 2
        p.act(out=scstage[0:8, par * 512:(par + 1) * 512], in_=ps_c[0:8, 0:512],
              func=AF.Sigmoid, w=[(s_t, s_t.n), (s_x, s_x.n)], i=(s_s, 1))
        p.gdma(out=cc_in.ap()[0:8, tof:tof + 512],
               in_=scstage[0:8, par * 512:(par + 1) * 512],
               w=[(s_s, s_s.n)], i=(s_x, 16))

    p.collective("AllGather", AluOpType.bypass, [list(range(8))],
                 [cc_in.ap().opt()], [cc_out.ap().opt()],
                 w=[(s_x, s_x.n)], i=(s_cc, 1))
    p.wait("gpsimd", s_cc, s_cc.n)
    p.gdma(out=sc128, in_=cc_out.ap(),
           w=[(s_t, s_t.n), (s_v, s_v.n), (s_s, s_s.n)], i=(s_x, 16))
    sc_ready = s_x.n

    cmpscr = env["xbuf"][0:64, :]

    def VS(fn):
        fn()
        p.wait("vector", s_v, s_v.n)
        p.wait("scalar", s_s, s_s.n)
        p.wait("vector", s_s, s_s.n)
        p.wait("scalar", s_v, s_v.n)
        p.wait("tensor", s_v, s_v.n)
        p.wait("vector", s_t, s_t.n)

    lo = th[0:8, 0:1]
    hi = th[0:8, 1:2]
    mid = th[0:8, 2:3]
    cge = th[0:8, 3:4]
    kk = th[0:8, 4:5]
    VS(lambda: p.vmemset(lo, 0.0, w=[(s_x, sc_ready)], i=(s_v, 1)))
    VS(lambda: p.vmemset(hi, 1.0, i=(s_v, 1)))
    VS(lambda: p.vmemset(mid, 0.5, i=(s_v, 1)))
    VS(lambda: p.vmemset(kk, 8192.0, i=(s_v, 1)))
    for it in range(44):
        VS(lambda: p.mm(ps_c[0:64, 0:1], e8t[0:8, 0:64], mid, i=(s_t, 1)))
        VS(lambda: p.ts(cmpscr, sc128, ps_c[0:64, 0:1],
                        s2=0.0, op0=AluOpType.is_ge, op1=AluOpType.add,
                        accum=rowcnt[0:64, :], i=(s_v, 1)))
        VS(lambda: p.mm(ps_bc[0:8, 0:1], e8s[0:64, :], rowcnt[0:64, :], i=(s_t, 1)))
        VS(lambda: p.ts(cge, ps_bc[0:8, 0:1], kk, op0=AluOpType.is_ge, i=(s_v, 1)))
        t1 = th[0:8, 5:6]
        VS(lambda: p.tt(t1, mid, lo, AluOpType.subtract, i=(s_v, 1)))
        VS(lambda: p.tt(t1, cge, t1, AluOpType.mult, i=(s_v, 1)))
        VS(lambda: p.tt(lo, lo, t1, AluOpType.add, i=(s_v, 1)))
        VS(lambda: p.tt(t1, hi, mid, AluOpType.subtract, i=(s_v, 1)))
        VS(lambda: p.tt(t1, cge, t1, AluOpType.mult, i=(s_v, 1)))
        VS(lambda: p.tt(hi, mid, t1, AluOpType.add, i=(s_v, 1)))
        VS(lambda: p.ts(mid, lo, hi, 0.5, op0=AluOpType.add, op1=AluOpType.mult,
                        i=(s_v, 1)))
    slocal = hbuf[0:8, 8192:16384].bitcast(F32)
    mk = hbuf[0:8, 0:4096].bitcast(F32)
    VS(lambda: p.gdma(out=slocal, in_=cc_in.ap(), w=[(s_v, s_v.n)], i=(s_x, 16)))
    for half in range(2):
        sl_h = hbuf[0:8, 8192 + half * 4096: 8192 + (half + 1) * 4096].bitcast(F32)
        w8_h = hbuf[0:8, 4096 + half * 2048: 4096 + (half + 1) * 2048]
        VS(lambda sl_h=sl_h: p.ts(mk, sl_h, lo, op0=AluOpType.is_ge,
                                  w=[(s_x, s_x.n)], i=(s_v, 1)))
        VS(lambda sl_h=sl_h, w8_h=w8_h: p.tt(w8_h, mk, sl_h,
                                             AluOpType.mult, i=(s_v, 1)))
    VS(lambda: p.gdma(out=wdram0.ap(), in_=w8, w=[(s_v, s_v.n)], i=(s_wr, 16)))
    wdram_ready = s_wr.n
    if dbg:
        p.gdma(out=wout, in_=wdram0.ap(), w=[(s_wr, wdram_ready)], i=(s_wr, 16))
        p.gdma(out=thout, in_=th[:], w=[(s_v, s_v.n)], i=(s_wr, 16))
        p.gdma(out=xlout, in_=xlnd.ap()[:, 0:512], i=(s_wr, 16))
        p.gdma(out=scout, in_=cc_out.ap(), i=(s_wr, 16))

    ysc_dma = {k: 0 for k in range(4)}
    ysc_ts = {0: 0, 1: 0}
    gelu_done = {0: 0, 1: 0}
    hread = 0
    xt_done = {0: 0, 1: 0}
    yscs = [env["xbf"][:, i * 1024:(i + 1) * 1024].bitcast(F32) for i in range(4)]
    wpar = 0
    iters = [(e, tb) for e in range(8) for tb in range(8)]

    def xt2_of(i):
        return xt2a if (i % 2) == 0 else xt2b

    def load_xt(i):
        e, tb = iters[i]
        par = i % 2
        p.gdma(out=xt2_of(i).rearrange("p (c t) -> p c t", c=KC),
               in_=xlnd.ap()[:, tb * 512:(tb + 1) * 512].rearrange(
                   "(c p) t -> p c t", p=128),
               w=[(s_t, xt_done[par])] if xt_done[par] else [(s_wr, wdram_ready)],
               i=(s_x, 16))
        return s_x.n

    def load_wrow(i):
        e, tb = iters[i]
        par = i % 2
        p.gdma(out=wrow[0:1, par * 512:(par + 1) * 512],
               in_=wdram0.ap()[e:e + 1, tb * 512:(tb + 1) * 512],
               w=[(s_wr, wdram_ready), (s_t, s_t.n)], i=(s_wr, 16))
        return s_wr.n

    xt_ld = {0: load_xt(0)}
    wr_ld = {0: load_wrow(0)}
    for e in range(8):
        wv = [(s_t, wpar)] if wpar else [(s_v, s_v.n), (s_s, s_s.n)]
        for c in range(KC):
            p.gdma(out=w1r[:, c * 4096:(c + 1) * 4096],
                   in_=w1_d[e, c * 128:(c + 1) * 128, :],
                   w=(wv if c == 0 else ()), i=(s_w, 16))
        for q in range(4):
            p.gdma(out=w2r[:, q * 8 * D:(q + 1) * 8 * D].rearrange(
                       "p (c o) -> p c o", c=8),
                   in_=w2_d[e, q * 8 * 128:(q + 1) * 8 * 128, :].rearrange(
                       "(c p) o -> p c o", p=128), i=(s_w, 16))
        p.wait("tensor", s_w, s_w.n)
        for tb in range(8):
            i = e * 8 + tb
            par = i % 2
            xt2 = xt2_of(i)
            if i + 1 < 64:
                xt_ld[i + 1] = load_xt(i + 1)
                wr_ld[i + 1] = load_wrow(i + 1)
            p.mm(ps_bc[:, 0:512], env["ones1b"][:],
                 wrow[0:1, par * 512:(par + 1) * 512],
                 w=[(s_wr, wr_ld[i]), (s_s, s_s.n), (s_v, s_v.n)], i=(s_t, 1))
            p.scopy(out=wtile, in_=ps_bc[:, 0:512], w=[(s_t, s_t.n)], i=(s_s, 1))
            p.wait("tensor", s_x, xt_ld[i])
            for fs in range(32):
                bank = fs % 2
                for c in range(KC):
                    wvh = []
                    if c == 0:
                        if gelu_done[bank]:
                            wvh.append((s_s, gelu_done[bank]))
                        if fs == 0:
                            wvh.append((s_s, s_s.n))
                    p.mm(ps_a[:, bank * 512: bank * 512 + 512],
                         w1r[:, c * 4096 + fs * 128: c * 4096 + (fs + 1) * 128],
                         xt2[:, c * 512:(c + 1) * 512],
                         start=(c == 0), stop=(c == KC - 1),
                         w=wvh, i=((s_t, 1) if c == KC - 1 else None))
                wvg = [(s_t, s_t.n)]
                if fs == 0 and hread:
                    wvg.append((s_t, hread))
                p.act(out=hbuf[:, fs * 512:(fs + 1) * 512],
                      in_=ps_a[:, bank * 512: bank * 512 + 512],
                      func=AF.Gelu, bias=b1T[:, e * 32 + fs: e * 32 + fs + 1],
                      w=wvg, i=(s_s, 1))
                gelu_done[bank] = s_s.n
            xt_done[par] = s_t.n
            for oc in range(KC):
                bank = oc % 2
                for fc in range(32):
                    wvy = []
                    if fc == 0:
                        wvy.append((s_s, s_s.n))
                        if ysc_ts[bank]:
                            wvy.append((s_v, ysc_ts[bank]))
                    p.mm(ps_o[:, bank * 512: bank * 512 + 512],
                         w2r[:, fc * D + oc * 128: fc * D + (oc + 1) * 128],
                         hbuf[:, fc * 512:(fc + 1) * 512],
                         start=(fc == 0), stop=(fc == 31),
                         w=wvy, i=((s_t, 1) if fc == 31 else None))
                ysc = yscs[oc % 4]
                p.ts(ysc, ps_o[:, bank * 512: bank * 512 + 512],
                     b2T[:, e * 8 + oc: e * 8 + oc + 1], op0=AluOpType.add,
                     w=[(s_t, s_t.n), (s_ac, max(xn_init, ysc_dma[oc % 4]))],
                     i=(s_v, 1))
                ysc_ts[bank] = s_v.n
                p.tt(ysc, ysc, wtile, AluOpType.mult, i=(s_v, 1))
                p.gdma(out=xn[oc * 128:(oc + 1) * 128, tb * 512:(tb + 1) * 512],
                       in_=ysc, accum_op=AluOpType.add,
                       w=[(s_v, s_v.n)], i=(s_ac, 16))
                ysc_dma[oc % 4] = s_ac.n
            hread = s_t.n
        wpar = s_t.n

    p.wait("gpsimd", s_ac, s_ac.n)
    p.scope_exit()
    return p


def run_ffn_l0(x_shards, Wg, W1, b1, W2, b2, mode="full"):
    nc = _get_prog("ffn_l0_" + mode, lambda: build_ffn_l0(mode))
    b1T = np.ascontiguousarray(np.asarray(b1, np.float32).reshape(8, 32, 128)
                               .transpose(2, 0, 1).reshape(128, 256))
    b2T = np.ascontiguousarray(np.asarray(b2, np.float32).reshape(8, 8, 128)
                               .transpose(2, 0, 1).reshape(128, 64))
    E8sel = np.zeros((128, 8), np.float32)
    for pp in range(64):
        E8sel[pp, pp % 8] = 1.0
    E8T = np.ascontiguousarray(E8sel.T)
    in_maps = []
    for c in range(NCORES):
        m = {"x": x_shards[c], "Wg": np.asarray(Wg, np.float32),
             "W1": np.asarray(W1, np.float32), "b1T": b1T,
             "W2": np.asarray(W2, np.float32), "b2T": b2T,
             "E8sel": E8sel, "E8T": E8T}
        in_maps.append(m)
    res = run_bass_kernel_spmd(nc, in_maps, core_ids=list(range(NCORES)))
    if mode == "wdump":
        return ([np.asarray(r["xn"]) for r in res.results],
                [(np.asarray(r["wout"]), np.asarray(r["thout"]),
                  np.asarray(r["xlout"]), np.asarray(r["scout"])) for r in res.results])
    return [np.asarray(r["xn"]) for r in res.results]



def build_attn_a_l1(P=None, env=None, xap=None, xnap=None, nm=None):
    p = P if P is not None else Prog()
    nm = nm or {}
    g = lambda n: nm.get(n, n)
    if xap is None:
        x = p.dram("x", [D, T], F32, kind="ExternalInput").ap()
        xn = p.dram("xn", [D, T], F32, kind="ExternalOutput").ap()
    else:
        x, xn = xap, xnap
    wq_d = p.dram(g("W_q"), [D, D], F32, kind="ExternalInput").ap()
    wk_d = p.dram(g("W_k"), [D, D], F32, kind="ExternalInput").ap()
    wv_d = p.dram(g("W_v"), [D, D], F32, kind="ExternalInput").ap()
    wo_d = p.dram(g("W_o"), [D, D], F32, kind="ExternalInput").ap()
    wdn_d = p.dram(g("W_down_kv"), [D, 256], F32, kind="ExternalInput").ap()
    wuk_d = p.dram(g("W_up_k"), [256, D], F32, kind="ExternalInput").ap()
    wuv_d = p.dram(g("W_up_v"), [256, D], F32, kind="ExternalInput").ap()
    e2sel = p.dram("e2sel", [128, 2], F32, kind="ExternalInput").ap()
    dec_d = p.dram("dec28", [128, 28 * 512], F32, kind="ExternalInput").ap()

    if env is None:
        env = make_env(p)
    s_x, s_v, s_s, s_t, s_st, s_w = (env[k] for k in
                                     ("s_x", "s_v", "s_s", "s_t", "s_st", "s_w"))
    p.scope_enter()

    xln = p.sbuf("xln", [128, KC * SEQ], BF16)
    wq = p.sbuf("wq", [128, KC * D], BF16)
    wdn = p.sbuf("wdn", [128, KC * 256], BF16)
    wuk = p.sbuf("wuk", [128, 2 * D], BF16)
    wuv = p.sbuf("wuv", [128, 2 * D], BF16)
    latent = p.sbuf("latent", [128, 2 * SEQ], BF16)
    vtm = p.sbuf("vtm", [128, 16 * D], BF16)
    vplus = p.sbuf("vplus", [128, 16 * 65], BF16)
    e2s = p.sbuf("e2s", [128, 2], BF16)
    negb = p.sbuf("negb", [128, 2], F32)
    rden = p.sbuf("rden", [1, 512], F32)
    bcast = p.sbuf("bcast", [64, 512], BF16)
    attn = p.sbuf("attn", [128, KC * SEQ], BF16)
    R = p.sbuf("Rbuf", [128, 28 * 512], BF16)
    qh = env["xbf"][:, 0:SEQ]
    kh = env["xbf"][:, SEQ:2 * SEQ]
    sqs = env["sqb"][:, 0:SEQ]
    sdec = env["xbuf"][:, 0:1024]
    ptb = env["xbuf"][:, 1024:2048].bitcast(BF16)

    ps_a, ps_o, ps_c = env["ps_a"], env["ps_o"], env["ps_c"]

    for c in range(KC):
        p.gdma(out=wq[:, c * D:(c + 1) * D], in_=wq_d[c * 128:(c + 1) * 128, :], i=(s_w, 16))
        p.gdma(out=wdn[:, c * 256:(c + 1) * 256], in_=wdn_d[c * 128:(c + 1) * 128, :],
               i=(s_w, 16))
    for lc in range(2):
        p.gdma(out=wuk[:, lc * D:(lc + 1) * D], in_=wuk_d[lc * 128:(lc + 1) * 128, :],
               i=(s_w, 16))
        p.gdma(out=wuv[:, lc * D:(lc + 1) * D], in_=wuv_d[lc * 128:(lc + 1) * 128, :],
               i=(s_w, 16))
    p.gdma(out=e2s[:], in_=e2sel, i=(s_w, 16))
    wload_total = s_w.n
    emit_env_init(p, env)

    exp_bank = {0: 0, 1: 0}
    pv_bank = {0: 0, 1: 0, 2: 0, 3: 0}
    dec_bank = {0: 0, 1: 0}

    for b in range(NB):
        for c in range(KC):
            p.gdma(out=attn[:, c * D:(c + 1) * D],
                   in_=wk_d[c * 128:(c + 1) * 128, :],
                   w=[(s_t, s_t.n), (s_v, s_v.n)], i=(s_w, 16))
            p.gdma(out=attn[:, KC * D + c * D:KC * D + (c + 1) * D],
                   in_=wv_d[c * 128:(c + 1) * 128, :], i=(s_w, 16))
        for q4 in range(4):
            p.gdma(out=R[:, q4 * 7 * 512:(q4 + 1) * 7 * 512],
                   in_=dec_d[:, q4 * 7 * 512:(q4 + 1) * 7 * 512],
                   w=[(s_t, s_t.n)], i=(s_w, 16))
        vw_loaded = s_w.n

        for blk in range(4):
            emit_ln_block(p, env, x,
                          lambda c, blk=blk: xln[:, c * SEQ + blk * 512:
                                                 c * SEQ + blk * 512 + 512],
                          b, blk)

        p.wait("tensor", s_w, vw_loaded)
        evrel01 = [s_v.n]
        for blk in range(4):
            for c in range(KC):
                for (woff, bank) in ((0, 0), (KC * D, 1)):
                    for cc in range(KC):
                        p.mm(ps_o[:, bank * 512: bank * 512 + 512],
                             attn[:, woff + cc * D + c * 128: woff + cc * D + (c + 1) * 128],
                             xln[:, cc * SEQ + blk * 512: cc * SEQ + (blk + 1) * 512],
                             start=(cc == 0), stop=(cc == KC - 1),
                             w=([(s_v, evrel01[0])] if cc == 0 else ()),
                             i=((s_t, 1) if cc == KC - 1 else None))
                p.vcopy(out=env["lnscr"][:], in_=ps_o[:, 0:512],
                        w=[(s_t, s_t.n)], i=(s_v, 1))
                p.tt(vtm[:, c * SEQ + blk * 512: c * SEQ + (blk + 1) * 512],
                     env["lnscr"][:], ps_o[:, 512:1024], AluOpType.add,
                     i=(s_v, 1))
                evrel01[0] = s_v.n
        evrelA = {0: s_v.n, 1: s_v.n}
        for blk in range(4):
            for lc in range(2):
                bank = lc % 2
                for c in range(KC):
                    p.mm(ps_a[:, bank * 512: bank * 512 + 512],
                         wdn[:, c * 256 + lc * 128: c * 256 + (lc + 1) * 128],
                         vtm[:, c * SEQ + blk * 512: c * SEQ + (blk + 1) * 512],
                         start=(c == 0), stop=(c == KC - 1),
                         w=([(s_v, evrelA[bank])] if c == 0 else ()),
                         i=((s_t, 1) if c == KC - 1 else None))
                p.vcopy(out=latent[:, lc * SEQ + blk * 512: lc * SEQ + (blk + 1) * 512],
                        in_=ps_a[:, bank * 512: bank * 512 + 512],
                        w=[(s_t, s_t.n)], i=(s_v, 1))
                evrelA[bank] = s_v.n
        evrelB = {0: s_v.n, 1: s_v.n}
        for tc in range(16):
            for half in range(2):
                bank = half % 2
                for lc in range(2):
                    p.mm(ps_o[:, bank * 512: bank * 512 + 512],
                         latent[:, lc * SEQ + tc * 128: lc * SEQ + (tc + 1) * 128],
                         wuv[:, lc * D + half * 512: lc * D + (half + 1) * 512],
                         start=(lc == 0), stop=(lc == 1),
                         w=([(s_v, evrelB[bank])] if lc == 0 else ()),
                         i=((s_t, 1) if lc == 1 else None))
                p.vcopy(out=vtm[:, tc * D + half * 512: tc * D + (half + 1) * 512],
                        in_=ps_o[:, bank * 512: bank * 512 + 512],
                        w=[(s_t, s_t.n)], i=(s_v, 1))
                evrelB[bank] = s_v.n

        for hp in range(8):
            evrelC = {0: s_v.n, 1: s_v.n}
            for qb in range(4):
                bank = qb % 2
                for c in range(KC):
                    p.mm(ps_o[:, bank * 512: bank * 512 + 512],
                         wq[:, c * D + hp * 128: c * D + (hp + 1) * 128],
                         xln[:, c * SEQ + qb * 512: c * SEQ + (qb + 1) * 512],
                         start=(c == 0), stop=(c == KC - 1),
                         w=([(s_v, evrelC[bank])] if c == 0 else ()),
                         i=((s_t, 1) if c == KC - 1 else None))
                p.vcopy(out=qh[:, qb * 512:(qb + 1) * 512],
                        in_=ps_o[:, bank * 512: bank * 512 + 512],
                        w=[(s_t, s_t.n)], i=(s_v, 1))
                evrelC[bank] = s_v.n
            for qb in range(4):
                bank = qb % 2
                for lc in range(2):
                    p.mm(ps_o[:, bank * 512: bank * 512 + 512],
                         wuk[:, lc * D + hp * 128: lc * D + (hp + 1) * 128],
                         latent[:, lc * SEQ + qb * 512: lc * SEQ + (qb + 1) * 512],
                         start=(lc == 0), stop=(lc == 1),
                         w=([(s_v, evrelC[bank])] if lc == 0 else ()),
                         i=((s_t, 1) if lc == 1 else None))
                p.vcopy(out=kh[:, qb * 512:(qb + 1) * 512],
                        in_=ps_o[:, bank * 512: bank * 512 + 512],
                        w=[(s_t, s_t.n)], i=(s_v, 1))
                evrelC[bank] = s_v.n

            for (src, col) in ((qh, 0), (kh, 1)):
                p.act(out=sqs[:], in_=src[:], func=AF.Square,
                      w=[(s_v, s_v.n)], i=(s_s, 1))
                for j in range(2):
                    slot = 8 + 2 * col + j
                    for qb in range(4):
                        p.mm(ps_a[0:1, (qb % 2) * 512:(qb % 2) * 512 + 512],
                             e2s[:, j:j + 1], sqs[:, qb * 512:(qb + 1) * 512],
                             start=True, stop=True,
                             w=[(s_s, s_s.n), (s_v, s_v.n)], i=(s_t, 1))
                        p.vredmax(env["small"][0:1, slot:slot + 1]
                                  if qb == 0 else env["small"][0:1, 12:13],
                                  ps_a[0:1, (qb % 2) * 512:(qb % 2) * 512 + 512],
                                  w=[(s_t, s_t.n)], i=(s_v, 1))
                        if qb > 0:
                            p.tt(env["small"][0:1, slot:slot + 1],
                                 env["small"][0:1, slot:slot + 1],
                                 env["small"][0:1, 12:13], AluOpType.max, i=(s_v, 1))
            for j in range(2):
                pj = env["small"][0:1, 0 + j:1 + j]
                p.tt(pj, env["small"][0:1, 8 + j:9 + j],
                     env["small"][0:1, 10 + j:11 + j], AluOpType.mult,
                     w=[(s_v, s_v.n)], i=(s_v, 1))
                p.act(out=pj, in_=pj, func=AF.Ln, w=[(s_v, s_v.n)], i=(s_s, 1))
                p.act(out=pj, in_=pj, func=AF.Exp, scale=0.5, i=(s_s, 1))
                p.ts(pj, pj, -1.0 / 32, w=[(s_s, s_s.n)], i=(s_v, 1))
                p.mm(ps_c[:, 0:1], env["ones1f"][:], pj, w=[(s_v, s_v.n)], i=(s_t, 1))
                p.vcopy(negb[:, j:j + 1], ps_c[:, 0:1], w=[(s_t, s_t.n)], i=(s_v, 1))

            for j in range(2):
                hg = hp * 2 + j
                p.vmemset(vplus[:], 1.0, w=[(s_t, s_t.n)], i=(s_v, 1))
                for tc in range(16):
                    p.vcopy(vplus[:, tc * 65: tc * 65 + 64],
                            vtm[:, tc * D + hg * 64: tc * D + (hg + 1) * 64],
                            i=(s_v, 1))
                vplus_ready = s_v.n
                for qc in range(4):
                    pend = []

                    def emit_pv(kt, slot, wv3, qc=qc):
                        p.mm(ps_o[0:65, (qc % 2) * 512:(qc % 2) * 512 + 512],
                             vplus[:, kt * 65:(kt + 1) * 65],
                             ptb[:, slot * 512:(slot + 1) * 512],
                             start=(kt == 0), stop=(kt == 15),
                             w=wv3, i=(s_t, 1))
                        pv_bank[slot] = s_t.n

                    for kt in range(16):
                        par = kt % 2
                        slot = kt % 4
                        wv = [(s_v, max(vplus_ready, dec_bank[par]))]
                        p.mm(ps_a[:, par * 512:(par + 1) * 512],
                             kh[j * 64:(j + 1) * 64, kt * 128:(kt + 1) * 128],
                             qh[j * 64:(j + 1) * 64, qc * 512:(qc + 1) * 512],
                             w=wv, i=(s_t, 1))
                        smm_done = s_t.n
                        p.tt(sdec[:, par * 512:(par + 1) * 512],
                             ps_a[:, par * 512:(par + 1) * 512],
                             R[:, (kt - 4 * qc + 12) * 512:(kt - 4 * qc + 13) * 512],
                             AluOpType.mult,
                             w=[(s_t, smm_done)] + ([(s_s, exp_bank[par])] if exp_bank[par] else []),
                             i=(s_v, 1))
                        dec_bank[par] = s_v.n
                        dec_done = s_v.n
                        wv2 = [(s_v, dec_done)]
                        if pv_bank[slot]:
                            wv2.append((s_t, pv_bank[slot]))
                        p.act(out=ptb[:, slot * 512:(slot + 1) * 512],
                              in_=sdec[:, par * 512:(par + 1) * 512],
                              func=AF.Exp, scale=1.0 / 32, bias=negb[:, j:j + 1],
                              w=wv2, i=(s_s, 1))
                        exp_bank[par] = s_s.n
                        wv3 = [(s_s, s_s.n)]
                        if kt == 0:
                            wv3.append((s_v, s_v.n))
                        pend.append((kt, slot, wv3))
                        if len(pend) > 2:
                            for a in pend[:-2]:
                                emit_pv(*a)
                            pend = pend[-2:]
                    for a in pend:
                        emit_pv(*a)
                    p.vrecip(rden[:], ps_o[64:65, (qc % 2) * 512:(qc % 2) * 512 + 512],
                             w=[(s_t, s_t.n)], i=(s_v, 1))
                    p.mm(ps_c[0:64, 0:512], env["ones1f"][0:1, 0:64], rden[:],
                         w=[(s_v, s_v.n)], i=(s_t, 1))
                    p.scopy(out=bcast[:], in_=ps_c[0:64, 0:512],
                            w=[(s_t, s_t.n)], i=(s_s, 1))
                    p.tt(attn[(hg % 2) * 64:(hg % 2) * 64 + 64,
                              (hg // 2) * SEQ + qc * 512:(hg // 2) * SEQ + (qc + 1) * 512],
                         ps_o[0:64, (qc % 2) * 512:(qc % 2) * 512 + 512],
                         bcast[:], AluOpType.mult,
                         w=[(s_s, s_s.n)], i=(s_v, 1))

        for c in range(KC):
            p.gdma(out=R[:, c * D:(c + 1) * D], in_=wo_d[c * 128:(c + 1) * 128, :],
                   w=[(s_s, s_s.n), (s_v, s_v.n)], i=(s_w, 16))
        wo_loaded = s_w.n
        p.wait("tensor", s_w, wo_loaded)
        for blk in range(4):
            tof = b * SEQ + blk * 512
            p.gdma(out=env["xbuf"][:].rearrange("p (c t) -> p c t", c=KC),
                   in_=x[:, tof:tof + 512].rearrange("(c p) t -> p c t", p=128),
                   w=[(s_v, s_v.n), (s_st, s_st.n)], i=(s_x, 16))
            store_done = {}
            evrelD = {0: s_v.n, 1: s_v.n}
            for oc in range(KC):
                bank = oc % 2
                for c in range(KC):
                    p.mm(ps_o[:, bank * 512: bank * 512 + 512],
                         R[:, c * D + oc * 128: c * D + (oc + 1) * 128],
                         attn[:, c * SEQ + blk * 512: c * SEQ + (blk + 1) * 512],
                         start=(c == 0), stop=(c == KC - 1),
                         w=([(s_v, evrelD[bank])] if c == 0 else ()),
                         i=((s_t, 1) if c == KC - 1 else None))
                wv = [(s_t, s_t.n), (s_x, s_x.n)]
                if oc - 2 in store_done:
                    wv.append((s_st, store_done[oc - 2]))
                oslot = latent[:, bank * 1024:(bank + 1) * 1024].bitcast(F32)
                p.tt(oslot,
                     ps_o[:, bank * 512: bank * 512 + 512],
                     env["xbuf"][:, oc * 512:(oc + 1) * 512], AluOpType.add,
                     w=wv, i=(s_v, 1))
                evrelD[bank] = s_v.n
                p.sydma(out=xn[oc * 128:(oc + 1) * 128, tof:tof + 512],
                        in_=oslot,
                        w=[(s_v, s_v.n)], i=(s_st, 16))
                store_done[oc] = s_st.n

    p.wait("sync", s_st, s_st.n)
    p.scope_exit()
    return p



def build_ffn_l1(mode="full", P=None, env=None, xap=None, xnap=None, nm=None):
    p = P if P is not None else Prog()
    nm = nm or {}
    g = lambda n: nm.get(n, n)
    if xap is None:
        x = p.dram("x", [D, T], F32, kind="ExternalInput").ap()
        xn = p.dram("xn", [D, T], F32, kind="ExternalOutput").ap()
    else:
        x, xn = xap, xnap
    wout = p.dram("wout", [8, T], BF16, kind="ExternalOutput").ap() if mode in ("wdump", "route", "routemin") else None
    aout = p.dram("aout", [128, 32 * 16], F32, kind="ExternalOutput").ap() if mode in ("wdump", "route", "routemin") else None
    adbg = p.sbuf("adbg", [128, 32 * 16], F32) if mode in ("wdump", "route", "routemin") else None
    mout = p.dram("mout", [128, 1024 + 64 + 8], BF16, kind="ExternalOutput").ap() if mode in ("wdump", "route", "routemin") else None
    dout = p.dram("dscr", [128, 16], F32) if mode in ("wdump", "route", "routemin") else None
    dout2 = (p.dram("dout", [128, 16], F32, kind="ExternalOutput").ap()
             if mode in ("wdump", "route", "routemin") else None)
    dout3 = (p.dram("dout3", [128, 8], BF16, kind="ExternalOutput").ap()
             if mode in ("wdump", "route", "routemin") else None)
    dout4 = (p.dram("dout4", [8, 512], BF16, kind="ExternalOutput").ap()
             if mode in ("wdump", "route", "routemin") else None)
    mdbg = p.sbuf("mdbg", [128, 1024 + 64 + 8], BF16) if mode in ("wdump", "route", "routemin") else None
    rwT_d = p.dram("routing_WT", [D, 8], F32, kind="ExternalInput").ap()
    eb_d = p.dram("expert_bias128", [128, 8], F32, kind="ExternalInput").ap()
    sw1_d = p.dram(g("sW1"), [D, 512], F32, kind="ExternalInput").ap()
    sb1_d = p.dram("sb1T", [128, 4], F32, kind="ExternalInput").ap()
    sw2_d = p.dram(g("sW2"), [512, D], F32, kind="ExternalInput").ap()
    sb2_d = p.dram("sb2T", [128, 8], F32, kind="ExternalInput").ap()
    rw1_d = p.dram(g("rW1"), [8, D, 512], F32, kind="ExternalInput").ap()
    rb1_d = p.dram("rb1T", [128, 32], F32, kind="ExternalInput").ap()
    rw2_d = p.dram(g("rW2"), [8, 512, D], F32, kind="ExternalInput").ap()
    rb2_d = p.dram("rb2T", [128, 64], F32, kind="ExternalInput").ap()
    id_d = p.dram("ident", [128, 128], F32, kind="ExternalInput").ap()
    wdram = p.dram("wdram", [8, T], BF16)

    if env is None:
        env = make_env(p)
    s_x, s_v, s_s, s_t, s_st, s_w = (env[k] for k in
                                     ("s_x", "s_v", "s_s", "s_t", "s_st", "s_w"))
    s_ac = env["s_ac"]
    s_wr = env["s_wr"]
    p.scope_enter()

    xln = p.sbuf("xln", [128, KC * T], BF16)
    sw1b = p.sbuf("sw1b", [128, KC * 512], BF16)
    sw2b = p.sbuf("sw2b", [128, 4 * D], BF16)
    rw1b = p.sbuf("rw1b", [128, 2 * KC * 512], BF16)
    rw2b = p.sbuf("rw2b", [128, 2 * 4 * D], BF16)
    rwT = p.sbuf("rwT", [128, KC * 8], BF16)
    hb = p.sbuf("hb", [128, 4 * 512], BF16)
    wfm = p.sbuf("wfm", [8, T], BF16)
    identb = p.sbuf("identb", [128, 128], BF16)
    sb1 = p.sbuf("sb1", [128, 4], F32)
    sb2 = p.sbuf("sb2", [128, 8], F32)
    rb1 = p.sbuf("rb1", [128, 32], F32)
    rb2 = p.sbuf("rb2", [128, 64], F32)
    ebias = p.sbuf("ebias", [128, 8], F32)
    onecol = p.sbuf("onecol", [128, 1], F32)
    lg8 = p.sbuf("lg8", [128, 48], F32)
    t_lg = p.sbuf("t_lg", [128, 8], F32)
    t_a = p.sbuf("t_a", [128, 8], F32)
    t_eq = p.sbuf("t_eq", [128, 8], F32)
    t_am = p.sbuf("t_am", [128, 8], F32)
    t_m1 = p.sbuf("t_m1", [128, 1], F32)
    t_m2 = p.sbuf("t_m2", [128, 1], F32)
    t_sc = p.sbuf("t_sc", [128, 8], F32)
    t_sc2 = p.sbuf("t_sc2", [128, 8], F32)
    wtm8 = p.sbuf("wtm8", [128, 8], BF16)
    wrow = p.sbuf("wrow", [1, 2 * 512], BF16)
    wtile = env["xbuf"][:, 0:512]
    yscs = [env["xbuf"][:, 512 + i * 512: 1024 + i * 512] for i in range(4)]

    ps_a, ps_o, ps_c, ps_bc = env["ps_a"], env["ps_o"], env["ps_c"], env["ps_bc"]
    ps_t = p.psum("ps_t", [8, 128], BF16)

    for c in range(KC):
        p.gdma(out=sw1b[:, c * 512:(c + 1) * 512], in_=sw1_d[c * 128:(c + 1) * 128, :],
               i=(s_w, 16))
        p.gdma(out=rwT[:, c * 8:(c + 1) * 8], in_=rwT_d[c * 128:(c + 1) * 128, :],
               i=(s_w, 16))
    for fc in range(4):
        p.gdma(out=sw2b[:, fc * D:(fc + 1) * D], in_=sw2_d[fc * 128:(fc + 1) * 128, :],
               i=(s_w, 16))
    for (dst, srcd) in ((sb1, sb1_d), (sb2, sb2_d), (rb1, rb1_d), (rb2, rb2_d)):
        p.gdma(out=dst[:], in_=srcd, i=(s_w, 16))
    p.gdma(out=identb[:], in_=id_d, i=(s_w, 16))
    p.gdma(out=xn, in_=x, i=(s_ac, 16))
    xn_init = s_ac.n
    p.gdma(out=ebias[:], in_=eb_d, i=(s_w, 16))
    wload_total = s_w.n
    emit_env_init(p, env)
    p.vmemset(onecol[:], 1.0, i=(s_v, 1))
    p.wait("vector", s_w, wload_total)

    NGB = 1 if mode == "routemin" else 8
    tr_done = 0
    for gblk in range(NGB):
        b, blk = divmod(gblk, 4)

        emit_ln_block(p, env, x,
                      lambda c, gblk=gblk: xln[:, c * T + gblk * 512:
                                               c * T + gblk * 512 + 512],
                      b, blk)
        lgbank = [ps_c[:, 0:8], ps_a[:, 0:8], ps_a[:, 512:520], ps_bc[:, 0:8]]
        for s4 in range(4):
            for c in range(KC):
                p.mm(lgbank[s4],
                     xln[:, c * T + gblk * 512 + s4 * 128: c * T + gblk * 512 + (s4 + 1) * 128],
                     rwT[:, c * 8:(c + 1) * 8],
                     start=(c == 0), stop=(c == KC - 1),
                     w=([(s_v, s_v.n), (s_s, s_s.n)] if c == 0 else ()),
                     i=((s_t, 1) if c == KC - 1 else None))
        if mdbg is not None and gblk == 0:
            for c in range(KC):
                p.vcopy(mdbg[:, c * 128:(c + 1) * 128],
                        xln[:, c * T: c * T + 128], w=[(s_t, s_t.n)], i=(s_v, 1))
            p.vcopy(mdbg[:, 1024:1088], rwT[:, 0:64], i=(s_v, 1))
            p.vcopy(mdbg[:, 1088:1096], ps_c[:, 0:8], i=(s_v, 1))
            p.vcopy(mdbg[:, 1060:1068], ebias[:], i=(s_v, 1))
        for s4 in range(4 if mode != "routemin" else 1):
            lg = t_lg[:]
            a = t_a[:]
            eq = t_eq[:]
            am = t_am[:]
            m1 = t_m1[:]
            m2 = t_m2[:]
            sc1 = t_sc[:]
            def VS(fn):
                fn()
                p.wait("vector", s_v, s_v.n)
                p.wait("scalar", s_s, s_s.n)
                p.wait("vector", s_s, s_s.n)
                p.wait("scalar", s_v, s_v.n)
            VS(lambda: p.tt(lg, lgbank[s4], ebias[:], AluOpType.add,
                            w=[(s_t, s_t.n), (s_s, s_s.n)], i=(s_v, 1)))
            VS(lambda: p.act(out=sc1, in_=lg, func=AF.Abs, i=(s_s, 1)))
            VS(lambda: p.act(out=t_sc2[:], in_=sc1, func=AF.Exp, scale=-1.0, i=(s_s, 1)))
            VS(lambda: p.act(out=sc1, in_=t_sc2[:], func=AF.Ln, bias=onecol[:], i=(s_s, 1)))
            VS(lambda: p.ts(a, lg, 0.0, op0=AluOpType.max, i=(s_v, 1)))
            VS(lambda: p.tt(am, a, sc1, AluOpType.add, i=(s_v, 1)))
            VS(lambda: p.act(out=t_sc2[:], in_=am, func=AF.Ln, i=(s_s, 1)))
            VS(lambda: p.act(out=a, in_=t_sc2[:], func=AF.Exp, scale=0.5, i=(s_s, 1)))
            VS(lambda: p.vredmax(m1, a, i=(s_v, 1)))
            VS(lambda: p.ts(eq, a, m1, op0=AluOpType.is_equal, i=(s_v, 1)))
            VS(lambda: p.ts(am, eq, 1e30, op0=AluOpType.mult, i=(s_v, 1)))
            VS(lambda: p.tt(am, a, am, AluOpType.subtract, i=(s_v, 1)))
            VS(lambda: p.vredmax(m2, am, i=(s_v, 1)))
            VS(lambda: p.ts(eq, a, m2, op0=AluOpType.is_ge, i=(s_v, 1)))
            VS(lambda: p.tt(wtm8[:], a, eq, AluOpType.mult, i=(s_v, 1)))
            if adbg is not None:
                p.vcopy(adbg[:, (gblk * 4 + s4) * 16 + 8:(gblk * 4 + s4) * 16 + 16], a,
                        i=(s_v, 1))
                if gblk == 1 and s4 == 0:
                    p.vcopy(adbg[:, 100 * 4:100 * 4 + 8], eq, i=(s_v, 1))
                    p.vcopy(adbg[:, 103 * 4:103 * 4 + 8], am, i=(s_v, 1))
                    p.vcopy(adbg[:, 106 * 4:106 * 4 + 1], m1, i=(s_v, 1))
                    p.vcopy(adbg[:, 107 * 4:107 * 4 + 1], m2, i=(s_v, 1))
            if mode == "routemin":
                continue
            p.transpose(ps_t[:], wtm8[:], identb[:],
                        w=[(s_v, s_v.n), (s_v, tr_done)], i=(s_t, 1))
            p.vcopy(wfm[0:8, gblk * 512 + s4 * 128: gblk * 512 + (s4 + 1) * 128],
                    ps_t[:], w=[(s_t, s_t.n)], i=(s_v, 1))
            tr_done = s_v.n
    if mode != "routemin":
        p.gdma(out=wdram.ap(), in_=wfm[:], w=[(s_v, s_v.n)], i=(s_w, 16))
    wdram_ready = s_w.n

    ysc_dma = {k: 0 for k in range(4)}
    ysc_ts = {0: 0, 1: 0}
    sil_done = {0: 0, 1: 0}
    hread = [0]
    wr_ld = {}

    def load_wrow1(wexp, tb):
        p.gdma(out=wrow[0:1, (tb % 2) * 512:(tb % 2 + 1) * 512],
               in_=wdram.ap()[wexp:wexp + 1, tb * 512:(tb + 1) * 512],
               w=[(s_w, wdram_ready), (s_t, s_t.n)], i=(s_wr, 16))
        return s_wr.n

    def ffn_pass(widx1, widx2, b1ap, b2ap, wexp):
        if wexp is not None:
            wr_ld[0] = load_wrow1(wexp, 0)
        for tb in range(8):
            if wexp is not None:
                par = tb % 2
                if tb + 1 < 8:
                    wr_ld[tb + 1] = load_wrow1(wexp, tb + 1)
                p.mm(ps_bc[:, 0:512], env["ones1b"][:],
                     wrow[0:1, par * 512:(par + 1) * 512],
                     w=[(s_wr, wr_ld[tb]), (s_s, s_s.n), (s_v, s_v.n)], i=(s_t, 1))
                p.scopy(out=wtile, in_=ps_bc[:, 0:512], w=[(s_t, s_t.n)], i=(s_s, 1))
            for fs in range(4):
                bank = fs % 2
                for c in range(KC):
                    wvh = []
                    if c == 0:
                        if sil_done[bank]:
                            wvh.append((s_s, sil_done[bank]))
                        if fs == 0:
                            wvh.append((s_s, s_s.n))
                    p.mm(ps_a[:, bank * 512: bank * 512 + 512],
                         widx1[:, c * 512 + fs * 128: c * 512 + (fs + 1) * 128],
                         xln[:, c * T + tb * 512: c * T + (tb + 1) * 512],
                         start=(c == 0), stop=(c == KC - 1),
                         w=wvh, i=((s_t, 1) if c == KC - 1 else None))
                wvg = [(s_t, s_t.n)]
                if fs == 0 and hread[0]:
                    wvg.append((s_t, hread[0]))
                p.act(out=hb[:, fs * 512:(fs + 1) * 512],
                      in_=ps_a[:, bank * 512: bank * 512 + 512],
                      func=AF.Silu, bias=b1ap[:, fs:fs + 1],
                      w=wvg, i=(s_s, 1))
                sil_done[bank] = s_s.n
            for oc in range(KC):
                bank = oc % 2
                for fc in range(4):
                    wvy = []
                    if fc == 0:
                        wvy.append((s_s, s_s.n))
                        if ysc_ts[bank]:
                            wvy.append((s_v, ysc_ts[bank]))
                    p.mm(ps_o[:, bank * 512: bank * 512 + 512],
                         widx2[:, fc * D + oc * 128: fc * D + (oc + 1) * 128],
                         hb[:, fc * 512:(fc + 1) * 512],
                         start=(fc == 0), stop=(fc == 3),
                         w=wvy, i=((s_t, 1) if fc == 3 else None))
                ysc = yscs[oc % 4]
                p.ts(ysc, ps_o[:, bank * 512: bank * 512 + 512], b2ap[:, oc:oc + 1],
                     op0=AluOpType.add,
                     w=[(s_t, s_t.n), (s_ac, max(xn_init, ysc_dma[oc % 4]))], i=(s_v, 1))
                ysc_ts[bank] = s_v.n
                if wexp is not None:
                    p.tt(ysc, ysc, wtile, AluOpType.mult, i=(s_v, 1))
                wv = [(s_v, s_v.n)]
                tof = tb * 512
                p.gdma(out=xn[oc * 128:(oc + 1) * 128, tof:tof + 512],
                       in_=ysc, accum_op=AluOpType.add, w=wv, i=(s_ac, 16))
                ysc_dma[oc % 4] = s_ac.n
            hread[0] = s_t.n

    p.wait("tensor", s_w, wload_total)
    if mode not in ("route", "routemin"):
        ffn_pass(sw1b, sw2b, sb1, sb2, None)

    if mode in ("wdump", "route", "routemin"):
        if mode != "routemin":
            p.gdma(out=wout, in_=wdram.ap(), w=[(s_w, wdram_ready)], i=(s_ac, 16))
        else:
            p.gdma(out=wout, in_=wfm[:], w=[(s_v, s_v.n)], i=(s_ac, 16))
        p.gdma(out=aout, in_=adbg[:], w=[(s_v, s_v.n)], i=(s_ac, 16))
        p.gdma(out=mout, in_=mdbg[:], i=(s_ac, 16))
        p.gdma(out=dout.ap()[0:128, 0:8], in_=t_a[:], i=(s_ac, 16))
        p.gdma(out=dout.ap()[0:128, 8:16], in_=t_eq[:], i=(s_ac, 16))
        p.gdma(out=dout2, in_=dout.ap(), w=[(s_ac, s_ac.n)], i=(s_ac, 16))
        p.gdma(out=dout3, in_=wtm8[:], i=(s_ac, 16))
        p.gdma(out=dout4, in_=wfm[0:8, 0:512], i=(s_ac, 16))
    wpar_done = {0: 0, 1: 0}
    for e in range(8) if mode in ("full", "wdump") else []:
        par = e % 2
        wv = [(s_t, wpar_done[par])] if wpar_done[par] else []
        for c in range(KC):
            p.gdma(out=rw1b[:, par * KC * 512 + c * 512: par * KC * 512 + (c + 1) * 512],
                   in_=rw1_d[e, c * 128:(c + 1) * 128, :],
                   w=(wv if c == 0 else ()), i=(s_w, 16))
        for fc in range(4):
            p.gdma(out=rw2b[:, par * 4 * D + fc * D: par * 4 * D + (fc + 1) * D],
                   in_=rw2_d[e, fc * 128:(fc + 1) * 128, :], i=(s_w, 16))
        p.wait("tensor", s_w, s_w.n)
        ffn_pass(rw1b[:, par * KC * 512:(par + 1) * KC * 512],
                 rw2b[:, par * 4 * D:(par + 1) * 4 * D],
                 rb1[:, e * 4:(e + 1) * 4], rb2[:, e * 8:(e + 1) * 8], e)
        wpar_done[par] = s_t.n

    p.wait("gpsimd", s_ac, s_ac.n)
    p.scope_exit()
    return p


def run_ffn_l1(x_shards, routing_W, expert_bias, sW1, sb1, sW2, sb2, rW1, rb1, rW2, rb2,
               mode="full"):
    nc = _get_prog("ffn_l1_" + mode, lambda: build_ffn_l1(mode))
    rwT = np.ascontiguousarray(np.asarray(routing_W, np.float32).T)
    sb1T = np.ascontiguousarray(np.asarray(sb1, np.float32).reshape(4, 128).T)
    sb2T = np.ascontiguousarray(np.asarray(sb2, np.float32).reshape(8, 128).T)
    rb1T = np.ascontiguousarray(np.asarray(rb1, np.float32).reshape(8, 4, 128)
                                .transpose(2, 0, 1).reshape(128, 32))
    rb2T = np.ascontiguousarray(np.asarray(rb2, np.float32).reshape(8, 8, 128)
                                .transpose(2, 0, 1).reshape(128, 64))
    ident = np.eye(128, dtype=np.float32)
    in_maps = []
    for c in range(NCORES):
        m = {"x": x_shards[c], "routing_WT": rwT,
             "expert_bias128": np.ascontiguousarray(
                 np.broadcast_to(np.asarray(expert_bias, np.float32).reshape(1, 8), (128, 8))),
             "sW1": np.asarray(sW1, np.float32), "sb1T": sb1T,
             "sW2": np.asarray(sW2, np.float32), "sb2T": sb2T,
             "rW1": np.asarray(rW1, np.float32), "rb1T": rb1T,
             "rW2": np.asarray(rW2, np.float32), "rb2T": rb2T,
             "ident": ident}
        in_maps.append(m)
    res = run_bass_kernel_spmd(nc, in_maps, core_ids=list(range(NCORES)))
    if mode in ("wdump", "route", "routemin"):
        return ([np.asarray(r["xn"]) for r in res.results],
                [(np.asarray(r["wout"]), np.asarray(r["aout"]), np.asarray(r["mout"]),
                  np.asarray(r["dout"]), np.asarray(r["dout3"]),
                  np.asarray(r["dout4"])) for r in res.results])
    return [np.asarray(r["xn"]) for r in res.results]


def _host_consts_attn_a():
    lam = 0.01
    dec = np.zeros((128, 28 * 512), np.float32)
    p_i = np.arange(128)[:, None]
    f_i = np.arange(512)[None, :]
    for im, m in enumerate(range(-12, 16)):
        dec[:, im * 512:(im + 1) * 512] = np.exp(-lam * np.abs(m * 128 + p_i - f_i))
    return dec


def run_attn_a_l1(x_shards, Wq, Wk, Wv, Wo, Wdn, Wuk, Wuv):
    consts = _host_consts_attn()
    dec28 = _host_consts_attn_a()
    nc = _get_prog("attn_a_l1", lambda: build_attn_a_l1())
    in_maps = []
    for c in range(NCORES):
        m = {"x": x_shards[c], "W_q": Wq, "W_k": Wk, "W_v": Wv, "W_o": Wo,
             "W_down_kv": Wdn, "W_up_k": Wuk, "W_up_v": Wuv,
             "e2sel": consts["e2sel"], "dec28": dec28}
        in_maps.append(m)
    res = run_bass_kernel_spmd(nc, in_maps, core_ids=list(range(NCORES)))
    return [np.asarray(r["xn"]) for r in res.results]


def _host_consts_attn():
    inv = 1.0 / (10000.0 ** (np.arange(0, 64, 2, dtype=np.float64) / 64))
    t = np.arange(SEQ, dtype=np.float64)
    fr = np.outer(t, inv)
    cos = np.cos(fr).astype(np.float32)
    sin = np.sin(fr).astype(np.float32)
    cos64 = np.concatenate([cos.T, cos.T], 0)
    sin64 = np.concatenate([sin.T, sin.T], 0)
    cos128 = np.concatenate([cos64, cos64], 0)
    sin128 = np.concatenate([sin64, sin64], 0)
    e2sel = np.zeros((128, 2), np.float32)
    e2sel[0:64, 0] = 1
    e2sel[64:128, 1] = 1
    cmask = np.zeros((128, 4 * 512), np.float32)
    for r in range(4):
        ki = np.arange(128)[:, None] + r * 128
        qi = np.arange(512)[None, :]
        cmask[:, r * 512:(r + 1) * 512] = (ki <= qi).astype(np.float32)
    return dict(cos_t=np.ascontiguousarray(cos128), sin_t=np.ascontiguousarray(sin128),
                e2sel=e2sel, cmask=cmask)



_PROG_CACHE = {}


def _get_prog(name, builder):
    if name not in _PROG_CACHE:
        p = builder()
        _PROG_CACHE[name] = p.build()
    return _PROG_CACHE[name]


def _shard_x(full):
    out = []
    for c in range(NCORES):
        xc = np.asarray(full[2 * c:2 * c + 2], np.float32)
        out.append(np.ascontiguousarray(xc.transpose(2, 0, 1).reshape(D, T)))
    return out


def _unshard_x(shards):
    full = np.empty((16, SEQ, D), np.float32)
    for c in range(NCORES):
        full[2 * c:2 * c + 2] = np.asarray(shards[c]).reshape(D, NB, SEQ).transpose(1, 2, 0)
    return full


def run_attn_b_l0(x_shards, w_qkv, w_o):
    consts = _host_consts_attn()
    nc = _get_prog("attn_b_l0", lambda: build_attn_b("l0"))
    in_maps = []
    for c in range(NCORES):
        m = {"x": x_shards[c], "w_qkv": w_qkv, "w_o": w_o,
             "cos_t": consts["cos_t"], "sin_t": consts["sin_t"],
             "e2sel": consts["e2sel"]}
        in_maps.append(m)
    res = run_bass_kernel_spmd(nc, in_maps, core_ids=list(range(NCORES)))
    return [np.asarray(r["xn"]) for r in res.results]


def run_attn_b_l1(x_shards, Wq, Wk, Wv, Wo, qnw, knw):
    consts = _host_consts_attn()
    nc = _get_prog("attn_b_l1", lambda: build_attn_b("l1"))
    in_maps = []
    for c in range(NCORES):
        m = {"x": x_shards[c], "Wq": Wq, "Wk": Wk, "Wv": Wv, "Wo": Wo,
             "q_norm_w": np.asarray(qnw, np.float32).reshape(1, 64),
             "k_norm_w": np.asarray(knw, np.float32).reshape(1, 64)}
        m.update(consts)
        in_maps.append(m)
    res = run_bass_kernel_spmd(nc, in_maps, core_ids=list(range(NCORES)))
    return [np.asarray(r["xn"]) for r in res.results]


def build_mega():
    p = Prog()
    x0 = p.dram("x0", [D, T], F32, kind="ExternalInput").ap()
    xn = p.dram("xn", [D, T], F32, kind="ExternalOutput").ap()
    mids = [p.dram("xmid", [D, T], F32).ap() for _ in range(7)]
    env = make_env(p)
    chain = [x0] + mids + [xn]
    specs = [
        (build_attn_a_l0, dict(nm={"q_w": "L0_attn_a_q_w", "k_w": "L0_attn_a_k_w",
                                   "v_w": "L0_attn_a_v_w", "out_w": "L0_attn_a_out_w"})),
        (build_ffn_l0, dict(nm={"Wg": "L0_ffn_Wg", "W1": "L0_ffn_W1",
                                "W2": "L0_ffn_W2"})),
        (lambda **kw: build_attn_b("l0", **kw),
         dict(nm={"w_qkv": "L0_attn_b_w_qkv", "w_o": "L0_attn_b_w_o"})),
        (build_ffn_l0, dict(nm={"Wg": "L0_ffn_Wg", "W1": "L0_ffn_W1",
                                "W2": "L0_ffn_W2"})),
        (build_attn_a_l1, dict(nm={"W_q": "L1_attn_a_W_q", "W_k": "L1_attn_a_W_k",
                                   "W_v": "L1_attn_a_W_v", "W_o": "L1_attn_a_W_o",
                                   "W_down_kv": "L1_attn_a_W_down_kv",
                                   "W_up_k": "L1_attn_a_W_up_k",
                                   "W_up_v": "L1_attn_a_W_up_v"})),
        (build_ffn_l1, dict(nm={"sW1": "L1_ffn_sW1", "sW2": "L1_ffn_sW2",
                                "rW1": "L1_ffn_rW1", "rW2": "L1_ffn_rW2"})),
        (lambda **kw: build_attn_b("l1", **kw),
         dict(nm={"Wq": "L1_attn_b_Wq", "Wk": "L1_attn_b_Wk", "Wv": "L1_attn_b_Wv",
                  "Wo": "L1_attn_b_Wo", "q_norm_w": "L1_attn_b_q_norm_w",
                  "k_norm_w": "L1_attn_b_k_norm_w"})),
        (build_ffn_l1, dict(nm={"sW1": "L1_ffn_sW1", "sW2": "L1_ffn_sW2",
                                "rW1": "L1_ffn_rW1", "rW2": "L1_ffn_rW2"})),
    ]
    for i, (fn, kw) in enumerate(specs):
        fn(P=p, env=env, xap=chain[i], xnap=chain[i + 1], **kw)
        emit_barrier(p, env)
    return p


def _host_all_arrays(inputs):
    g = lambda k: np.asarray(inputs[k], np.float32)
    m = {}
    consts = _host_consts_attn()
    m["cos_t"] = consts["cos_t"]
    m["sin_t"] = consts["sin_t"]
    m["e2sel"] = consts["e2sel"]
    m["cmask"] = consts["cmask"]
    m["dec28"] = _host_consts_attn_a()
    for k in ("L0_attn_a_q_w", "L0_attn_a_k_w", "L0_attn_a_v_w", "L0_attn_a_out_w",
              "L0_ffn_Wg", "L0_ffn_W1", "L0_ffn_W2",
              "L0_attn_b_w_qkv", "L0_attn_b_w_o",
              "L1_attn_a_W_q", "L1_attn_a_W_k", "L1_attn_a_W_v", "L1_attn_a_W_o",
              "L1_attn_a_W_down_kv", "L1_attn_a_W_up_k", "L1_attn_a_W_up_v",
              "L1_ffn_sW1", "L1_ffn_sW2", "L1_ffn_rW1", "L1_ffn_rW2",
              "L1_attn_b_Wq", "L1_attn_b_Wk", "L1_attn_b_Wv", "L1_attn_b_Wo"):
        m[k] = g(k)
    m["L1_attn_b_q_norm_w"] = g("L1_attn_b_q_norm_w").reshape(1, 64)
    m["L1_attn_b_k_norm_w"] = g("L1_attn_b_k_norm_w").reshape(1, 64)
    qkbT = np.zeros((128, 16), np.float32)
    qkbT[:, 0:8] = g("L0_attn_a_q_b").reshape(8, 128).T
    qkbT[:, 8:16] = g("L0_attn_a_k_b").reshape(8, 128).T
    m["qkbT"] = qkbT
    gboT = np.zeros((128, 24), np.float32)
    gboT[:, 0:8] = g("L0_attn_a_gamma").reshape(8, 128).T
    gboT[:, 8:16] = g("L0_attn_a_beta").reshape(8, 128).T
    gboT[:, 16:24] = g("L0_attn_a_out_b").reshape(8, 128).T
    m["gboT"] = gboT
    m["aa0_vb"] = g("L0_attn_a_v_b").reshape(1, D)
    pja = np.ones((65, 256), np.float32)
    pja[0:64, :] = g("L0_attn_a_proj").T
    m["projT_aug"] = np.ascontiguousarray(pja)
    m["utmask"] = (np.arange(128)[:, None] <= np.arange(128)[None, :]).astype(np.float32)
    m["b1T"] = np.ascontiguousarray(g("L0_ffn_b1").reshape(8, 32, 128)
                                    .transpose(2, 0, 1).reshape(128, 256))
    m["b2T"] = np.ascontiguousarray(g("L0_ffn_b2").reshape(8, 8, 128)
                                    .transpose(2, 0, 1).reshape(128, 64))
    E8sel = np.zeros((128, 8), np.float32)
    for pp in range(64):
        E8sel[pp, pp % 8] = 1.0
    m["E8sel"] = E8sel
    m["E8T"] = np.ascontiguousarray(E8sel.T)
    m["routing_WT"] = np.ascontiguousarray(g("L1_ffn_routing_W").T)
    m["expert_bias128"] = np.ascontiguousarray(
        np.broadcast_to(g("L1_ffn_expert_bias").reshape(1, 8), (128, 8)))
    m["sb1T"] = np.ascontiguousarray(g("L1_ffn_sb1").reshape(4, 128).T)
    m["sb2T"] = np.ascontiguousarray(g("L1_ffn_sb2").reshape(8, 128).T)
    m["rb1T"] = np.ascontiguousarray(g("L1_ffn_rb1").reshape(8, 4, 128)
                                     .transpose(2, 0, 1).reshape(128, 32))
    m["rb2T"] = np.ascontiguousarray(g("L1_ffn_rb2").reshape(8, 8, 128)
                                     .transpose(2, 0, 1).reshape(128, 64))
    m["ident"] = np.eye(128, dtype=np.float32)
    return m


def kernel(**inputs):
    x = np.asarray(inputs["L0_attn_a_x"], np.float32)
    shards = _shard_x(x)
    arrs = _host_all_arrays(inputs)
    nc = _get_prog("mega", lambda: build_mega())
    in_maps = []
    for c in range(NCORES):
        mm_ = dict(arrs)
        mm_["x0"] = shards[c]
        in_maps.append(mm_)
    res = run_bass_kernel_spmd(nc, in_maps, core_ids=list(range(NCORES)))
    return _unshard_x([np.asarray(r["xn"]) for r in res.results])


# version 05 (final)
# speedup vs baseline: 1.4937x; 1.0379x over previous
import math
import numpy as np
from contextlib import ExitStack

import concourse.bass as bass
import concourse.mybir as mybir
from concourse.alu_op_type import AluOpType
from concourse.bass_utils import run_bass_kernel_spmd

F32 = mybir.dt.float32
BF16 = mybir.dt.bfloat16
AF = mybir.ActivationFunctionType
AX = mybir.AxisListType

NCORES = 8
D = 1024
T = 4096
SEQ = 2048
NB = 2
KC = D // 128
LN_EPS = 1e-5



class Sem:
    def __init__(self, handle):
        self.h = handle
        self.n = 0

    def inc(self, k=1):
        self.n += k
        return self.n


class Prog:

    ENGS = ("tensor", "vector", "scalar", "gpsimd", "sync")

    def __init__(self):
        self.nc = bass.Bass("TRN2", target_bir_lowering=False, debug=False)
        self.es = ExitStack()
        self.stacks = [self.es]
        self.steps = []
        self._dram_cache = {}
        self._dram_n = 0

    def dram(self, name, shape, dtype=F32, kind=None):
        if kind is None:
            self._dram_n += 1
            return self.nc.dram_tensor(f"{name}__{self._dram_n}", shape, dtype)
        if name in self._dram_cache:
            return self._dram_cache[name]
        t = self.nc.dram_tensor(name, shape, dtype, kind=kind)
        self._dram_cache[name] = t
        return t

    def scope_enter(self):
        self.stacks.append(ExitStack())

    def scope_exit(self):
        self.stacks.pop().close()

    def sbuf(self, name, shape, dtype):
        return self.stacks[-1].enter_context(
            self.nc.sbuf_tensor(name, shape, dtype, allow_name_mangling=True))

    def psum(self, name, shape, dtype=F32):
        return self.stacks[-1].enter_context(
            self.nc.psum_tensor(name, shape, dtype, allow_name_mangling=True))

    def sem(self, name):
        return Sem(self.es.enter_context(self.nc.semaphore(name)))

    def _step(self, eng, fn, w, i):
        iv = None
        if i is not None:
            s, k = i
            s.inc(k)
            iv = (s.h, k)
        wv = [(s.h, v) for (s, v) in w if v > 0]

        def run(E, fn=fn, wv=wv, iv=iv):
            for h, v in wv:
                E.wait_ge(h, v)
            inst = fn(E)
            if iv is not None:
                if inst is None:
                    inst = E.nop()
                inst.then_inc(iv[0], iv[1])
        self.steps.append((eng, run))

    def mm(self, out, lhsT, rhs, start=True, stop=True, w=(), i=None):
        self._step("tensor",
                   lambda E: E.matmul(out, lhsT, rhs, start=start, stop=stop), w, i)

    def transpose(self, out, in_, ident, w=(), i=None):
        self._step("tensor", lambda E: E.transpose(out, in_, ident), w, i)

    def tt(self, out, in0, in1, op, w=(), i=None):
        self._step("vector", lambda E: E.tensor_tensor(out=out, in0=in0, in1=in1, op=op), w, i)

    def ts(self, out, in0, s1, s2=None, op0=AluOpType.mult, op1=None, accum=None, w=(), i=None):
        kw = {}
        if op1 is not None:
            kw["op1"] = op1
        if accum is not None:
            kw["accum_out"] = accum
        self._step("vector",
                   lambda E: E.tensor_scalar(out=out, in0=in0, scalar1=s1, scalar2=s2,
                                             op0=op0, **kw), w, i)

    def vcopy(self, out, in_, w=(), i=None):
        self._step("vector", lambda E: E.tensor_copy(out=out, in_=in_), w, i)

    def stt(self, out, in0, scalar, in1, op0, op1, w=(), i=None):
        self._step("vector",
                   lambda E: E.scalar_tensor_tensor(out=out, in0=in0, scalar=scalar,
                                                    in1=in1, op0=op0, op1=op1), w, i)

    def vmemset(self, ap, val, w=(), i=None):
        self._step("vector", lambda E: E.memset(ap, val), w, i)

    def vrecip(self, out, in_, w=(), i=None):
        self._step("vector", lambda E: E.reciprocal(out=out, in_=in_), w, i)

    def vredmax(self, out, in_, w=(), i=None):
        self._step("vector", lambda E: E.reduce_max(out=out, in_=in_, axis=AX.X), w, i)

    def vselect(self, out, mask, on_true, on_false, w=(), i=None):
        self._step("vector",
                   lambda E: E.select(out=out, mask=mask, on_true=on_true, on_false=on_false),
                   w, i)

    def act(self, out, in_, func, bias=None, scale=None, accum=None, w=(), i=None):
        kw = {}
        if bias is not None:
            kw["bias"] = bias
        if scale is not None:
            kw["scale"] = scale
        if accum is not None:
            kw["accum_out"] = accum
        self._step("scalar", lambda E: E.activation(out=out, in_=in_, func=func, **kw), w, i)

    def scopy(self, out, in_, w=(), i=None):
        self._step("scalar", lambda E: E.copy(out=out, in_=in_), w, i)

    def gdma(self, out, in_, accum_op=None, w=(), i=None):
        if accum_op is None:
            self._step("gpsimd", lambda E: E.dma_start(out=out, in_=in_), w, i)
        else:
            self._step("gpsimd",
                       lambda E: E.dma_start(out=out, in_=in_, accum_op=accum_op), w, i)

    def sydma(self, out, in_, w=(), i=None):
        self._step("sync", lambda E: E.dma_start(out=out, in_=in_), w, i)

    def collective(self, kind, op, groups, ins, outs, w=(), i=None):
        self._step("gpsimd",
                   lambda E: E.collective_compute(kind, op, replica_groups=groups,
                                                  ins=ins, outs=outs), w, i)

    def wait(self, eng, sem, val):
        self._step(eng, lambda E: None, [(sem, val)], None)

    def build(self):
        with self.nc.Block() as block:
            for eng in self.ENGS:
                deco = getattr(block, eng)

                def body(E, eng=eng):
                    for e2, fn in self.steps:
                        if e2 == eng:
                            fn(E)
                deco(body)
        self.es.close()
        return self.nc



def make_env(p):
    env = {}
    env["xbuf"] = p.sbuf("xbuf", [128, KC * 512], F32)
    env["xbf"] = p.sbuf("xbf", [128, KC * 512], BF16)
    env["sqb"] = p.sbuf("sqb", [128, KC * 512], BF16)
    env["rows"] = p.sbuf("rows", [1, 1024], F32)
    env["small"] = p.sbuf("small", [1, 32], F32)
    env["eps1"] = p.sbuf("eps1", [1, 1], F32)
    env["eps6"] = p.sbuf("eps6", [1, 1], F32)
    env["ones128"] = p.sbuf("ones128", [128, 1], BF16)
    env["ones1f"] = p.sbuf("ones1f", [1, 128], F32)
    env["ones1b"] = p.sbuf("ones1b", [1, 128], BF16)
    env["lnscr"] = p.sbuf("lnscr", [128, 512], F32)
    env["ps_a"] = p.psum("ps_a", [128, 1024])
    env["ps_o"] = p.psum("ps_o", [128, 1024])
    env["ps_c"] = p.psum("ps_c", [128, 512])
    env["ps_bc"] = p.psum("ps_bc", [128, 1024])
    for s in ("s_x", "s_v", "s_s", "s_t", "s_st", "s_w", "s_g", "s_ac", "s_wr",
              "s_cc"):
        env[s] = p.sem(s)
    return env


def emit_barrier(p, env):
    sems = [env[k] for k in ("s_x", "s_v", "s_s", "s_t", "s_st", "s_w", "s_ac",
                             "s_wr", "s_cc")]
    for eng in Prog.ENGS:
        for s in sems:
            if s.n:
                p.wait(eng, s, s.n)


def emit_env_init(p, env):
    s_v = env["s_v"]
    p.vmemset(env["ones128"][:], 1.0, i=(s_v, 1))
    p.vmemset(env["ones1f"][:], 1.0, i=(s_v, 1))
    p.vmemset(env["ones1b"][:], 1.0, i=(s_v, 1))
    p.vmemset(env["eps1"][:], LN_EPS, i=(s_v, 1))
    p.vmemset(env["eps6"][:], 1e-6, i=(s_v, 1))


def emit_ln_block(p, env, x_ap, xln_slice_fn, b, blk):
    xbuf, xbf, sqb = env["xbuf"], env["xbf"], env["sqb"]
    rows = env["rows"]
    ps_a, ps_bc = env["ps_a"], env["ps_bc"]
    s_x, s_v, s_s, s_t = env["s_x"], env["s_v"], env["s_s"], env["s_t"]

    tof = b * SEQ + blk * 512
    p.gdma(out=xbuf[:].rearrange("p (c t) -> p c t", c=KC),
           in_=x_ap[:, tof:tof + 512].rearrange("(c p) t -> p c t", p=128),
           w=[(s_v, s_v.n), (s_t, s_t.n)], i=(s_x, 16))
    p.vcopy(out=xbf[:], in_=xbuf[:], w=[(s_x, s_x.n), (s_s, s_s.n)], i=(s_v, 1))
    p.act(out=sqb[:], in_=xbf[:], func=AF.Square, w=[(s_v, s_v.n)], i=(s_s, 1))
    for c in range(KC):
        p.mm(ps_a[0:1, 0:512], env["ones128"][:], xbf[:, c * 512:(c + 1) * 512],
             start=(c == 0), stop=(c == KC - 1),
             w=([(s_v, s_v.n)] if c == 0 else ()),
             i=((s_t, 1) if c == KC - 1 else None))
    for c in range(KC):
        p.mm(ps_a[0:1, 512:1024], env["ones128"][:], sqb[:, c * 512:(c + 1) * 512],
             start=(c == 0), stop=(c == KC - 1),
             w=([(s_s, s_s.n)] if c == 0 else ()),
             i=((s_t, 1) if c == KC - 1 else None))
    mu = rows[0:1, 0:512]
    var = rows[0:1, 512:1024]
    p.ts(mu, ps_a[0:1, 0:512], 1.0 / D, w=[(s_t, s_t.n)], i=(s_v, 1))
    p.ts(var, ps_a[0:1, 512:1024], 1.0 / D, i=(s_v, 1))
    p.tt(env["lnscr"][0:1, 0:512], mu, mu, AluOpType.mult, i=(s_v, 1))
    p.tt(var, var, env["lnscr"][0:1, 0:512], AluOpType.subtract, i=(s_v, 1))
    p.act(out=var, in_=var, func=AF.Ln, bias=env["eps1"][:], w=[(s_v, s_v.n)], i=(s_s, 1))
    p.act(out=var, in_=var, func=AF.Exp, scale=-0.5, i=(s_s, 1))
    p.tt(mu, mu, var, AluOpType.mult, w=[(s_s, s_s.n)], i=(s_v, 1))
    p.mm(ps_bc[:, 0:512], env["ones1f"][:], var, w=[(s_v, s_v.n)], i=(s_t, 1))
    p.mm(ps_bc[:, 512:1024], env["ones1f"][:], mu, i=(s_t, 1))
    for c in range(KC):
        xc = xbuf[:, c * 512:(c + 1) * 512]
        p.tt(env["lnscr"][:], xc, ps_bc[:, 0:512], AluOpType.mult,
             w=[(s_t, s_t.n)], i=(s_v, 1))
        p.tt(xln_slice_fn(c), env["lnscr"][:], ps_bc[:, 512:1024],
             AluOpType.subtract, i=(s_v, 1))



def build_attn_b(kind, P=None, env=None, xap=None, xnap=None, nm=None):
    p = P if P is not None else Prog()
    nm = nm or {}
    g = lambda n: nm.get(n, n)
    if xap is None:
        x = p.dram("x", [D, T], F32, kind="ExternalInput").ap()
        xn = p.dram("xn", [D, T], F32, kind="ExternalOutput").ap()
    else:
        x, xn = xap, xnap
    cos_t = p.dram("cos_t", [128, SEQ], F32, kind="ExternalInput").ap()
    sin_t = p.dram("sin_t", [128, SEQ], F32, kind="ExternalInput").ap()
    e2sel = p.dram("e2sel", [128, 2], F32, kind="ExternalInput").ap()
    if kind == "l0":
        wqkv_d = p.dram(g("w_qkv"), [D, 3 * D], F32, kind="ExternalInput").ap()
        wo_d = p.dram(g("w_o"), [D, D], F32, kind="ExternalInput").ap()
    else:
        wq_d = p.dram(g("Wq"), [D, D], F32, kind="ExternalInput").ap()
        wk_d = p.dram(g("Wk"), [D, D], F32, kind="ExternalInput").ap()
        wv_d = p.dram(g("Wv"), [D, D], F32, kind="ExternalInput").ap()
        wo_d = p.dram(g("Wo"), [D, D], F32, kind="ExternalInput").ap()
        qnw_d = p.dram(g("q_norm_w"), [1, 64], F32, kind="ExternalInput").ap()
        knw_d = p.dram(g("k_norm_w"), [1, 64], F32, kind="ExternalInput").ap()
        cmask_d = p.dram("cmask", [128, 4 * 512], F32, kind="ExternalInput").ap()

    if env is None:
        env = make_env(p)
    s_x, s_v, s_s, s_t, s_st, s_w = (env[k] for k in
                                     ("s_x", "s_v", "s_s", "s_t", "s_st", "s_w"))
    p.scope_enter()

    xln = p.sbuf("xln", [128, KC * SEQ], BF16)
    wqkv = p.sbuf("wqkv", [128, KC * 2 * D], BF16)
    wo = p.sbuf("wo", [128, KC * D], BF16)
    vtm = p.sbuf("vtm", [128, 16 * D], BF16)
    vplus = p.sbuf("vplus", [128, 2 * 16 * 65], BF16)
    cos2 = p.sbuf("cos2", [128, SEQ], BF16)
    sin2 = p.sbuf("sin2", [128, SEQ], BF16)
    e2s = p.sbuf("e2s", [128, 2], BF16)
    negb = p.sbuf("negb", [128, 2], F32)
    ptb = p.sbuf("ptb", [128, 4 * 512], BF16)
    rden = p.sbuf("rden", [1, 512], F32)
    bcast = p.sbuf("bcast", [64, 512], BF16)
    attn = p.sbuf("attn", [128, KC * SEQ], BF16)
    outsc = ptb[:, 0:2048].bitcast(F32)
    qh = env["xbf"][:, 0:SEQ]
    kh = env["xbf"][:, SEQ:2 * SEQ]
    sqs = env["sqb"][:, 0:SEQ]
    rot = env["sqb"][:, SEQ:2 * SEQ]
    if kind == "l1":
        qnw = p.sbuf("qnw", [1, 64], F32)
        knw = p.sbuf("knw", [1, 64], F32)
        cmask = p.sbuf("cmask_s", [128, 4 * 512], BF16)

    ps_a, ps_o, ps_c, ps_bc = (env["ps_a"], env["ps_o"], env["ps_c"], env["ps_bc"])

    if kind == "l0":
        for c in range(KC):
            p.gdma(out=wqkv[:, c * 2 * D: c * 2 * D + D],
                   in_=wqkv_d[c * 128:(c + 1) * 128, 0:D], i=(s_w, 16))
            p.gdma(out=wqkv[:, c * 2 * D + D: c * 2 * D + 2 * D],
                   in_=wqkv_d[c * 128:(c + 1) * 128, D:2 * D], i=(s_w, 16))
    else:
        for c in range(KC):
            p.gdma(out=wqkv[:, c * 2 * D: c * 2 * D + D],
                   in_=wq_d[c * 128:(c + 1) * 128, :], i=(s_w, 16))
            p.gdma(out=wqkv[:, c * 2 * D + D: c * 2 * D + 2 * D],
                   in_=wk_d[c * 128:(c + 1) * 128, :], i=(s_w, 16))
    for c in range(KC):
        p.gdma(out=wo[:, c * D:(c + 1) * D],
               in_=wo_d[c * 128:(c + 1) * 128, :], i=(s_w, 16))
    p.gdma(out=cos2[:], in_=cos_t, i=(s_w, 16))
    p.gdma(out=sin2[:], in_=sin_t, i=(s_w, 16))
    p.gdma(out=e2s[:], in_=e2sel, i=(s_w, 16))
    if kind == "l1":
        p.gdma(out=qnw[:], in_=qnw_d, i=(s_w, 16))
        p.gdma(out=knw[:], in_=knw_d, i=(s_w, 16))
        p.gdma(out=cmask[:], in_=cmask_d, i=(s_w, 16))
    wload_total = s_w.n

    emit_env_init(p, env)

    HOFF = {"q": 0, "k": D}

    exp_bank = {0: 0, 1: 0, 2: 0, 3: 0}
    pv_bank = {0: 0, 1: 0, 2: 0, 3: 0}

    for b in range(NB):
        for c in range(KC):
            p.gdma(out=attn[:, c * D:(c + 1) * D],
                   in_=(wqkv_d[c * 128:(c + 1) * 128, 2 * D:3 * D] if kind == "l0"
                        else wv_d[c * 128:(c + 1) * 128, :]),
                   w=[(s_t, s_t.n), (s_v, s_v.n)], i=(s_w, 16))
        vw_loaded = s_w.n
        for blk in range(4):
            emit_ln_block(p, env, x,
                          lambda c, blk=blk: xln[:, c * SEQ + blk * 512:
                                                 c * SEQ + blk * 512 + 512],
                          b, blk)

        p.wait("tensor", s_w, vw_loaded)
        evrel = {0: s_v.n, 1: s_v.n}
        for tc in range(16):
            for half in range(2):
                bank = (2 * tc + half) % 2
                for c in range(KC):
                    p.mm(ps_o[:, bank * 512: bank * 512 + 512],
                         xln[:, c * SEQ + tc * 128: c * SEQ + (tc + 1) * 128],
                         attn[:, c * D + half * 512: c * D + (half + 1) * 512],
                         start=(c == 0), stop=(c == KC - 1),
                         w=([(s_v, evrel[bank])] if c == 0 else ()),
                         i=((s_t, 1) if c == KC - 1 else None))
                p.vcopy(out=vtm[:, tc * D + half * 512: tc * D + (half + 1) * 512],
                        in_=ps_o[:, bank * 512: bank * 512 + 512],
                        w=[(s_t, s_t.n)], i=(s_v, 1))
                evrel[bank] = s_v.n

        for hp in range(8):
            evrel = {0: s_v.n, 1: s_v.n}
            for (dst, key) in ((qh, "q"), (kh, "k")):
                for qb in range(4):
                    bank = qb % 2
                    for c in range(KC):
                        p.mm(ps_o[:, bank * 512: bank * 512 + 512],
                             wqkv[:, c * 2 * D + HOFF[key] + hp * 128:
                                  c * 2 * D + HOFF[key] + (hp + 1) * 128],
                             xln[:, c * SEQ + qb * 512: c * SEQ + (qb + 1) * 512],
                             start=(c == 0), stop=(c == KC - 1),
                             w=([(s_v, evrel[bank])] if c == 0 else ()),
                             i=((s_t, 1) if c == KC - 1 else None))
                    p.vcopy(out=dst[:, qb * 512:(qb + 1) * 512],
                            in_=ps_o[:, bank * 512: bank * 512 + 512],
                            w=[(s_t, s_t.n)], i=(s_v, 1))
                    evrel[bank] = s_v.n

            if kind == "l1":
                for (dst, nw) in ((qh, qnw), (kh, knw)):
                    p.act(out=sqs[:], in_=dst[:], func=AF.Square,
                          w=[(s_v, s_v.n)], i=(s_s, 1))
                    for j in range(2):
                        for qb in range(4):
                            p.mm(ps_a[0:1, (qb % 2) * 512:(qb % 2) * 512 + 512],
                                 e2s[:, j:j + 1],
                                 sqs[:, qb * 512:(qb + 1) * 512],
                                 start=True, stop=True,
                                 w=[(s_s, s_s.n), (s_v, s_v.n)],
                                 i=(s_t, 1))
                            p.act(out=env["rows"][0:1, 0:512],
                                  in_=ps_a[0:1, (qb % 2) * 512:(qb % 2) * 512 + 512],
                                  func=AF.Ln, bias=env["eps6"][:], scale=1.0 / 64,
                                  w=[(s_t, s_t.n)], i=(s_s, 1))
                            p.act(out=env["rows"][0:1, 0:512], in_=env["rows"][0:1, 0:512],
                                  func=AF.Exp, scale=-0.5, i=(s_s, 1))
                            p.mm(ps_c[j * 64:(j + 1) * 64, 0:512], nw[:],
                                 env["rows"][0:1, 0:512],
                                 w=[(s_s, s_s.n), (s_v, s_v.n)], i=(s_t, 1))
                            p.tt(dst[j * 64:(j + 1) * 64, qb * 512:(qb + 1) * 512],
                                 dst[j * 64:(j + 1) * 64, qb * 512:(qb + 1) * 512],
                                 ps_c[j * 64:(j + 1) * 64, 0:512], AluOpType.mult,
                                 w=[(s_t, s_t.n)], i=(s_v, 1))

            for dst in (qh, kh):
                p.ts(rot[0:32, :], dst[32:64, :], -1.0, w=[(s_v, s_v.n), (s_s, s_s.n)], i=(s_v, 1))
                p.vcopy(rot[32:64, :], dst[0:32, :], i=(s_v, 1))
                p.ts(rot[64:96, :], dst[96:128, :], -1.0, i=(s_v, 1))
                p.vcopy(rot[96:128, :], dst[64:96, :], i=(s_v, 1))
                p.tt(rot[:], rot[:], sin2[:], AluOpType.mult, i=(s_v, 1))
                p.tt(dst[:], dst[:], cos2[:], AluOpType.mult, i=(s_v, 1))
                p.tt(dst[:], dst[:], rot[:], AluOpType.add, i=(s_v, 1))

            for (src, col) in ((qh, 0), (kh, 1)):
                p.act(out=sqs[:], in_=src[:], func=AF.Square,
                      w=[(s_v, s_v.n)], i=(s_s, 1))
                for j in range(2):
                    slot = 8 + 2 * col + j
                    for qb in range(4):
                        p.mm(ps_a[0:1, (qb % 2) * 512:(qb % 2) * 512 + 512],
                             e2s[:, j:j + 1], sqs[:, qb * 512:(qb + 1) * 512],
                             start=True, stop=True,
                             w=[(s_s, s_s.n), (s_v, s_v.n)], i=(s_t, 1))
                        p.vredmax(env["small"][0:1, slot:slot + 1]
                                  if qb == 0 else env["small"][0:1, 12:13],
                                  ps_a[0:1, (qb % 2) * 512:(qb % 2) * 512 + 512],
                                  w=[(s_t, s_t.n)], i=(s_v, 1))
                        if qb > 0:
                            p.tt(env["small"][0:1, slot:slot + 1],
                                 env["small"][0:1, slot:slot + 1],
                                 env["small"][0:1, 12:13], AluOpType.max, i=(s_v, 1))
            for j in range(2):
                pj = env["small"][0:1, 0 + j:1 + j]
                p.tt(pj, env["small"][0:1, 8 + j:9 + j],
                     env["small"][0:1, 10 + j:11 + j], AluOpType.mult,
                     w=[(s_v, s_v.n)], i=(s_v, 1))
                p.act(out=pj, in_=pj, func=AF.Ln, w=[(s_v, s_v.n)], i=(s_s, 1))
                p.act(out=pj, in_=pj, func=AF.Exp, scale=0.5, i=(s_s, 1))
                p.ts(pj, pj, -0.125, w=[(s_s, s_s.n)], i=(s_v, 1))
                p.mm(ps_c[:, 0:1], env["ones1f"][:], pj, w=[(s_v, s_v.n)], i=(s_t, 1))
                p.vcopy(negb[:, j:j + 1], ps_c[:, 0:1], w=[(s_t, s_t.n)], i=(s_v, 1))

            for j in range(2):
                hg = hp * 2 + j
                p.vmemset(vplus[:, j * 1040:(j + 1) * 1040], 1.0,
                          w=[(s_t, s_t.n)], i=(s_v, 1))
                for tc in range(16):
                    p.vcopy(vplus[:, j * 1040 + tc * 65: j * 1040 + tc * 65 + 64],
                            vtm[:, tc * D + hg * 64: tc * D + (hg + 1) * 64],
                            i=(s_v, 1))
            vplus_ready = s_v.n
            evac_done = {0: 0, 1: 0}
            for qc in range(4):
                nkt = 16 if kind == "l0" else (qc + 1) * 4
                pend = []

                def emit_pv(j, kt, slot, wv3):
                    if kt == 0 and evac_done[j]:
                        wv3 = wv3 + [(s_v, evac_done[j])]
                    p.mm(ps_o[0:65, j * 512: j * 512 + 512],
                         vplus[:, j * 1040 + kt * 65: j * 1040 + (kt + 1) * 65],
                         ptb[:, slot * 512:(slot + 1) * 512],
                         start=(kt == 0), stop=(kt == nkt - 1),
                         w=wv3, i=(s_t, 1))
                    pv_bank[slot] = s_t.n

                for kt in range(nkt):
                    for j in range(2):
                        slot = j * 2 + (kt % 2)
                        sbank = (ps_a[:, (kt % 2) * 512:(kt % 2) * 512 + 512]
                                 if j == 0 else
                                 ps_bc[:, (kt % 2) * 512:(kt % 2) * 512 + 512])
                        wv = [(s_v, vplus_ready)]
                        if exp_bank[slot]:
                            wv.append((s_s, exp_bank[slot]))
                        p.mm(sbank,
                             kh[j * 64:(j + 1) * 64, kt * 128:(kt + 1) * 128],
                             qh[j * 64:(j + 1) * 64, qc * 512:(qc + 1) * 512],
                             w=wv, i=(s_t, 1))
                        smm_done = s_t.n
                        wv2 = [(s_t, smm_done)]
                        if pv_bank[slot]:
                            wv2.append((s_t, pv_bank[slot]))
                        p.act(out=ptb[:, slot * 512:(slot + 1) * 512],
                              in_=sbank,
                              func=AF.Exp, scale=0.125, bias=negb[:, j:j + 1],
                              w=wv2, i=(s_s, 1))
                        exp_bank[slot] = s_s.n
                        wv3 = [(s_s, exp_bank[slot])]
                        if kind == "l1" and kt >= 4 * qc:
                            p.tt(ptb[:, slot * 512:(slot + 1) * 512],
                                 ptb[:, slot * 512:(slot + 1) * 512],
                                 cmask[:, (kt - 4 * qc) * 512:(kt - 4 * qc + 1) * 512],
                                 AluOpType.mult, w=[(s_s, exp_bank[slot])], i=(s_v, 1))
                            wv3 = [(s_v, s_v.n)]
                        pend.append((j, kt, slot, wv3))
                    if len(pend) > 2:
                        for args in pend[:-2]:
                            emit_pv(*args)
                        pend = pend[-2:]
                for args in pend:
                    emit_pv(*args)
                for j in range(2):
                    hg = hp * 2 + j
                    p.vrecip(rden[:], ps_o[64:65, j * 512: j * 512 + 512],
                             w=[(s_t, s_t.n)], i=(s_v, 1))
                    p.mm(ps_c[0:64, 0:512], env["ones1f"][0:1, 0:64], rden[:],
                         w=[(s_v, s_v.n)], i=(s_t, 1))
                    p.scopy(out=bcast[:], in_=ps_c[0:64, 0:512],
                            w=[(s_t, s_t.n)], i=(s_s, 1))
                    p.tt(attn[(hg % 2) * 64:(hg % 2) * 64 + 64,
                              (hg // 2) * SEQ + qc * 512:(hg // 2) * SEQ + (qc + 1) * 512],
                         ps_o[0:64, j * 512: j * 512 + 512],
                         bcast[:], AluOpType.mult,
                         w=[(s_s, s_s.n)], i=(s_v, 1))
                    evac_done[j] = s_v.n

        for blk in range(4):
            tof = b * SEQ + blk * 512
            p.gdma(out=env["xbuf"][:].rearrange("p (c t) -> p c t", c=KC),
                   in_=x[:, tof:tof + 512].rearrange("(c p) t -> p c t", p=128),
                   w=[(s_v, s_v.n), (s_st, s_st.n)], i=(s_x, 16))
            store_done = {}
            evrel = {0: s_v.n, 1: s_v.n}
            for oc in range(KC):
                bank = oc % 2
                for c in range(KC):
                    p.mm(ps_o[:, bank * 512: bank * 512 + 512],
                         wo[:, c * D + oc * 128: c * D + (oc + 1) * 128],
                         attn[:, c * SEQ + blk * 512: c * SEQ + (blk + 1) * 512],
                         start=(c == 0), stop=(c == KC - 1),
                         w=([(s_v, evrel[bank])] if c == 0 else ()),
                         i=((s_t, 1) if c == KC - 1 else None))
                wv = [(s_t, s_t.n), (s_x, s_x.n)]
                if oc - 2 in store_done:
                    wv.append((s_st, store_done[oc - 2]))
                p.tt(outsc[:, bank * 512: bank * 512 + 512],
                     ps_o[:, bank * 512: bank * 512 + 512],
                     env["xbuf"][:, oc * 512:(oc + 1) * 512], AluOpType.add,
                     w=wv, i=(s_v, 1))
                evrel[bank] = s_v.n
                p.sydma(out=xn[oc * 128:(oc + 1) * 128, tof:tof + 512],
                        in_=outsc[:, bank * 512: bank * 512 + 512],
                        w=[(s_v, s_v.n)], i=(s_st, 16))
                store_done[oc] = s_st.n

    p.wait("sync", s_st, s_st.n)
    p.scope_exit()
    return p



def build_attn_a_l0(P=None, env=None, xap=None, xnap=None, nm=None):
    p = P if P is not None else Prog()
    nm = nm or {}
    g = lambda n: nm.get(n, n)
    if xap is None:
        x = p.dram("x", [D, T], F32, kind="ExternalInput").ap()
        xn = p.dram("xn", [D, T], F32, kind="ExternalOutput").ap()
    else:
        x, xn = xap, xnap
    qw_d = p.dram(g("q_w"), [D, D], F32, kind="ExternalInput").ap()
    kw_d = p.dram(g("k_w"), [D, D], F32, kind="ExternalInput").ap()
    vw_d = p.dram(g("v_w"), [D, D], F32, kind="ExternalInput").ap()
    ow_d = p.dram(g("out_w"), [D, D], F32, kind="ExternalInput").ap()
    qkb_d = p.dram("qkbT", [128, 16], F32, kind="ExternalInput").ap()
    vb_d = p.dram("aa0_vb", [1, D], F32, kind="ExternalInput").ap()
    gbo_d = p.dram("gboT", [128, 24], F32, kind="ExternalInput").ap()
    pja_d = p.dram("projT_aug", [65, 256], F32, kind="ExternalInput").ap()
    ut_d = p.dram("utmask", [128, 128], F32, kind="ExternalInput").ap()
    e2sel = p.dram("e2sel", [128, 2], F32, kind="ExternalInput").ap()

    if env is None:
        env = make_env(p)
    s_x, s_v, s_s, s_t, s_st, s_w = (env[k] for k in
                                     ("s_x", "s_v", "s_s", "s_t", "s_st", "s_w"))
    p.scope_enter()

    xln = p.sbuf("xln", [128, KC * SEQ], BF16)
    attn = p.sbuf("attn", [128, KC * SEQ], BF16)
    vtm = p.sbuf("vtm", [128, 16 * D], BF16)
    wqh = p.sbuf("wqh", [128, 2 * KC * 128], BF16)
    wkh = p.sbuf("wkh", [128, 2 * KC * 128], BF16)
    qaug = p.sbuf("qaug", [128, SEQ], BF16)
    kaug = p.sbuf("kaug", [128, SEQ], BF16)
    qpf = p.sbuf("qpf", [128, 2 * SEQ], BF16)
    kpf = p.sbuf("kpf", [128, 2 * SEQ], BF16)
    kptm = p.sbuf("kptm", [128, 16 * 256], BF16)
    pja = p.sbuf("pja", [128, 256], BF16)
    utm = p.sbuf("utm", [128, 128], BF16)
    e2s = p.sbuf("e2s", [128, 2], BF16)
    vplus = p.sbuf("vplus", [128, 16 * 65], BF16)
    Ss = p.sbuf("Ss", [128, 2 * 65], F32)
    Sb = p.sbuf("Sb", [128, 2 * 65], BF16)
    abuf = p.sbuf("abuf", [128, 2 * 128], BF16)
    qkb = p.sbuf("qkb", [128, 16], F32)
    gbo = p.sbuf("gbo", [128, 24], F32)
    vbt = p.sbuf("vbt", [128, 2 * 512], F32)
    rden = p.sbuf("rdenA", [1, 128], F32)
    bca = p.sbuf("bca", [64, 128], BF16)
    outsc = qpf[:, 0:2048].bitcast(F32)
    ones128f = p.sbuf("ones128f", [128, 1], F32)
    qh_pair = env["xbf"][:, 0:SEQ]
    kh_pair = env["xbf"][:, SEQ:2 * SEQ]
    sqs = env["sqb"][:, 0:SEQ]
    resbuf = env["xbuf"]

    ps_a, ps_o, ps_c, ps_bc = env["ps_a"], env["ps_o"], env["ps_c"], env["ps_bc"]

    for (dst, srcd) in ((pja[0:65, :], pja_d), (utm[:], ut_d), (e2s[:], e2sel),
                        (qkb[:], qkb_d), (gbo[:], gbo_d)):
        p.gdma(out=dst, in_=srcd, i=(s_w, 16))
    wload_total = s_w.n
    emit_env_init(p, env)
    p.vmemset(ones128f[:], 1.0, i=(s_v, 1))
    vbrow = p.sbuf("vbrow", [1, D], F32)
    p.gdma(out=vbrow[:], in_=vb_d, i=(s_w, 16))
    p.wait("tensor", s_w, s_w.n)
    for half in range(2):
        p.mm(ps_c[:, 0:512], env["ones1f"][:], vbrow[0:1, half * 512:(half + 1) * 512],
             w=[(s_v, s_v.n)], i=(s_t, 1))
        p.vcopy(vbt[:, half * 512:(half + 1) * 512], ps_c[:, 0:512],
                w=[(s_t, s_t.n)], i=(s_v, 1))

    at_bank = {0: 0, 1: 0}
    nd_bank = {0: 0, 1: 0}

    for b in range(NB):
        for c in range(KC):
            p.gdma(out=attn[:, c * D:(c + 1) * D],
                   in_=vw_d[c * 128:(c + 1) * 128, :],
                   w=[(s_t, s_t.n), (s_v, s_v.n)], i=(s_w, 16))
        vw_loaded = s_w.n

        for blk in range(4):
            emit_ln_block(p, env, x,
                          lambda c, blk=blk: xln[:, c * SEQ + blk * 512:
                                                 c * SEQ + blk * 512 + 512],
                          b, blk)

        p.wait("tensor", s_w, vw_loaded)
        evrelV = {0: s_v.n, 1: s_v.n}
        for tc in range(16):
            for half in range(2):
                bank = half
                for c in range(KC):
                    p.mm(ps_o[:, bank * 512: bank * 512 + 512],
                         xln[:, c * SEQ + tc * 128: c * SEQ + (tc + 1) * 128],
                         attn[:, c * D + half * 512: c * D + (half + 1) * 512],
                         start=(c == 0), stop=(c == KC - 1),
                         w=([(s_v, evrelV[bank])] if c == 0 else ()),
                         i=((s_t, 1) if c == KC - 1 else None))
                p.tt(vtm[:, tc * D + half * 512: tc * D + (half + 1) * 512],
                     ps_o[:, bank * 512: bank * 512 + 512],
                     vbt[:, half * 512:(half + 1) * 512], AluOpType.add,
                     w=[(s_t, s_t.n)], i=(s_v, 1))
                evrelV[bank] = s_v.n

        wh_done = {0: 0, 1: 0}
        for hp in range(8):
            hpar = hp % 2
            for (whdst, wsrcd) in ((wqh, qw_d), (wkh, kw_d)):
                p.gdma(out=whdst[:, hpar * KC * 128:(hpar + 1) * KC * 128].rearrange(
                           "p (c o) -> p c o", c=KC),
                       in_=wsrcd[:, hp * 128:(hp + 1) * 128].rearrange(
                           "(c p) o -> p c o", p=128),
                       w=([(s_t, wh_done[hpar])] if wh_done[hpar] else ()),
                       i=(s_w, 16))
            wh_ld = s_w.n
            p.wait("tensor", s_w, wh_ld)
            evrelQ = {0: s_v.n, 1: s_v.n}
            for (dst, wsrc, bcol) in ((qh_pair, wqh, hp), (kh_pair, wkh, 8 + hp)):
                for qb in range(4):
                    bank = qb % 2
                    for c in range(KC):
                        p.mm(ps_o[:, bank * 512: bank * 512 + 512],
                             wsrc[:, hpar * KC * 128 + c * 128:
                                  hpar * KC * 128 + (c + 1) * 128],
                             xln[:, c * SEQ + qb * 512: c * SEQ + (qb + 1) * 512],
                             start=(c == 0), stop=(c == KC - 1),
                             w=([(s_v, evrelQ[bank])] if c == 0 else ()),
                             i=((s_t, 1) if c == KC - 1 else None))
                    p.ts(dst[:, qb * 512:(qb + 1) * 512],
                         ps_o[:, bank * 512: bank * 512 + 512],
                         qkb[:, bcol:bcol + 1], op0=AluOpType.add,
                         w=[(s_t, s_t.n)], i=(s_v, 1))
                    evrelQ[bank] = s_v.n
            wh_done[hpar] = s_t.n
            for j in range(2):
                h = hp * 2 + j
                for (aug, pair) in ((qaug, qh_pair), (kaug, kh_pair)):
                    p.vcopy(aug[0:64, :], pair[j * 64:(j + 1) * 64, :],
                            w=[(s_v, s_v.n)], i=(s_v, 1))
                    p.act(out=sqs[:], in_=pair[:], func=AF.Square,
                          w=[(s_v, s_v.n)], i=(s_s, 1))
                    for qb in range(4):
                        p.mm(ps_c[0:1, 0:512], e2s[:, j:j + 1],
                             sqs[:, qb * 512:(qb + 1) * 512],
                             w=[(s_s, s_s.n), (s_v, s_v.n)], i=(s_t, 1))
                        p.ts(aug[64:65, qb * 512:(qb + 1) * 512], ps_c[0:1, 0:512],
                             -1.0 / 16, op0=AluOpType.mult,
                             w=[(s_t, s_t.n)], i=(s_v, 1))
                exprel = {0: 0, 1: 0}
                first_map = [True]
                for (src, dst) in ((qaug, qpf), (kaug, kpf)):
                    for mc in range(2):
                        for qb in range(4):
                            bank = qb % 2
                            wvm = []
                            if first_map[0]:
                                wvm = [(s_v, s_v.n), (s_s, s_s.n)]
                                first_map[0] = False
                            elif exprel[bank]:
                                wvm = [(s_s, exprel[bank])]
                            p.mm(ps_o[:, bank * 512: bank * 512 + 512],
                                 pja[0:65, mc * 128:(mc + 1) * 128],
                                 src[0:65, qb * 512:(qb + 1) * 512],
                                 w=wvm, i=(s_t, 1))
                            p.act(out=dst[:, mc * SEQ + qb * 512: mc * SEQ + (qb + 1) * 512],
                                  in_=ps_o[:, bank * 512: bank * 512 + 512],
                                  func=AF.Exp, scale=0.125,
                                  w=[(s_t, s_t.n)], i=(s_s, 1))
                            exprel[bank] = s_s.n
                for tc in range(16):
                    bank = tc % 2
                    p.mm(ps_o[:, bank * 512: bank * 512 + 256],
                         kaug[0:65, tc * 128:(tc + 1) * 128],
                         pja[0:65, :],
                         w=[(s_s, exprel[bank])], i=(s_t, 1))
                    p.act(out=kptm[:, tc * 256:(tc + 1) * 256],
                          in_=ps_o[:, bank * 512: bank * 512 + 256],
                          func=AF.Exp, scale=0.125,
                          w=[(s_t, s_t.n)], i=(s_s, 1))
                    exprel[bank] = s_s.n
                p.vmemset(vplus[:], 1.0, w=[(s_t, s_t.n)], i=(s_v, 1))
                for tc in range(16):
                    p.vcopy(vplus[:, tc * 65: tc * 65 + 64],
                            vtm[:, tc * D + h * 64: tc * D + (h + 1) * 64],
                            i=(s_v, 1))
                p.vmemset(Ss[:], 0.0, i=(s_v, 1))
                p.vmemset(Sb[:], 0.0, i=(s_v, 1))
                ready = s_v.n
                p.wait("scalar", s_v, ready)
                for tc in range(16):
                    par = tc % 2
                    wv = [(s_s, s_s.n), (s_v, max(ready, at_bank[par]))]
                    for mc in range(2):
                        p.mm(ps_a[:, par * 512: par * 512 + 128],
                             kpf[:, mc * SEQ + tc * 128: mc * SEQ + (tc + 1) * 128],
                             qpf[:, mc * SEQ + tc * 128: mc * SEQ + (tc + 1) * 128],
                             start=(mc == 0), stop=(mc == 1),
                             w=(wv if mc == 0 else ()),
                             i=((s_t, 1) if mc == 1 else None))
                    p.tt(abuf[:, par * 128:(par + 1) * 128],
                         ps_a[:, par * 512: par * 512 + 128], utm[:],
                         AluOpType.mult, w=[(s_t, s_t.n)], i=(s_v, 1))
                    at_bank[par] = s_v.n
                    wv2 = [(s_v, s_v.n)]
                    if nd_bank[par]:
                        wv2.append((s_v, nd_bank[par]))
                    p.mm(ps_o[0:65, par * 512: par * 512 + 128],
                         Sb[:, 0:65], qpf[:, 0 * SEQ + tc * 128:(tc + 1) * 128],
                         start=True, stop=False, w=wv2, i=None)
                    p.mm(ps_o[0:65, par * 512: par * 512 + 128],
                         Sb[:, 65:130], qpf[:, SEQ + tc * 128: SEQ + (tc + 1) * 128],
                         start=False, stop=False, i=None)
                    p.mm(ps_o[0:65, par * 512: par * 512 + 128],
                         vplus[:, tc * 65:(tc + 1) * 65],
                         abuf[:, par * 128:(par + 1) * 128],
                         start=False, stop=True, i=(s_t, 1))
                    numden_done = s_t.n
                    for mc in range(2):
                        p.mm(ps_bc[:, mc * 512: mc * 512 + 65],
                             kptm[:, tc * 256 + mc * 128: tc * 256 + (mc + 1) * 128],
                             vplus[:, tc * 65:(tc + 1) * 65],
                             w=([(s_v, s_v.n)] if mc == 0 else ()), i=(s_t, 1))
                    p.wait("vector", s_t, s_t.n)
                    for mc in range(2):
                        p.tt(Ss[:, mc * 65:(mc + 1) * 65], Ss[:, mc * 65:(mc + 1) * 65],
                             ps_bc[:, mc * 512: mc * 512 + 65], AluOpType.add,
                             i=(s_v, 1))
                        p.vcopy(Sb[:, mc * 65:(mc + 1) * 65],
                                Ss[:, mc * 65:(mc + 1) * 65], i=(s_v, 1))
                    p.vrecip(rden[:], ps_o[64:65, par * 512: par * 512 + 128],
                             w=[(s_t, numden_done)], i=(s_v, 1))
                    p.mm(ps_c[0:64, 0:128], env["ones1f"][0:1, 0:64], rden[:],
                         w=[(s_v, s_v.n)], i=(s_t, 1))
                    p.scopy(out=bca[:], in_=ps_c[0:64, 0:128],
                            w=[(s_t, s_t.n)], i=(s_s, 1))
                    p.tt(attn[(h % 2) * 64:(h % 2) * 64 + 64,
                              (h // 2) * SEQ + tc * 128:(h // 2) * SEQ + (tc + 1) * 128],
                         ps_o[0:64, par * 512: par * 512 + 128],
                         bca[:], AluOpType.mult,
                         w=[(s_s, s_s.n)], i=(s_v, 1))
                    nd_bank[par] = s_v.n

        for c in range(KC):
            p.gdma(out=vtm[:, c * D:(c + 1) * D], in_=ow_d[c * 128:(c + 1) * 128, :],
                   w=[(s_t, s_t.n), (s_v, s_v.n)], i=(s_w, 16))
        ow_loaded = s_w.n
        p.wait("tensor", s_w, ow_loaded)
        for blk in range(4):
            tof = b * SEQ + blk * 512
            evrelO = {0: s_v.n, 1: s_v.n}
            for oc in range(KC):
                bank = oc % 2
                for c in range(KC):
                    p.mm(ps_o[:, bank * 512: bank * 512 + 512],
                         vtm[:, c * D + oc * 128: c * D + (oc + 1) * 128],
                         attn[:, c * SEQ + blk * 512: c * SEQ + (blk + 1) * 512],
                         start=(c == 0), stop=(c == KC - 1),
                         w=([(s_v, evrelO[bank])] if c == 0 else ()),
                         i=((s_t, 1) if c == KC - 1 else None))
                p.ts(resbuf[:, oc * 512:(oc + 1) * 512],
                     ps_o[:, bank * 512: bank * 512 + 512],
                     gbo[:, 16 + oc:16 + oc + 1], op0=AluOpType.add,
                     w=[(s_t, s_t.n)], i=(s_v, 1))
                p.tt(resbuf[:, oc * 512:(oc + 1) * 512],
                     resbuf[:, oc * 512:(oc + 1) * 512],
                     xln[:, oc * SEQ + blk * 512: oc * SEQ + (blk + 1) * 512],
                     AluOpType.add, i=(s_v, 1))
                evrelO[bank] = s_v.n
            p.act(out=env["sqb"][:], in_=resbuf[:], func=AF.Square,
                  w=[(s_v, s_v.n)], i=(s_s, 1))
            for c in range(KC):
                p.mm(ps_a[0:1, 0:512], ones128f[:], resbuf[:, c * 512:(c + 1) * 512],
                     start=(c == 0), stop=(c == KC - 1),
                     w=([(s_v, s_v.n)] if c == 0 else ()),
                     i=((s_t, 1) if c == KC - 1 else None))
            for c in range(KC):
                p.mm(ps_a[0:1, 512:1024], env["ones128"][:],
                     env["sqb"][:, c * 512:(c + 1) * 512],
                     start=(c == 0), stop=(c == KC - 1),
                     w=([(s_s, s_s.n)] if c == 0 else ()),
                     i=((s_t, 1) if c == KC - 1 else None))
            mu = env["rows"][0:1, 0:512]
            var = env["rows"][0:1, 512:1024]
            p.ts(mu, ps_a[0:1, 0:512], 1.0 / D, w=[(s_t, s_t.n)], i=(s_v, 1))
            p.ts(var, ps_a[0:1, 512:1024], 1.0 / D, i=(s_v, 1))
            p.tt(env["lnscr"][0:1, 0:512], mu, mu, AluOpType.mult, i=(s_v, 1))
            p.tt(var, var, env["lnscr"][0:1, 0:512], AluOpType.subtract, i=(s_v, 1))
            p.act(out=var, in_=var, func=AF.Ln, bias=env["eps6"][:],
                  w=[(s_v, s_v.n)], i=(s_s, 1))
            p.act(out=var, in_=var, func=AF.Exp, scale=-0.5, i=(s_s, 1))
            p.tt(mu, mu, var, AluOpType.mult, w=[(s_s, s_s.n)], i=(s_v, 1))
            p.mm(ps_bc[:, 0:512], env["ones1f"][:], var, w=[(s_v, s_v.n)], i=(s_t, 1))
            p.mm(ps_bc[:, 512:1024], env["ones1f"][:], mu, i=(s_t, 1))
            store_done = {}
            for oc in range(KC):
                par2 = oc % 2
                p.gdma(out=env["lnscr"][:],
                       in_=x[:, tof:tof + 512].rearrange("(c p) t -> p c t", p=128)[
                           :, oc, :] if False else
                           x[oc * 128:(oc + 1) * 128, tof:tof + 512],
                       w=[(s_v, s_v.n)], i=(s_x, 16))
                o = outsc[:, par2 * 512:(par2 + 1) * 512]
                wv = [(s_t, s_t.n), (s_x, s_x.n)]
                if oc - 2 in store_done:
                    wv.append((s_st, store_done[oc - 2]))
                p.tt(o, resbuf[:, oc * 512:(oc + 1) * 512], ps_bc[:, 0:512],
                     AluOpType.mult, w=wv, i=(s_v, 1))
                p.tt(o, o, ps_bc[:, 512:1024], AluOpType.subtract, i=(s_v, 1))
                p.ts(o, o, gbo[:, oc:oc + 1], gbo[:, 8 + oc:8 + oc + 1],
                     op0=AluOpType.mult, op1=AluOpType.add, i=(s_v, 1))
                p.tt(o, o, env["lnscr"][:], AluOpType.add, i=(s_v, 1))
                p.sydma(out=xn[oc * 128:(oc + 1) * 128, tof:tof + 512],
                        in_=o, w=[(s_v, s_v.n)], i=(s_st, 16))
                store_done[oc] = s_st.n

    p.wait("sync", s_st, s_st.n)
    p.scope_exit()
    return p


def run_attn_a_l0(x_shards, q_w, q_b, k_w, k_b, v_w, v_b, out_w, out_b, proj,
                  gamma, beta):
    consts = _host_consts_attn()
    nc = _get_prog("attn_a_l0", lambda: build_attn_a_l0())
    qkbT = np.zeros((128, 16), np.float32)
    qkbT[:, 0:8] = np.asarray(q_b, np.float32).reshape(8, 128).T
    qkbT[:, 8:16] = np.asarray(k_b, np.float32).reshape(8, 128).T
    gboT = np.zeros((128, 24), np.float32)
    gboT[:, 0:8] = np.asarray(gamma, np.float32).reshape(8, 128).T
    gboT[:, 8:16] = np.asarray(beta, np.float32).reshape(8, 128).T
    gboT[:, 16:24] = np.asarray(out_b, np.float32).reshape(8, 128).T
    pja = np.ones((65, 256), np.float32)
    pja[0:64, :] = np.asarray(proj, np.float32).T
    ut = (np.arange(128)[:, None] <= np.arange(128)[None, :]).astype(np.float32)
    in_maps = []
    for c in range(NCORES):
        m = {"x": x_shards[c], "q_w": np.asarray(q_w, np.float32),
             "k_w": np.asarray(k_w, np.float32), "v_w": np.asarray(v_w, np.float32),
             "out_w": np.asarray(out_w, np.float32),
             "qkbT": qkbT, "aa0_vb": np.asarray(v_b, np.float32).reshape(1, D),
             "gboT": gboT, "projT_aug": pja, "utmask": ut,
             "e2sel": consts["e2sel"]}
        in_maps.append(m)
    res = run_bass_kernel_spmd(nc, in_maps, core_ids=list(range(NCORES)))
    return [np.asarray(r["xn"]) for r in res.results]



def build_ffn_l0(mode="full", P=None, env=None, xap=None, xnap=None, nm=None):
    p = P if P is not None else Prog()
    nm = nm or {}
    g = lambda n: nm.get(n, n)
    dbg = (mode == "wdump")
    if xap is None:
        x = p.dram("x", [D, T], F32, kind="ExternalInput").ap()
        xn = p.dram("xn", [D, T], F32, kind="ExternalOutput").ap()
    else:
        x, xn = xap, xnap
    wg_d = p.dram(g("Wg"), [D, 8], F32, kind="ExternalInput").ap()
    w1_d = p.dram(g("W1"), [8, D, 4096], F32, kind="ExternalInput").ap()
    b1_d = p.dram("b1T", [128, 256], F32, kind="ExternalInput").ap()
    w2_d = p.dram(g("W2"), [8, 4096, D], F32, kind="ExternalInput").ap()
    b2_d = p.dram("b2T", [128, 64], F32, kind="ExternalInput").ap()
    e8s_d = p.dram("E8sel", [128, 8], F32, kind="ExternalInput").ap()
    e8t_d = p.dram("E8T", [8, 128], F32, kind="ExternalInput").ap()

    cc_in = p.dram("cc_in", [8, T], F32)
    cc_out = p.dram("cc_out", [64, T], F32)
    xlnd = p.dram("xlnd", [D, T], BF16)
    wdram0 = p.dram("wdram0", [8, T], BF16)
    if dbg:
        wout = p.dram("wout", [8, T], BF16, kind="ExternalOutput").ap()
        thout = p.dram("thout", [8, 8], F32, kind="ExternalOutput").ap()
        xlout = p.dram("xlout", [D, 512], BF16, kind="ExternalOutput").ap()
        scout = p.dram("scout", [64, T], F32, kind="ExternalOutput").ap()

    if env is None:
        env = make_env(p)
    s_x, s_v, s_s, s_t, s_st, s_w = (env[k] for k in
                                     ("s_x", "s_v", "s_s", "s_t", "s_st", "s_w"))
    s_ac = env["s_ac"]
    s_wr = env["s_wr"]
    s_cc = env["s_cc"]
    p.scope_enter()

    w1r = p.sbuf("w1r", [128, KC * 4096], BF16)
    w2r = p.sbuf("w2r", [128, 32 * D], BF16)
    hbuf = p.sbuf("hbuf", [128, 32 * 512], BF16)
    wgb = p.sbuf("wgb", [128, KC * 8], BF16)
    b1T = p.sbuf("b1Ts", [128, 256], F32)
    b2T = p.sbuf("b2Ts", [128, 64], F32)
    e8s = p.sbuf("e8s", [128, 8], F32)
    e8t = p.sbuf("e8t", [8, 128], F32)
    scstage = p.sbuf("scstage", [8, 1024], F32)
    th = p.sbuf("th", [8, 8], F32)
    rowcnt = p.sbuf("rowcnt", [128, 1], F32)
    wrow = p.sbuf("wrow", [1, 2 * 512], BF16)
    xlnb = hbuf[:, 0:4096]
    sc128 = hbuf[0:64, 0:8192].bitcast(F32)
    sloc = hbuf[0:8, 8192:16384].bitcast(F32)
    w8 = hbuf[0:8, 4096:8192]
    xt2a = env["xbuf"][:, 0:2048].bitcast(BF16)
    xt2b = env["xbuf"][:, 2048:4096].bitcast(BF16)
    ysc0 = env["xbf"][:, 0:1024].bitcast(F32)
    ysc1 = env["xbf"][:, 1024:2048].bitcast(F32)
    wtile = env["sqb"][:, 0:1024].bitcast(F32)

    ps_a, ps_o, ps_c, ps_bc = env["ps_a"], env["ps_o"], env["ps_c"], env["ps_bc"]

    for c in range(KC):
        p.gdma(out=wgb[:, c * 8:(c + 1) * 8], in_=wg_d[c * 128:(c + 1) * 128, :],
               i=(s_w, 16))
    for (dst, srcd) in ((b1T, b1_d), (b2T, b2_d), (e8s, e8s_d), (e8t, e8t_d)):
        p.gdma(out=dst[:], in_=srcd, i=(s_w, 16))
    p.gdma(out=xn, in_=x, i=(s_ac, 16))
    xn_init = s_ac.n
    wload_total = s_w.n
    emit_env_init(p, env)
    p.wait("vector", s_w, wload_total)
    p.wait("tensor", s_w, wload_total)

    for gblk in range(8):
        b, blk = divmod(gblk, 4)
        emit_ln_block(p, env, x,
                      lambda c: xlnb[:, c * 512:(c + 1) * 512],
                      b, blk)
        tof = gblk * 512
        p.gdma(out=xlnd[:, tof:tof + 512].rearrange("(c p) t -> p c t", p=128),
               in_=xlnb.rearrange("p (c t) -> p c t", c=KC),
               w=[(s_v, s_v.n)], i=(s_x, 16))
        for c in range(KC):
            p.mm(ps_c[0:8, 0:512], wgb[:, c * 8:(c + 1) * 8],
                 xlnb[:, c * 512:(c + 1) * 512],
                 start=(c == 0), stop=(c == KC - 1),
                 w=([(s_v, s_v.n), (s_s, s_s.n)] if c == 0 else ()),
                 i=((s_t, 1) if c == KC - 1 else None))
        par = gblk % 2
        p.act(out=scstage[0:8, par * 512:(par + 1) * 512], in_=ps_c[0:8, 0:512],
              func=AF.Sigmoid, w=[(s_t, s_t.n), (s_x, s_x.n)], i=(s_s, 1))
        p.gdma(out=cc_in.ap()[0:8, tof:tof + 512],
               in_=scstage[0:8, par * 512:(par + 1) * 512],
               w=[(s_s, s_s.n)], i=(s_x, 16))

    p.collective("AllGather", AluOpType.bypass, [list(range(8))],
                 [cc_in.ap().opt()], [cc_out.ap().opt()],
                 w=[(s_x, s_x.n)], i=(s_cc, 1))
    p.wait("gpsimd", s_cc, s_cc.n)
    p.gdma(out=sc128, in_=cc_out.ap(),
           w=[(s_t, s_t.n), (s_v, s_v.n), (s_s, s_s.n)], i=(s_x, 16))
    sc_ready = s_x.n

    cmpscr = env["xbuf"][0:64, :]

    def VS(fn):
        fn()
        p.wait("vector", s_v, s_v.n)
        p.wait("scalar", s_s, s_s.n)
        p.wait("vector", s_s, s_s.n)
        p.wait("scalar", s_v, s_v.n)
        p.wait("tensor", s_v, s_v.n)
        p.wait("vector", s_t, s_t.n)

    lo = th[0:8, 0:1]
    hi = th[0:8, 1:2]
    mid = th[0:8, 2:3]
    cge = th[0:8, 3:4]
    kk = th[0:8, 4:5]
    VS(lambda: p.vmemset(lo, 0.0, w=[(s_x, sc_ready)], i=(s_v, 1)))
    VS(lambda: p.vmemset(hi, 1.0, i=(s_v, 1)))
    VS(lambda: p.vmemset(mid, 0.5, i=(s_v, 1)))
    VS(lambda: p.vmemset(kk, 8192.0, i=(s_v, 1)))
    for it in range(44):
        VS(lambda: p.mm(ps_c[0:64, 0:1], e8t[0:8, 0:64], mid, i=(s_t, 1)))
        VS(lambda: p.ts(cmpscr, sc128, ps_c[0:64, 0:1],
                        s2=0.0, op0=AluOpType.is_ge, op1=AluOpType.add,
                        accum=rowcnt[0:64, :], i=(s_v, 1)))
        VS(lambda: p.mm(ps_bc[0:8, 0:1], e8s[0:64, :], rowcnt[0:64, :], i=(s_t, 1)))
        VS(lambda: p.ts(cge, ps_bc[0:8, 0:1], kk, op0=AluOpType.is_ge, i=(s_v, 1)))
        t1 = th[0:8, 5:6]
        VS(lambda: p.tt(t1, mid, lo, AluOpType.subtract, i=(s_v, 1)))
        VS(lambda: p.tt(t1, cge, t1, AluOpType.mult, i=(s_v, 1)))
        VS(lambda: p.tt(lo, lo, t1, AluOpType.add, i=(s_v, 1)))
        VS(lambda: p.tt(t1, hi, mid, AluOpType.subtract, i=(s_v, 1)))
        VS(lambda: p.tt(t1, cge, t1, AluOpType.mult, i=(s_v, 1)))
        VS(lambda: p.tt(hi, mid, t1, AluOpType.add, i=(s_v, 1)))
        VS(lambda: p.ts(mid, lo, hi, 0.5, op0=AluOpType.add, op1=AluOpType.mult,
                        i=(s_v, 1)))
    slocal = hbuf[0:8, 8192:16384].bitcast(F32)
    mk = hbuf[0:8, 0:4096].bitcast(F32)
    VS(lambda: p.gdma(out=slocal, in_=cc_in.ap(), w=[(s_v, s_v.n)], i=(s_x, 16)))
    for half in range(2):
        sl_h = hbuf[0:8, 8192 + half * 4096: 8192 + (half + 1) * 4096].bitcast(F32)
        w8_h = hbuf[0:8, 4096 + half * 2048: 4096 + (half + 1) * 2048]
        VS(lambda sl_h=sl_h: p.ts(mk, sl_h, lo, op0=AluOpType.is_ge,
                                  w=[(s_x, s_x.n)], i=(s_v, 1)))
        VS(lambda sl_h=sl_h, w8_h=w8_h: p.tt(w8_h, mk, sl_h,
                                             AluOpType.mult, i=(s_v, 1)))
    VS(lambda: p.gdma(out=wdram0.ap(), in_=w8, w=[(s_v, s_v.n)], i=(s_wr, 16)))
    wdram_ready = s_wr.n
    if dbg:
        p.gdma(out=wout, in_=wdram0.ap(), w=[(s_wr, wdram_ready)], i=(s_wr, 16))
        p.gdma(out=thout, in_=th[:], w=[(s_v, s_v.n)], i=(s_wr, 16))
        p.gdma(out=xlout, in_=xlnd.ap()[:, 0:512], i=(s_wr, 16))
        p.gdma(out=scout, in_=cc_out.ap(), i=(s_wr, 16))

    ysc_dma = {k: 0 for k in range(4)}
    ysc_ts = {0: 0, 1: 0}
    gelu_done = {0: 0, 1: 0}
    hread = 0
    xt_done = {0: 0, 1: 0}
    yscs = [env["xbf"][:, i * 1024:(i + 1) * 1024].bitcast(F32) for i in range(4)]
    wpar = 0
    iters = [(e, tb) for e in range(8) for tb in range(8)]

    def xt2_of(i):
        return xt2a if (i % 2) == 0 else xt2b

    def load_xt(i):
        e, tb = iters[i]
        par = i % 2
        p.gdma(out=xt2_of(i).rearrange("p (c t) -> p c t", c=KC),
               in_=xlnd.ap()[:, tb * 512:(tb + 1) * 512].rearrange(
                   "(c p) t -> p c t", p=128),
               w=[(s_t, xt_done[par])] if xt_done[par] else [(s_wr, wdram_ready)],
               i=(s_x, 16))
        return s_x.n

    def load_wrow(i):
        e, tb = iters[i]
        par = i % 2
        p.gdma(out=wrow[0:1, par * 512:(par + 1) * 512],
               in_=wdram0.ap()[e:e + 1, tb * 512:(tb + 1) * 512],
               w=[(s_wr, wdram_ready), (s_t, s_t.n)], i=(s_wr, 16))
        return s_wr.n

    xt_ld = {0: load_xt(0)}
    wr_ld = {0: load_wrow(0)}

    def load_w1(e, wv):
        for c in range(KC):
            p.gdma(out=w1r[:, c * 4096:(c + 1) * 4096],
                   in_=w1_d[e, c * 128:(c + 1) * 128, :],
                   w=(wv if c == 0 else ()), i=(s_w, 16))
        return s_w.n

    def load_w2(e, wv):
        for q in range(4):
            p.gdma(out=w2r[:, q * 8 * D:(q + 1) * 8 * D].rearrange(
                       "p (c o) -> p c o", c=8),
                   in_=w2_d[e, q * 8 * 128:(q + 1) * 8 * 128, :].rearrange(
                       "(c p) o -> p c o", p=128),
                   w=(wv if q == 0 else ()), i=(s_w, 16))
        return s_w.n

    w1_ready = load_w1(0, [(s_v, s_v.n), (s_s, s_s.n)])
    w2_ready = load_w2(0, [])
    for e in range(8):
        p.wait("tensor", s_w, w1_ready)
        for tb in range(8):
            i = e * 8 + tb
            par = i % 2
            xt2 = xt2_of(i)
            if i + 1 < 64:
                xt_ld[i + 1] = load_xt(i + 1)
                wr_ld[i + 1] = load_wrow(i + 1)
            p.mm(ps_bc[:, 0:512], env["ones1b"][:],
                 wrow[0:1, par * 512:(par + 1) * 512],
                 w=[(s_wr, wr_ld[i]), (s_s, s_s.n), (s_v, s_v.n)], i=(s_t, 1))
            p.scopy(out=wtile, in_=ps_bc[:, 0:512], w=[(s_t, s_t.n)], i=(s_s, 1))
            p.wait("tensor", s_x, xt_ld[i])
            if tb == 7:
                p.wait("tensor", s_w, w2_ready)
            for fs in range(32):
                bank = fs % 2
                for c in range(KC):
                    wvh = []
                    if c == 0:
                        if gelu_done[bank]:
                            wvh.append((s_s, gelu_done[bank]))
                        if fs == 0:
                            wvh.append((s_s, s_s.n))
                    p.mm(ps_a[:, bank * 512: bank * 512 + 512],
                         w1r[:, c * 4096 + fs * 128: c * 4096 + (fs + 1) * 128],
                         xt2[:, c * 512:(c + 1) * 512],
                         start=(c == 0), stop=(c == KC - 1),
                         w=wvh, i=((s_t, 1) if c == KC - 1 else None))
                wvg = [(s_t, s_t.n)]
                if fs == 0 and hread:
                    wvg.append((s_t, hread))
                p.act(out=hbuf[:, fs * 512:(fs + 1) * 512],
                      in_=ps_a[:, bank * 512: bank * 512 + 512],
                      func=AF.Gelu, bias=b1T[:, e * 32 + fs: e * 32 + fs + 1],
                      w=wvg, i=(s_s, 1))
                gelu_done[bank] = s_s.n
            xt_done[par] = s_t.n
            if tb == 7 and e + 1 < 8:
                w1_ready = load_w1(e + 1, [(s_t, s_t.n)])
            if tb == 0:
                p.wait("tensor", s_w, w2_ready)
            for oc in range(KC):
                bank = oc % 2
                for fc in range(32):
                    wvy = []
                    if fc == 0:
                        wvy.append((s_s, s_s.n))
                        if ysc_ts[bank]:
                            wvy.append((s_v, ysc_ts[bank]))
                    p.mm(ps_o[:, bank * 512: bank * 512 + 512],
                         w2r[:, fc * D + oc * 128: fc * D + (oc + 1) * 128],
                         hbuf[:, fc * 512:(fc + 1) * 512],
                         start=(fc == 0), stop=(fc == 31),
                         w=wvy, i=((s_t, 1) if fc == 31 else None))
                ysc = yscs[oc % 4]
                p.ts(ysc, ps_o[:, bank * 512: bank * 512 + 512],
                     b2T[:, e * 8 + oc: e * 8 + oc + 1], op0=AluOpType.add,
                     w=[(s_t, s_t.n), (s_ac, max(xn_init, ysc_dma[oc % 4]))],
                     i=(s_v, 1))
                ysc_ts[bank] = s_v.n
                p.tt(ysc, ysc, wtile, AluOpType.mult, i=(s_v, 1))
                p.gdma(out=xn[oc * 128:(oc + 1) * 128, tb * 512:(tb + 1) * 512],
                       in_=ysc, accum_op=AluOpType.add,
                       w=[(s_v, s_v.n)], i=(s_ac, 16))
                ysc_dma[oc % 4] = s_ac.n
            hread = s_t.n
            if tb == 7 and e + 1 < 8:
                w2_ready = load_w2(e + 1, [(s_t, s_t.n)])
        wpar = s_t.n

    p.wait("gpsimd", s_ac, s_ac.n)
    p.scope_exit()
    return p


def run_ffn_l0(x_shards, Wg, W1, b1, W2, b2, mode="full"):
    nc = _get_prog("ffn_l0_" + mode, lambda: build_ffn_l0(mode))
    b1T = np.ascontiguousarray(np.asarray(b1, np.float32).reshape(8, 32, 128)
                               .transpose(2, 0, 1).reshape(128, 256))
    b2T = np.ascontiguousarray(np.asarray(b2, np.float32).reshape(8, 8, 128)
                               .transpose(2, 0, 1).reshape(128, 64))
    E8sel = np.zeros((128, 8), np.float32)
    for pp in range(64):
        E8sel[pp, pp % 8] = 1.0
    E8T = np.ascontiguousarray(E8sel.T)
    in_maps = []
    for c in range(NCORES):
        m = {"x": x_shards[c], "Wg": np.asarray(Wg, np.float32),
             "W1": np.asarray(W1, np.float32), "b1T": b1T,
             "W2": np.asarray(W2, np.float32), "b2T": b2T,
             "E8sel": E8sel, "E8T": E8T}
        in_maps.append(m)
    res = run_bass_kernel_spmd(nc, in_maps, core_ids=list(range(NCORES)))
    if mode == "wdump":
        return ([np.asarray(r["xn"]) for r in res.results],
                [(np.asarray(r["wout"]), np.asarray(r["thout"]),
                  np.asarray(r["xlout"]), np.asarray(r["scout"])) for r in res.results])
    return [np.asarray(r["xn"]) for r in res.results]



def build_attn_a_l1(P=None, env=None, xap=None, xnap=None, nm=None):
    p = P if P is not None else Prog()
    nm = nm or {}
    g = lambda n: nm.get(n, n)
    if xap is None:
        x = p.dram("x", [D, T], F32, kind="ExternalInput").ap()
        xn = p.dram("xn", [D, T], F32, kind="ExternalOutput").ap()
    else:
        x, xn = xap, xnap
    wq_d = p.dram(g("W_q"), [D, D], F32, kind="ExternalInput").ap()
    wk_d = p.dram(g("W_k"), [D, D], F32, kind="ExternalInput").ap()
    wv_d = p.dram(g("W_v"), [D, D], F32, kind="ExternalInput").ap()
    wo_d = p.dram(g("W_o"), [D, D], F32, kind="ExternalInput").ap()
    wdn_d = p.dram(g("W_down_kv"), [D, 256], F32, kind="ExternalInput").ap()
    wuk_d = p.dram(g("W_up_k"), [256, D], F32, kind="ExternalInput").ap()
    wuv_d = p.dram(g("W_up_v"), [256, D], F32, kind="ExternalInput").ap()
    e2sel = p.dram("e2sel", [128, 2], F32, kind="ExternalInput").ap()
    dec_d = p.dram("dec28", [128, 28 * 512], F32, kind="ExternalInput").ap()

    if env is None:
        env = make_env(p)
    s_x, s_v, s_s, s_t, s_st, s_w = (env[k] for k in
                                     ("s_x", "s_v", "s_s", "s_t", "s_st", "s_w"))
    p.scope_enter()

    xln = p.sbuf("xln", [128, KC * SEQ], BF16)
    wq = p.sbuf("wq", [128, KC * D], BF16)
    wdn = p.sbuf("wdn", [128, KC * 256], BF16)
    wuk = p.sbuf("wuk", [128, 2 * D], BF16)
    wuv = p.sbuf("wuv", [128, 2 * D], BF16)
    latent = p.sbuf("latent", [128, 2 * SEQ], BF16)
    vtm = p.sbuf("vtm", [128, 16 * D], BF16)
    vplus = p.sbuf("vplus", [128, 16 * 65], BF16)
    e2s = p.sbuf("e2s", [128, 2], BF16)
    negb = p.sbuf("negb", [128, 2], F32)
    rden = p.sbuf("rden", [1, 512], F32)
    bcast = p.sbuf("bcast", [64, 512], BF16)
    attn = p.sbuf("attn", [128, KC * SEQ], BF16)
    R = p.sbuf("Rbuf", [128, 28 * 512], BF16)
    qh = env["xbf"][:, 0:SEQ]
    kh = env["xbf"][:, SEQ:2 * SEQ]
    sqs = env["sqb"][:, 0:SEQ]
    sdec = env["xbuf"][:, 0:1024]
    ptb = env["xbuf"][:, 1024:2048].bitcast(BF16)

    ps_a, ps_o, ps_c = env["ps_a"], env["ps_o"], env["ps_c"]

    for c in range(KC):
        p.gdma(out=wq[:, c * D:(c + 1) * D], in_=wq_d[c * 128:(c + 1) * 128, :], i=(s_w, 16))
        p.gdma(out=wdn[:, c * 256:(c + 1) * 256], in_=wdn_d[c * 128:(c + 1) * 128, :],
               i=(s_w, 16))
    for lc in range(2):
        p.gdma(out=wuk[:, lc * D:(lc + 1) * D], in_=wuk_d[lc * 128:(lc + 1) * 128, :],
               i=(s_w, 16))
        p.gdma(out=wuv[:, lc * D:(lc + 1) * D], in_=wuv_d[lc * 128:(lc + 1) * 128, :],
               i=(s_w, 16))
    p.gdma(out=e2s[:], in_=e2sel, i=(s_w, 16))
    wload_total = s_w.n
    emit_env_init(p, env)

    exp_bank = {0: 0, 1: 0}
    pv_bank = {0: 0, 1: 0, 2: 0, 3: 0}
    dec_bank = {0: 0, 1: 0}

    for b in range(NB):
        for c in range(KC):
            p.gdma(out=attn[:, c * D:(c + 1) * D],
                   in_=wk_d[c * 128:(c + 1) * 128, :],
                   w=[(s_t, s_t.n), (s_v, s_v.n)], i=(s_w, 16))
            p.gdma(out=attn[:, KC * D + c * D:KC * D + (c + 1) * D],
                   in_=wv_d[c * 128:(c + 1) * 128, :], i=(s_w, 16))
        for q4 in range(4):
            p.gdma(out=R[:, q4 * 7 * 512:(q4 + 1) * 7 * 512],
                   in_=dec_d[:, q4 * 7 * 512:(q4 + 1) * 7 * 512],
                   w=[(s_t, s_t.n)], i=(s_w, 16))
        vw_loaded = s_w.n

        for blk in range(4):
            emit_ln_block(p, env, x,
                          lambda c, blk=blk: xln[:, c * SEQ + blk * 512:
                                                 c * SEQ + blk * 512 + 512],
                          b, blk)

        p.wait("tensor", s_w, vw_loaded)
        evrel01 = [s_v.n]
        for blk in range(4):
            for c in range(KC):
                for (woff, bank) in ((0, 0), (KC * D, 1)):
                    for cc in range(KC):
                        p.mm(ps_o[:, bank * 512: bank * 512 + 512],
                             attn[:, woff + cc * D + c * 128: woff + cc * D + (c + 1) * 128],
                             xln[:, cc * SEQ + blk * 512: cc * SEQ + (blk + 1) * 512],
                             start=(cc == 0), stop=(cc == KC - 1),
                             w=([(s_v, evrel01[0])] if cc == 0 else ()),
                             i=((s_t, 1) if cc == KC - 1 else None))
                p.vcopy(out=env["lnscr"][:], in_=ps_o[:, 0:512],
                        w=[(s_t, s_t.n)], i=(s_v, 1))
                p.tt(vtm[:, c * SEQ + blk * 512: c * SEQ + (blk + 1) * 512],
                     env["lnscr"][:], ps_o[:, 512:1024], AluOpType.add,
                     i=(s_v, 1))
                evrel01[0] = s_v.n
        evrelA = {0: s_v.n, 1: s_v.n}
        for blk in range(4):
            for lc in range(2):
                bank = lc % 2
                for c in range(KC):
                    p.mm(ps_a[:, bank * 512: bank * 512 + 512],
                         wdn[:, c * 256 + lc * 128: c * 256 + (lc + 1) * 128],
                         vtm[:, c * SEQ + blk * 512: c * SEQ + (blk + 1) * 512],
                         start=(c == 0), stop=(c == KC - 1),
                         w=([(s_v, evrelA[bank])] if c == 0 else ()),
                         i=((s_t, 1) if c == KC - 1 else None))
                p.vcopy(out=latent[:, lc * SEQ + blk * 512: lc * SEQ + (blk + 1) * 512],
                        in_=ps_a[:, bank * 512: bank * 512 + 512],
                        w=[(s_t, s_t.n)], i=(s_v, 1))
                evrelA[bank] = s_v.n
        evrelB = {0: s_v.n, 1: s_v.n}
        for tc in range(16):
            for half in range(2):
                bank = half % 2
                for lc in range(2):
                    p.mm(ps_o[:, bank * 512: bank * 512 + 512],
                         latent[:, lc * SEQ + tc * 128: lc * SEQ + (tc + 1) * 128],
                         wuv[:, lc * D + half * 512: lc * D + (half + 1) * 512],
                         start=(lc == 0), stop=(lc == 1),
                         w=([(s_v, evrelB[bank])] if lc == 0 else ()),
                         i=((s_t, 1) if lc == 1 else None))
                p.vcopy(out=vtm[:, tc * D + half * 512: tc * D + (half + 1) * 512],
                        in_=ps_o[:, bank * 512: bank * 512 + 512],
                        w=[(s_t, s_t.n)], i=(s_v, 1))
                evrelB[bank] = s_v.n

        for hp in range(8):
            evrelC = {0: s_v.n, 1: s_v.n}
            for qb in range(4):
                bank = qb % 2
                for c in range(KC):
                    p.mm(ps_o[:, bank * 512: bank * 512 + 512],
                         wq[:, c * D + hp * 128: c * D + (hp + 1) * 128],
                         xln[:, c * SEQ + qb * 512: c * SEQ + (qb + 1) * 512],
                         start=(c == 0), stop=(c == KC - 1),
                         w=([(s_v, evrelC[bank])] if c == 0 else ()),
                         i=((s_t, 1) if c == KC - 1 else None))
                p.vcopy(out=qh[:, qb * 512:(qb + 1) * 512],
                        in_=ps_o[:, bank * 512: bank * 512 + 512],
                        w=[(s_t, s_t.n)], i=(s_v, 1))
                evrelC[bank] = s_v.n
            for qb in range(4):
                bank = qb % 2
                for lc in range(2):
                    p.mm(ps_o[:, bank * 512: bank * 512 + 512],
                         wuk[:, lc * D + hp * 128: lc * D + (hp + 1) * 128],
                         latent[:, lc * SEQ + qb * 512: lc * SEQ + (qb + 1) * 512],
                         start=(lc == 0), stop=(lc == 1),
                         w=([(s_v, evrelC[bank])] if lc == 0 else ()),
                         i=((s_t, 1) if lc == 1 else None))
                p.vcopy(out=kh[:, qb * 512:(qb + 1) * 512],
                        in_=ps_o[:, bank * 512: bank * 512 + 512],
                        w=[(s_t, s_t.n)], i=(s_v, 1))
                evrelC[bank] = s_v.n

            for (src, col) in ((qh, 0), (kh, 1)):
                p.act(out=sqs[:], in_=src[:], func=AF.Square,
                      w=[(s_v, s_v.n)], i=(s_s, 1))
                for j in range(2):
                    slot = 8 + 2 * col + j
                    for qb in range(4):
                        p.mm(ps_a[0:1, (qb % 2) * 512:(qb % 2) * 512 + 512],
                             e2s[:, j:j + 1], sqs[:, qb * 512:(qb + 1) * 512],
                             start=True, stop=True,
                             w=[(s_s, s_s.n), (s_v, s_v.n)], i=(s_t, 1))
                        p.vredmax(env["small"][0:1, slot:slot + 1]
                                  if qb == 0 else env["small"][0:1, 12:13],
                                  ps_a[0:1, (qb % 2) * 512:(qb % 2) * 512 + 512],
                                  w=[(s_t, s_t.n)], i=(s_v, 1))
                        if qb > 0:
                            p.tt(env["small"][0:1, slot:slot + 1],
                                 env["small"][0:1, slot:slot + 1],
                                 env["small"][0:1, 12:13], AluOpType.max, i=(s_v, 1))
            for j in range(2):
                pj = env["small"][0:1, 0 + j:1 + j]
                p.tt(pj, env["small"][0:1, 8 + j:9 + j],
                     env["small"][0:1, 10 + j:11 + j], AluOpType.mult,
                     w=[(s_v, s_v.n)], i=(s_v, 1))
                p.act(out=pj, in_=pj, func=AF.Ln, w=[(s_v, s_v.n)], i=(s_s, 1))
                p.act(out=pj, in_=pj, func=AF.Exp, scale=0.5, i=(s_s, 1))
                p.ts(pj, pj, -1.0 / 32, w=[(s_s, s_s.n)], i=(s_v, 1))
                p.mm(ps_c[:, 0:1], env["ones1f"][:], pj, w=[(s_v, s_v.n)], i=(s_t, 1))
                p.vcopy(negb[:, j:j + 1], ps_c[:, 0:1], w=[(s_t, s_t.n)], i=(s_v, 1))

            for j in range(2):
                hg = hp * 2 + j
                p.vmemset(vplus[:], 1.0, w=[(s_t, s_t.n)], i=(s_v, 1))
                for tc in range(16):
                    p.vcopy(vplus[:, tc * 65: tc * 65 + 64],
                            vtm[:, tc * D + hg * 64: tc * D + (hg + 1) * 64],
                            i=(s_v, 1))
                vplus_ready = s_v.n
                for qc in range(4):
                    pend = []

                    def emit_pv(kt, slot, wv3, qc=qc):
                        p.mm(ps_o[0:65, (qc % 2) * 512:(qc % 2) * 512 + 512],
                             vplus[:, kt * 65:(kt + 1) * 65],
                             ptb[:, slot * 512:(slot + 1) * 512],
                             start=(kt == 0), stop=(kt == 15),
                             w=wv3, i=(s_t, 1))
                        pv_bank[slot] = s_t.n

                    for kt in range(16):
                        par = kt % 2
                        slot = kt % 4
                        wv = [(s_v, max(vplus_ready, dec_bank[par]))]
                        p.mm(ps_a[:, par * 512:(par + 1) * 512],
                             kh[j * 64:(j + 1) * 64, kt * 128:(kt + 1) * 128],
                             qh[j * 64:(j + 1) * 64, qc * 512:(qc + 1) * 512],
                             w=wv, i=(s_t, 1))
                        smm_done = s_t.n
                        p.tt(sdec[:, par * 512:(par + 1) * 512],
                             ps_a[:, par * 512:(par + 1) * 512],
                             R[:, (kt - 4 * qc + 12) * 512:(kt - 4 * qc + 13) * 512],
                             AluOpType.mult,
                             w=[(s_t, smm_done)] + ([(s_s, exp_bank[par])] if exp_bank[par] else []),
                             i=(s_v, 1))
                        dec_bank[par] = s_v.n
                        dec_done = s_v.n
                        wv2 = [(s_v, dec_done)]
                        if pv_bank[slot]:
                            wv2.append((s_t, pv_bank[slot]))
                        p.act(out=ptb[:, slot * 512:(slot + 1) * 512],
                              in_=sdec[:, par * 512:(par + 1) * 512],
                              func=AF.Exp, scale=1.0 / 32, bias=negb[:, j:j + 1],
                              w=wv2, i=(s_s, 1))
                        exp_bank[par] = s_s.n
                        wv3 = [(s_s, s_s.n)]
                        if kt == 0:
                            wv3.append((s_v, s_v.n))
                        pend.append((kt, slot, wv3))
                        if len(pend) > 2:
                            for a in pend[:-2]:
                                emit_pv(*a)
                            pend = pend[-2:]
                    for a in pend:
                        emit_pv(*a)
                    p.vrecip(rden[:], ps_o[64:65, (qc % 2) * 512:(qc % 2) * 512 + 512],
                             w=[(s_t, s_t.n)], i=(s_v, 1))
                    p.mm(ps_c[0:64, 0:512], env["ones1f"][0:1, 0:64], rden[:],
                         w=[(s_v, s_v.n)], i=(s_t, 1))
                    p.scopy(out=bcast[:], in_=ps_c[0:64, 0:512],
                            w=[(s_t, s_t.n)], i=(s_s, 1))
                    p.tt(attn[(hg % 2) * 64:(hg % 2) * 64 + 64,
                              (hg // 2) * SEQ + qc * 512:(hg // 2) * SEQ + (qc + 1) * 512],
                         ps_o[0:64, (qc % 2) * 512:(qc % 2) * 512 + 512],
                         bcast[:], AluOpType.mult,
                         w=[(s_s, s_s.n)], i=(s_v, 1))

        for c in range(KC):
            p.gdma(out=R[:, c * D:(c + 1) * D], in_=wo_d[c * 128:(c + 1) * 128, :],
                   w=[(s_s, s_s.n), (s_v, s_v.n)], i=(s_w, 16))
        wo_loaded = s_w.n
        p.wait("tensor", s_w, wo_loaded)
        for blk in range(4):
            tof = b * SEQ + blk * 512
            p.gdma(out=env["xbuf"][:].rearrange("p (c t) -> p c t", c=KC),
                   in_=x[:, tof:tof + 512].rearrange("(c p) t -> p c t", p=128),
                   w=[(s_v, s_v.n), (s_st, s_st.n)], i=(s_x, 16))
            store_done = {}
            evrelD = {0: s_v.n, 1: s_v.n}
            for oc in range(KC):
                bank = oc % 2
                for c in range(KC):
                    p.mm(ps_o[:, bank * 512: bank * 512 + 512],
                         R[:, c * D + oc * 128: c * D + (oc + 1) * 128],
                         attn[:, c * SEQ + blk * 512: c * SEQ + (blk + 1) * 512],
                         start=(c == 0), stop=(c == KC - 1),
                         w=([(s_v, evrelD[bank])] if c == 0 else ()),
                         i=((s_t, 1) if c == KC - 1 else None))
                wv = [(s_t, s_t.n), (s_x, s_x.n)]
                if oc - 2 in store_done:
                    wv.append((s_st, store_done[oc - 2]))
                oslot = latent[:, bank * 1024:(bank + 1) * 1024].bitcast(F32)
                p.tt(oslot,
                     ps_o[:, bank * 512: bank * 512 + 512],
                     env["xbuf"][:, oc * 512:(oc + 1) * 512], AluOpType.add,
                     w=wv, i=(s_v, 1))
                evrelD[bank] = s_v.n
                p.sydma(out=xn[oc * 128:(oc + 1) * 128, tof:tof + 512],
                        in_=oslot,
                        w=[(s_v, s_v.n)], i=(s_st, 16))
                store_done[oc] = s_st.n

    p.wait("sync", s_st, s_st.n)
    p.scope_exit()
    return p



def build_ffn_l1(mode="full", P=None, env=None, xap=None, xnap=None, nm=None):
    p = P if P is not None else Prog()
    nm = nm or {}
    g = lambda n: nm.get(n, n)
    if xap is None:
        x = p.dram("x", [D, T], F32, kind="ExternalInput").ap()
        xn = p.dram("xn", [D, T], F32, kind="ExternalOutput").ap()
    else:
        x, xn = xap, xnap
    wout = p.dram("wout", [8, T], BF16, kind="ExternalOutput").ap() if mode in ("wdump", "route", "routemin") else None
    aout = p.dram("aout", [128, 32 * 16], F32, kind="ExternalOutput").ap() if mode in ("wdump", "route", "routemin") else None
    adbg = p.sbuf("adbg", [128, 32 * 16], F32) if mode in ("wdump", "route", "routemin") else None
    mout = p.dram("mout", [128, 1024 + 64 + 8], BF16, kind="ExternalOutput").ap() if mode in ("wdump", "route", "routemin") else None
    dout = p.dram("dscr", [128, 16], F32) if mode in ("wdump", "route", "routemin") else None
    dout2 = (p.dram("dout", [128, 16], F32, kind="ExternalOutput").ap()
             if mode in ("wdump", "route", "routemin") else None)
    dout3 = (p.dram("dout3", [128, 8], BF16, kind="ExternalOutput").ap()
             if mode in ("wdump", "route", "routemin") else None)
    dout4 = (p.dram("dout4", [8, 512], BF16, kind="ExternalOutput").ap()
             if mode in ("wdump", "route", "routemin") else None)
    mdbg = p.sbuf("mdbg", [128, 1024 + 64 + 8], BF16) if mode in ("wdump", "route", "routemin") else None
    rwT_d = p.dram("routing_WT", [D, 8], F32, kind="ExternalInput").ap()
    eb_d = p.dram("expert_bias128", [128, 8], F32, kind="ExternalInput").ap()
    sw1_d = p.dram(g("sW1"), [D, 512], F32, kind="ExternalInput").ap()
    sb1_d = p.dram("sb1T", [128, 4], F32, kind="ExternalInput").ap()
    sw2_d = p.dram(g("sW2"), [512, D], F32, kind="ExternalInput").ap()
    sb2_d = p.dram("sb2T", [128, 8], F32, kind="ExternalInput").ap()
    rw1_d = p.dram(g("rW1"), [8, D, 512], F32, kind="ExternalInput").ap()
    rb1_d = p.dram("rb1T", [128, 32], F32, kind="ExternalInput").ap()
    rw2_d = p.dram(g("rW2"), [8, 512, D], F32, kind="ExternalInput").ap()
    rb2_d = p.dram("rb2T", [128, 64], F32, kind="ExternalInput").ap()
    id_d = p.dram("ident", [128, 128], F32, kind="ExternalInput").ap()
    wdram = p.dram("wdram", [8, T], BF16)

    if env is None:
        env = make_env(p)
    s_x, s_v, s_s, s_t, s_st, s_w = (env[k] for k in
                                     ("s_x", "s_v", "s_s", "s_t", "s_st", "s_w"))
    s_ac = env["s_ac"]
    s_wr = env["s_wr"]
    p.scope_enter()

    xln = p.sbuf("xln", [128, KC * T], BF16)
    sw1b = p.sbuf("sw1b", [128, KC * 512], BF16)
    sw2b = p.sbuf("sw2b", [128, 4 * D], BF16)
    rw1b = p.sbuf("rw1b", [128, 2 * KC * 512], BF16)
    rw2b = p.sbuf("rw2b", [128, 2 * 4 * D], BF16)
    rwT = p.sbuf("rwT", [128, KC * 8], BF16)
    hb = p.sbuf("hb", [128, 4 * 512], BF16)
    wfm = p.sbuf("wfm", [8, T], BF16)
    identb = p.sbuf("identb", [128, 128], BF16)
    sb1 = p.sbuf("sb1", [128, 4], F32)
    sb2 = p.sbuf("sb2", [128, 8], F32)
    rb1 = p.sbuf("rb1", [128, 32], F32)
    rb2 = p.sbuf("rb2", [128, 64], F32)
    ebias = p.sbuf("ebias", [128, 8], F32)
    onecol = p.sbuf("onecol", [128, 1], F32)
    lg8 = p.sbuf("lg8", [128, 48], F32)
    t_lg = p.sbuf("t_lg", [128, 8], F32)
    t_a = p.sbuf("t_a", [128, 8], F32)
    t_eq = p.sbuf("t_eq", [128, 8], F32)
    t_am = p.sbuf("t_am", [128, 8], F32)
    t_m1 = p.sbuf("t_m1", [128, 1], F32)
    t_m2 = p.sbuf("t_m2", [128, 1], F32)
    t_sc = p.sbuf("t_sc", [128, 8], F32)
    t_sc2 = p.sbuf("t_sc2", [128, 8], F32)
    wtm8 = p.sbuf("wtm8", [128, 8], BF16)
    wrow = p.sbuf("wrow", [1, 2 * 512], BF16)
    wtile = env["xbuf"][:, 0:512]
    yscs = [env["xbuf"][:, 512 + i * 512: 1024 + i * 512] for i in range(4)]

    ps_a, ps_o, ps_c, ps_bc = env["ps_a"], env["ps_o"], env["ps_c"], env["ps_bc"]
    ps_t = p.psum("ps_t", [8, 128], BF16)

    for c in range(KC):
        p.gdma(out=sw1b[:, c * 512:(c + 1) * 512], in_=sw1_d[c * 128:(c + 1) * 128, :],
               i=(s_w, 16))
        p.gdma(out=rwT[:, c * 8:(c + 1) * 8], in_=rwT_d[c * 128:(c + 1) * 128, :],
               i=(s_w, 16))
    for fc in range(4):
        p.gdma(out=sw2b[:, fc * D:(fc + 1) * D], in_=sw2_d[fc * 128:(fc + 1) * 128, :],
               i=(s_w, 16))
    for (dst, srcd) in ((sb1, sb1_d), (sb2, sb2_d), (rb1, rb1_d), (rb2, rb2_d)):
        p.gdma(out=dst[:], in_=srcd, i=(s_w, 16))
    p.gdma(out=identb[:], in_=id_d, i=(s_w, 16))
    p.gdma(out=xn, in_=x, i=(s_ac, 16))
    xn_init = s_ac.n
    p.gdma(out=ebias[:], in_=eb_d, i=(s_w, 16))
    wload_total = s_w.n
    emit_env_init(p, env)
    p.vmemset(onecol[:], 1.0, i=(s_v, 1))
    p.wait("vector", s_w, wload_total)

    NGB = 1 if mode == "routemin" else 8
    tr_done = 0
    for gblk in range(NGB):
        b, blk = divmod(gblk, 4)

        emit_ln_block(p, env, x,
                      lambda c, gblk=gblk: xln[:, c * T + gblk * 512:
                                               c * T + gblk * 512 + 512],
                      b, blk)
        lgbank = [ps_c[:, 0:8], ps_a[:, 0:8], ps_a[:, 512:520], ps_bc[:, 0:8]]
        for s4 in range(4):
            for c in range(KC):
                p.mm(lgbank[s4],
                     xln[:, c * T + gblk * 512 + s4 * 128: c * T + gblk * 512 + (s4 + 1) * 128],
                     rwT[:, c * 8:(c + 1) * 8],
                     start=(c == 0), stop=(c == KC - 1),
                     w=([(s_v, s_v.n), (s_s, s_s.n)] if c == 0 else ()),
                     i=((s_t, 1) if c == KC - 1 else None))
        if mdbg is not None and gblk == 0:
            for c in range(KC):
                p.vcopy(mdbg[:, c * 128:(c + 1) * 128],
                        xln[:, c * T: c * T + 128], w=[(s_t, s_t.n)], i=(s_v, 1))
            p.vcopy(mdbg[:, 1024:1088], rwT[:, 0:64], i=(s_v, 1))
            p.vcopy(mdbg[:, 1088:1096], ps_c[:, 0:8], i=(s_v, 1))
            p.vcopy(mdbg[:, 1060:1068], ebias[:], i=(s_v, 1))
        for s4 in range(4 if mode != "routemin" else 1):
            lg = t_lg[:]
            a = t_a[:]
            eq = t_eq[:]
            am = t_am[:]
            m1 = t_m1[:]
            m2 = t_m2[:]
            sc1 = t_sc[:]
            def VS(fn):
                fn()
                p.wait("vector", s_v, s_v.n)
                p.wait("scalar", s_s, s_s.n)
                p.wait("vector", s_s, s_s.n)
                p.wait("scalar", s_v, s_v.n)
            VS(lambda: p.tt(lg, lgbank[s4], ebias[:], AluOpType.add,
                            w=[(s_t, s_t.n), (s_s, s_s.n)], i=(s_v, 1)))
            VS(lambda: p.act(out=sc1, in_=lg, func=AF.Abs, i=(s_s, 1)))
            VS(lambda: p.act(out=t_sc2[:], in_=sc1, func=AF.Exp, scale=-1.0, i=(s_s, 1)))
            VS(lambda: p.act(out=sc1, in_=t_sc2[:], func=AF.Ln, bias=onecol[:], i=(s_s, 1)))
            VS(lambda: p.ts(a, lg, 0.0, op0=AluOpType.max, i=(s_v, 1)))
            VS(lambda: p.tt(am, a, sc1, AluOpType.add, i=(s_v, 1)))
            VS(lambda: p.act(out=t_sc2[:], in_=am, func=AF.Ln, i=(s_s, 1)))
            VS(lambda: p.act(out=a, in_=t_sc2[:], func=AF.Exp, scale=0.5, i=(s_s, 1)))
            VS(lambda: p.vredmax(m1, a, i=(s_v, 1)))
            VS(lambda: p.ts(eq, a, m1, op0=AluOpType.is_equal, i=(s_v, 1)))
            VS(lambda: p.ts(am, eq, 1e30, op0=AluOpType.mult, i=(s_v, 1)))
            VS(lambda: p.tt(am, a, am, AluOpType.subtract, i=(s_v, 1)))
            VS(lambda: p.vredmax(m2, am, i=(s_v, 1)))
            VS(lambda: p.ts(eq, a, m2, op0=AluOpType.is_ge, i=(s_v, 1)))
            VS(lambda: p.tt(wtm8[:], a, eq, AluOpType.mult, i=(s_v, 1)))
            if adbg is not None:
                p.vcopy(adbg[:, (gblk * 4 + s4) * 16 + 8:(gblk * 4 + s4) * 16 + 16], a,
                        i=(s_v, 1))
                if gblk == 1 and s4 == 0:
                    p.vcopy(adbg[:, 100 * 4:100 * 4 + 8], eq, i=(s_v, 1))
                    p.vcopy(adbg[:, 103 * 4:103 * 4 + 8], am, i=(s_v, 1))
                    p.vcopy(adbg[:, 106 * 4:106 * 4 + 1], m1, i=(s_v, 1))
                    p.vcopy(adbg[:, 107 * 4:107 * 4 + 1], m2, i=(s_v, 1))
            if mode == "routemin":
                continue
            p.transpose(ps_t[:], wtm8[:], identb[:],
                        w=[(s_v, s_v.n), (s_v, tr_done)], i=(s_t, 1))
            p.vcopy(wfm[0:8, gblk * 512 + s4 * 128: gblk * 512 + (s4 + 1) * 128],
                    ps_t[:], w=[(s_t, s_t.n)], i=(s_v, 1))
            tr_done = s_v.n
    if mode != "routemin":
        p.gdma(out=wdram.ap(), in_=wfm[:], w=[(s_v, s_v.n)], i=(s_w, 16))
    wdram_ready = s_w.n

    ysc_dma = {k: 0 for k in range(4)}
    ysc_ts = {0: 0, 1: 0}
    sil_done = {0: 0, 1: 0}
    hread = [0]
    wr_ld = {}

    def load_wrow1(wexp, tb):
        p.gdma(out=wrow[0:1, (tb % 2) * 512:(tb % 2 + 1) * 512],
               in_=wdram.ap()[wexp:wexp + 1, tb * 512:(tb + 1) * 512],
               w=[(s_w, wdram_ready), (s_t, s_t.n)], i=(s_wr, 16))
        return s_wr.n

    def ffn_pass(widx1, widx2, b1ap, b2ap, wexp):
        if wexp is not None:
            wr_ld[0] = load_wrow1(wexp, 0)
        for tb in range(8):
            if wexp is not None:
                par = tb % 2
                if tb + 1 < 8:
                    wr_ld[tb + 1] = load_wrow1(wexp, tb + 1)
                p.mm(ps_bc[:, 0:512], env["ones1b"][:],
                     wrow[0:1, par * 512:(par + 1) * 512],
                     w=[(s_wr, wr_ld[tb]), (s_s, s_s.n), (s_v, s_v.n)], i=(s_t, 1))
                p.scopy(out=wtile, in_=ps_bc[:, 0:512], w=[(s_t, s_t.n)], i=(s_s, 1))
            for fs in range(4):
                bank = fs % 2
                for c in range(KC):
                    wvh = []
                    if c == 0:
                        if sil_done[bank]:
                            wvh.append((s_s, sil_done[bank]))
                        if fs == 0:
                            wvh.append((s_s, s_s.n))
                    p.mm(ps_a[:, bank * 512: bank * 512 + 512],
                         widx1[:, c * 512 + fs * 128: c * 512 + (fs + 1) * 128],
                         xln[:, c * T + tb * 512: c * T + (tb + 1) * 512],
                         start=(c == 0), stop=(c == KC - 1),
                         w=wvh, i=((s_t, 1) if c == KC - 1 else None))
                wvg = [(s_t, s_t.n)]
                if fs == 0 and hread[0]:
                    wvg.append((s_t, hread[0]))
                p.act(out=hb[:, fs * 512:(fs + 1) * 512],
                      in_=ps_a[:, bank * 512: bank * 512 + 512],
                      func=AF.Silu, bias=b1ap[:, fs:fs + 1],
                      w=wvg, i=(s_s, 1))
                sil_done[bank] = s_s.n
            for oc in range(KC):
                bank = oc % 2
                for fc in range(4):
                    wvy = []
                    if fc == 0:
                        wvy.append((s_s, s_s.n))
                        if ysc_ts[bank]:
                            wvy.append((s_v, ysc_ts[bank]))
                    p.mm(ps_o[:, bank * 512: bank * 512 + 512],
                         widx2[:, fc * D + oc * 128: fc * D + (oc + 1) * 128],
                         hb[:, fc * 512:(fc + 1) * 512],
                         start=(fc == 0), stop=(fc == 3),
                         w=wvy, i=((s_t, 1) if fc == 3 else None))
                ysc = yscs[oc % 4]
                p.ts(ysc, ps_o[:, bank * 512: bank * 512 + 512], b2ap[:, oc:oc + 1],
                     op0=AluOpType.add,
                     w=[(s_t, s_t.n), (s_ac, max(xn_init, ysc_dma[oc % 4]))], i=(s_v, 1))
                ysc_ts[bank] = s_v.n
                if wexp is not None:
                    p.tt(ysc, ysc, wtile, AluOpType.mult, i=(s_v, 1))
                wv = [(s_v, s_v.n)]
                tof = tb * 512
                p.gdma(out=xn[oc * 128:(oc + 1) * 128, tof:tof + 512],
                       in_=ysc, accum_op=AluOpType.add, w=wv, i=(s_ac, 16))
                ysc_dma[oc % 4] = s_ac.n
            hread[0] = s_t.n

    p.wait("tensor", s_w, wload_total)
    if mode not in ("route", "routemin"):
        ffn_pass(sw1b, sw2b, sb1, sb2, None)

    if mode in ("wdump", "route", "routemin"):
        if mode != "routemin":
            p.gdma(out=wout, in_=wdram.ap(), w=[(s_w, wdram_ready)], i=(s_ac, 16))
        else:
            p.gdma(out=wout, in_=wfm[:], w=[(s_v, s_v.n)], i=(s_ac, 16))
        p.gdma(out=aout, in_=adbg[:], w=[(s_v, s_v.n)], i=(s_ac, 16))
        p.gdma(out=mout, in_=mdbg[:], i=(s_ac, 16))
        p.gdma(out=dout.ap()[0:128, 0:8], in_=t_a[:], i=(s_ac, 16))
        p.gdma(out=dout.ap()[0:128, 8:16], in_=t_eq[:], i=(s_ac, 16))
        p.gdma(out=dout2, in_=dout.ap(), w=[(s_ac, s_ac.n)], i=(s_ac, 16))
        p.gdma(out=dout3, in_=wtm8[:], i=(s_ac, 16))
        p.gdma(out=dout4, in_=wfm[0:8, 0:512], i=(s_ac, 16))
    wpar_done = {0: 0, 1: 0}
    for e in range(8) if mode in ("full", "wdump") else []:
        par = e % 2
        wv = [(s_t, wpar_done[par])] if wpar_done[par] else []
        for c in range(KC):
            p.gdma(out=rw1b[:, par * KC * 512 + c * 512: par * KC * 512 + (c + 1) * 512],
                   in_=rw1_d[e, c * 128:(c + 1) * 128, :],
                   w=(wv if c == 0 else ()), i=(s_w, 16))
        for fc in range(4):
            p.gdma(out=rw2b[:, par * 4 * D + fc * D: par * 4 * D + (fc + 1) * D],
                   in_=rw2_d[e, fc * 128:(fc + 1) * 128, :], i=(s_w, 16))
        p.wait("tensor", s_w, s_w.n)
        ffn_pass(rw1b[:, par * KC * 512:(par + 1) * KC * 512],
                 rw2b[:, par * 4 * D:(par + 1) * 4 * D],
                 rb1[:, e * 4:(e + 1) * 4], rb2[:, e * 8:(e + 1) * 8], e)
        wpar_done[par] = s_t.n

    p.wait("gpsimd", s_ac, s_ac.n)
    p.scope_exit()
    return p


def run_ffn_l1(x_shards, routing_W, expert_bias, sW1, sb1, sW2, sb2, rW1, rb1, rW2, rb2,
               mode="full"):
    nc = _get_prog("ffn_l1_" + mode, lambda: build_ffn_l1(mode))
    rwT = np.ascontiguousarray(np.asarray(routing_W, np.float32).T)
    sb1T = np.ascontiguousarray(np.asarray(sb1, np.float32).reshape(4, 128).T)
    sb2T = np.ascontiguousarray(np.asarray(sb2, np.float32).reshape(8, 128).T)
    rb1T = np.ascontiguousarray(np.asarray(rb1, np.float32).reshape(8, 4, 128)
                                .transpose(2, 0, 1).reshape(128, 32))
    rb2T = np.ascontiguousarray(np.asarray(rb2, np.float32).reshape(8, 8, 128)
                                .transpose(2, 0, 1).reshape(128, 64))
    ident = np.eye(128, dtype=np.float32)
    in_maps = []
    for c in range(NCORES):
        m = {"x": x_shards[c], "routing_WT": rwT,
             "expert_bias128": np.ascontiguousarray(
                 np.broadcast_to(np.asarray(expert_bias, np.float32).reshape(1, 8), (128, 8))),
             "sW1": np.asarray(sW1, np.float32), "sb1T": sb1T,
             "sW2": np.asarray(sW2, np.float32), "sb2T": sb2T,
             "rW1": np.asarray(rW1, np.float32), "rb1T": rb1T,
             "rW2": np.asarray(rW2, np.float32), "rb2T": rb2T,
             "ident": ident}
        in_maps.append(m)
    res = run_bass_kernel_spmd(nc, in_maps, core_ids=list(range(NCORES)))
    if mode in ("wdump", "route", "routemin"):
        return ([np.asarray(r["xn"]) for r in res.results],
                [(np.asarray(r["wout"]), np.asarray(r["aout"]), np.asarray(r["mout"]),
                  np.asarray(r["dout"]), np.asarray(r["dout3"]),
                  np.asarray(r["dout4"])) for r in res.results])
    return [np.asarray(r["xn"]) for r in res.results]


def _host_consts_attn_a():
    lam = 0.01
    dec = np.zeros((128, 28 * 512), np.float32)
    p_i = np.arange(128)[:, None]
    f_i = np.arange(512)[None, :]
    for im, m in enumerate(range(-12, 16)):
        dec[:, im * 512:(im + 1) * 512] = np.exp(-lam * np.abs(m * 128 + p_i - f_i))
    return dec


def run_attn_a_l1(x_shards, Wq, Wk, Wv, Wo, Wdn, Wuk, Wuv):
    consts = _host_consts_attn()
    dec28 = _host_consts_attn_a()
    nc = _get_prog("attn_a_l1", lambda: build_attn_a_l1())
    in_maps = []
    for c in range(NCORES):
        m = {"x": x_shards[c], "W_q": Wq, "W_k": Wk, "W_v": Wv, "W_o": Wo,
             "W_down_kv": Wdn, "W_up_k": Wuk, "W_up_v": Wuv,
             "e2sel": consts["e2sel"], "dec28": dec28}
        in_maps.append(m)
    res = run_bass_kernel_spmd(nc, in_maps, core_ids=list(range(NCORES)))
    return [np.asarray(r["xn"]) for r in res.results]


def _host_consts_attn():
    inv = 1.0 / (10000.0 ** (np.arange(0, 64, 2, dtype=np.float64) / 64))
    t = np.arange(SEQ, dtype=np.float64)
    fr = np.outer(t, inv)
    cos = np.cos(fr).astype(np.float32)
    sin = np.sin(fr).astype(np.float32)
    cos64 = np.concatenate([cos.T, cos.T], 0)
    sin64 = np.concatenate([sin.T, sin.T], 0)
    cos128 = np.concatenate([cos64, cos64], 0)
    sin128 = np.concatenate([sin64, sin64], 0)
    e2sel = np.zeros((128, 2), np.float32)
    e2sel[0:64, 0] = 1
    e2sel[64:128, 1] = 1
    cmask = np.zeros((128, 4 * 512), np.float32)
    for r in range(4):
        ki = np.arange(128)[:, None] + r * 128
        qi = np.arange(512)[None, :]
        cmask[:, r * 512:(r + 1) * 512] = (ki <= qi).astype(np.float32)
    return dict(cos_t=np.ascontiguousarray(cos128), sin_t=np.ascontiguousarray(sin128),
                e2sel=e2sel, cmask=cmask)



_PROG_CACHE = {}


def _get_prog(name, builder):
    if name not in _PROG_CACHE:
        p = builder()
        _PROG_CACHE[name] = p.build()
    return _PROG_CACHE[name]


def _shard_x(full):
    out = []
    for c in range(NCORES):
        xc = np.asarray(full[2 * c:2 * c + 2], np.float32)
        out.append(np.ascontiguousarray(xc.transpose(2, 0, 1).reshape(D, T)))
    return out


def _unshard_x(shards):
    full = np.empty((16, SEQ, D), np.float32)
    for c in range(NCORES):
        full[2 * c:2 * c + 2] = np.asarray(shards[c]).reshape(D, NB, SEQ).transpose(1, 2, 0)
    return full


def run_attn_b_l0(x_shards, w_qkv, w_o):
    consts = _host_consts_attn()
    nc = _get_prog("attn_b_l0", lambda: build_attn_b("l0"))
    in_maps = []
    for c in range(NCORES):
        m = {"x": x_shards[c], "w_qkv": w_qkv, "w_o": w_o,
             "cos_t": consts["cos_t"], "sin_t": consts["sin_t"],
             "e2sel": consts["e2sel"]}
        in_maps.append(m)
    res = run_bass_kernel_spmd(nc, in_maps, core_ids=list(range(NCORES)))
    return [np.asarray(r["xn"]) for r in res.results]


def run_attn_b_l1(x_shards, Wq, Wk, Wv, Wo, qnw, knw):
    consts = _host_consts_attn()
    nc = _get_prog("attn_b_l1", lambda: build_attn_b("l1"))
    in_maps = []
    for c in range(NCORES):
        m = {"x": x_shards[c], "Wq": Wq, "Wk": Wk, "Wv": Wv, "Wo": Wo,
             "q_norm_w": np.asarray(qnw, np.float32).reshape(1, 64),
             "k_norm_w": np.asarray(knw, np.float32).reshape(1, 64)}
        m.update(consts)
        in_maps.append(m)
    res = run_bass_kernel_spmd(nc, in_maps, core_ids=list(range(NCORES)))
    return [np.asarray(r["xn"]) for r in res.results]


def build_mega():
    p = Prog()
    x0 = p.dram("x0", [D, T], F32, kind="ExternalInput").ap()
    xn = p.dram("xn", [D, T], F32, kind="ExternalOutput").ap()
    mids = [p.dram("xmid", [D, T], F32).ap() for _ in range(7)]
    env = make_env(p)
    chain = [x0] + mids + [xn]
    specs = [
        (build_attn_a_l0, dict(nm={"q_w": "L0_attn_a_q_w", "k_w": "L0_attn_a_k_w",
                                   "v_w": "L0_attn_a_v_w", "out_w": "L0_attn_a_out_w"})),
        (build_ffn_l0, dict(nm={"Wg": "L0_ffn_Wg", "W1": "L0_ffn_W1",
                                "W2": "L0_ffn_W2"})),
        (lambda **kw: build_attn_b("l0", **kw),
         dict(nm={"w_qkv": "L0_attn_b_w_qkv", "w_o": "L0_attn_b_w_o"})),
        (build_ffn_l0, dict(nm={"Wg": "L0_ffn_Wg", "W1": "L0_ffn_W1",
                                "W2": "L0_ffn_W2"})),
        (build_attn_a_l1, dict(nm={"W_q": "L1_attn_a_W_q", "W_k": "L1_attn_a_W_k",
                                   "W_v": "L1_attn_a_W_v", "W_o": "L1_attn_a_W_o",
                                   "W_down_kv": "L1_attn_a_W_down_kv",
                                   "W_up_k": "L1_attn_a_W_up_k",
                                   "W_up_v": "L1_attn_a_W_up_v"})),
        (build_ffn_l1, dict(nm={"sW1": "L1_ffn_sW1", "sW2": "L1_ffn_sW2",
                                "rW1": "L1_ffn_rW1", "rW2": "L1_ffn_rW2"})),
        (lambda **kw: build_attn_b("l1", **kw),
         dict(nm={"Wq": "L1_attn_b_Wq", "Wk": "L1_attn_b_Wk", "Wv": "L1_attn_b_Wv",
                  "Wo": "L1_attn_b_Wo", "q_norm_w": "L1_attn_b_q_norm_w",
                  "k_norm_w": "L1_attn_b_k_norm_w"})),
        (build_ffn_l1, dict(nm={"sW1": "L1_ffn_sW1", "sW2": "L1_ffn_sW2",
                                "rW1": "L1_ffn_rW1", "rW2": "L1_ffn_rW2"})),
    ]
    for i, (fn, kw) in enumerate(specs):
        fn(P=p, env=env, xap=chain[i], xnap=chain[i + 1], **kw)
        emit_barrier(p, env)
    return p


def _host_all_arrays(inputs):
    g = lambda k: np.asarray(inputs[k], np.float32)
    m = {}
    consts = _host_consts_attn()
    m["cos_t"] = consts["cos_t"]
    m["sin_t"] = consts["sin_t"]
    m["e2sel"] = consts["e2sel"]
    m["cmask"] = consts["cmask"]
    m["dec28"] = _host_consts_attn_a()
    for k in ("L0_attn_a_q_w", "L0_attn_a_k_w", "L0_attn_a_v_w", "L0_attn_a_out_w",
              "L0_ffn_Wg", "L0_ffn_W1", "L0_ffn_W2",
              "L0_attn_b_w_qkv", "L0_attn_b_w_o",
              "L1_attn_a_W_q", "L1_attn_a_W_k", "L1_attn_a_W_v", "L1_attn_a_W_o",
              "L1_attn_a_W_down_kv", "L1_attn_a_W_up_k", "L1_attn_a_W_up_v",
              "L1_ffn_sW1", "L1_ffn_sW2", "L1_ffn_rW1", "L1_ffn_rW2",
              "L1_attn_b_Wq", "L1_attn_b_Wk", "L1_attn_b_Wv", "L1_attn_b_Wo"):
        m[k] = g(k)
    m["L1_attn_b_q_norm_w"] = g("L1_attn_b_q_norm_w").reshape(1, 64)
    m["L1_attn_b_k_norm_w"] = g("L1_attn_b_k_norm_w").reshape(1, 64)
    qkbT = np.zeros((128, 16), np.float32)
    qkbT[:, 0:8] = g("L0_attn_a_q_b").reshape(8, 128).T
    qkbT[:, 8:16] = g("L0_attn_a_k_b").reshape(8, 128).T
    m["qkbT"] = qkbT
    gboT = np.zeros((128, 24), np.float32)
    gboT[:, 0:8] = g("L0_attn_a_gamma").reshape(8, 128).T
    gboT[:, 8:16] = g("L0_attn_a_beta").reshape(8, 128).T
    gboT[:, 16:24] = g("L0_attn_a_out_b").reshape(8, 128).T
    m["gboT"] = gboT
    m["aa0_vb"] = g("L0_attn_a_v_b").reshape(1, D)
    pja = np.ones((65, 256), np.float32)
    pja[0:64, :] = g("L0_attn_a_proj").T
    m["projT_aug"] = np.ascontiguousarray(pja)
    m["utmask"] = (np.arange(128)[:, None] <= np.arange(128)[None, :]).astype(np.float32)
    m["b1T"] = np.ascontiguousarray(g("L0_ffn_b1").reshape(8, 32, 128)
                                    .transpose(2, 0, 1).reshape(128, 256))
    m["b2T"] = np.ascontiguousarray(g("L0_ffn_b2").reshape(8, 8, 128)
                                    .transpose(2, 0, 1).reshape(128, 64))
    E8sel = np.zeros((128, 8), np.float32)
    for pp in range(64):
        E8sel[pp, pp % 8] = 1.0
    m["E8sel"] = E8sel
    m["E8T"] = np.ascontiguousarray(E8sel.T)
    m["routing_WT"] = np.ascontiguousarray(g("L1_ffn_routing_W").T)
    m["expert_bias128"] = np.ascontiguousarray(
        np.broadcast_to(g("L1_ffn_expert_bias").reshape(1, 8), (128, 8)))
    m["sb1T"] = np.ascontiguousarray(g("L1_ffn_sb1").reshape(4, 128).T)
    m["sb2T"] = np.ascontiguousarray(g("L1_ffn_sb2").reshape(8, 128).T)
    m["rb1T"] = np.ascontiguousarray(g("L1_ffn_rb1").reshape(8, 4, 128)
                                     .transpose(2, 0, 1).reshape(128, 32))
    m["rb2T"] = np.ascontiguousarray(g("L1_ffn_rb2").reshape(8, 8, 128)
                                     .transpose(2, 0, 1).reshape(128, 64))
    m["ident"] = np.eye(128, dtype=np.float32)
    return m


def kernel(**inputs):
    x = np.asarray(inputs["L0_attn_a_x"], np.float32)
    shards = _shard_x(x)
    arrs = _host_all_arrays(inputs)
    nc = _get_prog("mega", lambda: build_mega())
    in_maps = []
    for c in range(NCORES):
        mm_ = dict(arrs)
        mm_["x0"] = shards[c]
        in_maps.append(mm_)
    res = run_bass_kernel_spmd(nc, in_maps, core_ids=list(range(NCORES)))
    return _unshard_x([np.asarray(r["xn"]) for r in res.results])
